# version 01
# baseline (speedup 1.0000x reference)
import numpy as np
import ml_dtypes
import concourse.bass as bass
import concourse.mybir as mybir
from concourse.bass_utils import run_bass_kernel_spmd

F32 = mybir.dt.float32
BF16 = mybir.dt.bfloat16
AF = mybir.ActivationFunctionType
ALU = mybir.AluOpType
AX = mybir.AxisListType
bfdt = ml_dtypes.bfloat16

T = 4096
C = 2048
HD = 128
NH = 16
NKV_A = 4
INTER = 5632
NI = INTER // 128
NCC = C // 128
QT = 512
NQT = T // QT
GROUP = 1024
NG = T // GROUP
TPG = GROUP // QT
NTT = 32
EPS = 1e-5
WINDOW = 512


class Sems:
    def __init__(self, nc, stack, names):
        self.h = {}
        self.n = {}
        for nm in names:
            self.h[nm] = stack.enter_context(nc.semaphore(nm))
            self.n[nm] = 0

    def inc(self, nm, k):
        self.n[nm] += k
        return self.h[nm], k

    def val(self, nm):
        return self.n[nm]


SEM_NAMES = [
    "xld0", "xld1", "yld0", "yld1", "hld", "hst0", "hst1", "xst",
    "yst0", "yst1", "xst0", "xst1",
    "vln", "sln", "happ",
    "gld0", "gld1", "uld0", "uld1", "dwld0", "dwld1",
    "mm", "act", "zst", "ymm", "yev",
    "bar", "aux",
    "kw0", "kw1", "vw0", "vw1", "tabc0", "tabc1", "tabs0", "tabs1",
    "hq0", "hq1", "qw0", "qw1", "ow0", "ow1",
    "kev1", "kev2", "kev3", "kmax", "vev1", "vev2",
    "qev1", "qev2", "qev3", "cr", "cev", "varg", "pexp", "pmask", "lr", "oev",
]


def emit_ln_phase(block, env, x_src, x_dst, y_src, do_ln, h_dst):
    S, B = env["sems"], env["bufs"]
    xt, yt, ht = B["xt"], B["yt"], B["ht"]
    st8, bn6, sqs = B["st8"], B["bn6"], B["sqs"]
    have_y = y_src is not None
    b = {nm: S.val(nm) for nm in ("xld0", "xld1", "yld0", "yld1", "vln", "sln",
                                  "happ", "xst0", "xst1", "hst0", "hst1")}

    @block.sync
    def _(s):
        for i in range(NTT + 2):
            if 2 <= i:
                j = i - 2
                s.wait_ge(S.h["vln"], b["vln"] + j + 1)
                s.dma_start(out=x_dst[j * 128:(j + 1) * 128, :], in_=xt[j % 2][:, :])\
                    .then_inc(*S.inc(f"xst{j % 2}", 16))
                if do_ln:
                    s.dma_start(out=h_dst[j * 128:(j + 1) * 128, :], in_=ht[j % 2][:, :])\
                        .then_inc(*S.inc(f"hst{j % 2}", 16))
            if i < NTT:
                if i >= 2:
                    s.wait_ge(S.h[f"xst{i % 2}"], b[f"xst{i % 2}"] + 16 * (i // 2))
                s.dma_start(out=xt[i % 2][:, :], in_=x_src[i * 128:(i + 1) * 128, :])\
                    .then_inc(*S.inc(f"xld{i % 2}", 16))
                if have_y:
                    s.dma_start_transpose(yt[i % 2][:, :], y_src[:, i * 128:(i + 1) * 128])\
                        .then_inc(*S.inc(f"yld{i % 2}", 16))

    @block.vector
    def _(v):
        for i in range(NTT):
            p = i % 2
            v.wait_ge(S.h[f"xld{p}"], b[f"xld{p}"] + 16 * (i // 2 + 1))
            if have_y:
                v.wait_ge(S.h[f"yld{p}"], b[f"yld{p}"] + 16 * (i // 2 + 1))
            if do_ln:
                if have_y:
                    v.tensor_tensor(xt[p][:, :], xt[p][:, :], yt[p][:, :], ALU.add)
                    v.drain()
                if i >= 2:
                    v.wait_ge(S.h["happ"], b["happ"] + i - 1)
                for g in range(4):
                    v.bn_stats(bn6[:, p * 24 + 6 * g: p * 24 + 6 * g + 6],
                               xt[p][:, 512 * g:512 * (g + 1)])
                v.drain()
                b6 = bn6[:, p * 24: p * 24 + 24].rearrange("a (g s) -> a g s", s=3)
                col = st8[:, p * 8: p * 8 + 8]
                v.tensor_reduce(col[:, 0:1], b6[:, :, 1], AX.X, ALU.add)
                v.tensor_reduce(col[:, 1:2], b6[:, :, 2], AX.X, ALU.add)
                v.scalar_tensor_tensor(sqs[:, p * 8: p * 8 + 8],
                                       b6[:, :, 1], 1.0, b6[:, :, 1],
                                       ALU.bypass, ALU.mult,
                                       accum_out=col[:, 2:3])
                v.drain()
                v.tensor_scalar(col[:, 4:5], col[:, 0:1], 1.0 / 8, None, ALU.mult)
                v.tensor_scalar(col[:, 6:7], col[:, 2:3], 1.0 / 8, None, ALU.mult)
                v.drain()
                v.tensor_scalar(col[:, 5:6], col[:, 4:5], -1.0, None, ALU.mult)
                v.scalar_tensor_tensor(col[:, 6:7], col[:, 1:2], 1.0 / 2048,
                                       col[:, 6:7], ALU.mult, ALU.add)
                v.drain()
                v.scalar_tensor_tensor(col[:, 6:7], col[:, 4:5], col[:, 5:6],
                                       col[:, 6:7], ALU.mult, ALU.add)
                v.drain()
                v.tensor_single_scalar(col[:, 6:7], col[:, 6:7], EPS, ALU.add)\
                    .then_inc(*S.inc("sln", 1))
                v.wait_ge(S.h["happ"], b["happ"] + i + 1)
                if i >= 2:
                    v.wait_ge(S.h[f"hst{p}"], b[f"hst{p}"] + 16 * (i // 2))
                v.tensor_scalar(ht[p][:, :], xt[p][:, :], col[:, 4:5], col[:, 7:8],
                                ALU.subtract, ALU.mult).then_inc(*S.inc("vln", 1))
            elif have_y:
                v.tensor_tensor(xt[p][:, :], xt[p][:, :], yt[p][:, :], ALU.add)\
                    .then_inc(*S.inc("vln", 1))
            else:
                v.tensor_single_scalar(xt[p][:, :], xt[p][:, :], 0.0, ALU.add)\
                    .then_inc(*S.inc("vln", 1))

    if do_ln:
        @block.scalar
        def _(sc):
            for i in range(NTT):
                p = i % 2
                col = st8[:, p * 8: p * 8 + 8]
                sc.wait_ge(S.h["sln"], b["sln"] + i + 1)
                sc.activation(col[:, 7:8], col[:, 6:7], AF.Ln)
                sc.drain()
                sc.activation(col[:, 7:8], col[:, 7:8], AF.Exp, scale=-0.5)\
                    .then_inc(*S.inc("happ", 1))


def emit_ffn_phase(block, env, h_src, y_dst, w):
    S, B = env["sems"], env["bufs"]
    hsb, zsb = B["hsb"], B["zsb"]
    gw, uw, dw = B["gw"], B["uw"], B["dw"]
    silu, ysb = B["silu"], B["ysb"]
    pg, pu, py = B["pg"], B["pu"], B["py"]
    gS, uS, dS = w["gS"], w["uS"], w["dS"]

    b = {nm: S.val(nm) for nm in ("hld", "gld0", "gld1", "uld0", "uld1",
                                  "dwld0", "dwld1", "mm", "act", "zst",
                                  "ymm", "yev", "yst0", "yst1")}

    gWTr = w["gWT"]
    uWTr = w["uWT"]
    dWT3r = w["dWT3"]

    mm_g, mm_u, mm_y = {}, {}, {}
    mmc = 0
    for grp in range(NG):
        for i in range(NI):
            for tt in range(TPG):
                mmc += 1
                mm_g[(grp, i, tt)] = mmc
            for tt in range(TPG):
                mmc += 1
                mm_u[(grp, i, tt)] = mmc
        for co in range(NCC):
            for tt in range(TPG):
                mmc += 1
                mm_y[(grp, co, tt)] = mmc

    def chunk_gi(n):
        return divmod(n, NI)

    @block.gpsimd
    def _(g):
        for grp in range(NG):
            for i in range(NI):
                n = grp * NI + i
                par = n % 2
                if n >= 2:
                    pg_, pi_ = chunk_gi(n - 2)
                    g.wait_ge(S.h["mm"], b["mm"] + mm_g[(pg_, pi_, TPG - 1)])
                g.dma_start(out=gw[par][:, :], in_=gWTr[i, :, :])\
                    .then_inc(*S.inc(f"gld{par}", 16))
                if n >= 2:
                    pg_, pi_ = chunk_gi(n - 2)
                    g.wait_ge(S.h["mm"], b["mm"] + mm_u[(pg_, pi_, TPG - 1)])
                g.dma_start(out=uw[par][:, :], in_=uWTr[i, :, :])\
                    .then_inc(*S.inc(f"uld{par}", 16))
            for co in range(NCC):
                n = grp * NCC + co
                if n >= 2:
                    pg_, pco_ = divmod(n - 2, NCC)
                    g.wait_ge(S.h["mm"], b["mm"] + mm_y[(pg_, pco_, TPG - 1)])
                g.dma_start(out=dw[n % 2][:, :], in_=dWT3r[co, :, :])\
                    .then_inc(*S.inc(f"dwld{n % 2}", 16))

    @block.sync
    def _(s):
        for grp in range(NG):
            if grp >= 1:
                s.wait_ge(S.h["mm"], b["mm"] + mm_u[(grp - 1, NI - 1, TPG - 1)])
            for cc in range(NCC):
                for tt in range(TPG):
                    t0 = grp * GROUP + tt * QT
                    s.dma_start_transpose(
                        hsb[:, cc * GROUP + tt * QT: cc * GROUP + (tt + 1) * QT],
                        h_src[t0:t0 + QT, cc * 128:(cc + 1) * 128],
                    ).then_inc(*S.inc("hld", 16))

    @block.tensor
    def _(t):
        for grp in range(NG):
            t.wait_ge(S.h["hld"], b["hld"] + 16 * NCC * TPG * (grp + 1))
            for i in range(NI):
                n = grp * NI + i
                par = n % 2
                t.wait_ge(S.h[f"gld{par}"], b[f"gld{par}"] + 16 * (n // 2 + 1))
                for tt in range(TPG):
                    gi = n * TPG + tt
                    if n >= 1:
                        t.wait_ge(S.h["act"], b["act"] + gi - TPG + 1)
                    for cc in range(NCC):
                        ins = t.matmul(pg[tt][:, :], gw[par][:, cc * 128:(cc + 1) * 128],
                                       hsb[:, cc * GROUP + tt * QT: cc * GROUP + (tt + 1) * QT],
                                       start=(cc == 0), stop=(cc == NCC - 1))
                    ins.then_inc(*S.inc("mm", 1))
                t.wait_ge(S.h[f"uld{par}"], b[f"uld{par}"] + 16 * (n // 2 + 1))
                for tt in range(TPG):
                    gi = n * TPG + tt
                    if n >= 1:
                        t.wait_ge(S.h["zst"], b["zst"] + gi - TPG + 1)
                    for cc in range(NCC):
                        ins = t.matmul(pu[tt][:, :], uw[par][:, cc * 128:(cc + 1) * 128],
                                       hsb[:, cc * GROUP + tt * QT: cc * GROUP + (tt + 1) * QT],
                                       start=(cc == 0), stop=(cc == NCC - 1))
                    ins.then_inc(*S.inc("mm", 1))
            t.wait_ge(S.h["zst"], b["zst"] + (grp + 1) * NI * TPG)
            for co in range(NCC):
                n = grp * NCC + co
                t.wait_ge(S.h[f"dwld{n % 2}"], b[f"dwld{n % 2}"] + 16 * (n // 2 + 1))
                for tt in range(TPG):
                    yi = n * TPG + tt
                    if yi >= 2:
                        t.wait_ge(S.h["yev"], b["yev"] + yi - 1)
                    for ich in range(NI):
                        ins = t.matmul(py[yi % 2][:, :], dw[n % 2][:, ich * 128:(ich + 1) * 128],
                                       zsb[:, ich * GROUP + tt * QT: ich * GROUP + (tt + 1) * QT],
                                       start=(ich == 0), stop=(ich == NI - 1))
                    ins.then_inc(*S.inc("mm", 1))
                    t.sem_inc(*S.inc("ymm", 1))

    @block.scalar
    def _(sc):
        for grp in range(NG):
            for i in range(NI):
                for tt in range(TPG):
                    gi = (grp * NI + i) * TPG + tt
                    sc.wait_ge(S.h["mm"], b["mm"] + mm_g[(grp, i, tt)])
                    if gi >= 2:
                        sc.wait_ge(S.h["zst"], b["zst"] + gi - 1)
                    sc.activation(silu[gi % 2][:, :], pg[tt][:, :], AF.Silu,
                                  scale=gS[:, i:i + 1]).then_inc(*S.inc("act", 1))
            for co in range(NCC):
                for tt in range(TPG):
                    yi = (grp * NCC + co) * TPG + tt
                    t0 = grp * GROUP + tt * QT
                    sc.wait_ge(S.h["yev"], b["yev"] + yi + 1)
                    sc.dma_start(out=y_dst[co * 128:(co + 1) * 128, t0:t0 + QT],
                                 in_=ysb[yi % 2][:, :]).then_inc(*S.inc(f"yst{yi % 2}", 16))

    @block.vector
    def _(v):
        for grp in range(NG):
            for i in range(NI):
                for tt in range(TPG):
                    gi = (grp * NI + i) * TPG + tt
                    v.wait_ge(S.h["mm"], b["mm"] + mm_u[(grp, i, tt)])
                    v.wait_ge(S.h["act"], b["act"] + gi + 1)
                    if grp >= 1 and i == 0 and tt == 0:
                        v.wait_ge(S.h["mm"], b["mm"] + mm_y[(grp - 1, NCC - 1, TPG - 1)])
                    v.scalar_tensor_tensor(
                        zsb[:, i * GROUP + tt * QT: i * GROUP + (tt + 1) * QT],
                        pu[tt][:, :], uS[:, i:i + 1], silu[gi % 2][:, :],
                        ALU.mult, ALU.mult).then_inc(*S.inc("zst", 1))
            for co in range(NCC):
                for tt in range(TPG):
                    yi = (grp * NCC + co) * TPG + tt
                    v.wait_ge(S.h["mm"], b["mm"] + mm_y[(grp, co, tt)])
                    if yi >= 2:
                        v.wait_ge(S.h[f"yst{yi % 2}"], b[f"yst{yi % 2}"] + 16 * (yi // 2))
                    v.tensor_scalar(ysb[yi % 2][:, :], py[yi % 2][:, :],
                                    dS[:, co:co + 1], None, ALU.mult)\
                        .then_inc(*S.inc("yev", 1))




def attn_chunks(kind, qi):
    Q0 = qi * QT
    lo = 0 if kind == "a" else max(0, Q0 - WINDOW)
    out = []
    for K0 in range(lo, Q0 + QT - 128 + 1, 128):
        d = K0 - Q0
        if d >= 0:
            slot = d // 128
        elif kind == "b":
            slot = 4 + (d + WINDOW) // 128
        else:
            slot = None
        out.append((K0, slot))
    return out


def emit_attn_phase(block, env, h_src, y_dst, w, kind):
    sfx = env.get("sfx", "")
    """One attention sublayer (kind 'a' = GQA causal, 'b' = MQA window-512).
    Reads h_src (HBM bf16 token-major), writes y_dst (HBM bf16 feature-major).
    """
    S, B = env["sems"], env["bufs"]
    nkv = NKV_A if kind == "a" else 1
    r_imm = 1.0 if kind == "a" else float(np.sqrt(2.0))
    qk_scale = 0.5 if kind == "a" else 0.25

    masks = B["masks"]
    ident, ones = B["ident"], B["ones"]
    R_sb = B["Ra"] if kind == "a" else B["Rb"]
    bq = B.get("bqA") if kind == "a" else None
    bk = B.get("bkA") if kind == "a" else None
    bv = B.get("bvA") if kind == "a" else None
    bo = B["boA"] if kind == "a" else B["boB"]
    cosT, sinT = (w["cosA"], w["sinA"]) if kind == "a" else (w["cosB"], w["sinB"])
    WqT, WkT, WvT, WoT = w["WqT"], w["WkT"], w["WvT"], w["WoT"]

    hq = B["hq"]
    ksb, vsb = B["ksb"], B["vsb"]
    tabc, tabs = B["tabc"], B["tabs"]
    mk2 = B["mk2"]

    base = {nm: S.val(nm) for nm in (
        "mm", "kw0", "kw1", "vw0", "vw1", "tabc0", "tabc1", "tabs0", "tabs1",
        "hq0", "hq1", "qw0", "qw1", "ow0", "ow1",
        "kev1", "kev2", "kev3", "kmax", "vev1", "vev2",
        "qev1", "qev2", "qev3", "cr", "cev", "varg", "pexp", "pmask", "lr",
        "oev", "yev", "yst0", "yst1")}

    chunks = {qi: attn_chunks(kind, qi) for qi in range(NQT)}
    pair_of = {}
    pairs_of_g = {}
    pmask_idx = {}
    pgc = 0
    pmc = 0
    for qi in range(NQT):
        for h in range(NH):
            g = qi * NH + h
            lst = []
            for n, (K0, slot) in enumerate(chunks[qi]):
                pgc += 1
                pair_of[(g, n)] = pgc
                lst.append(pgc)
                if slot is not None:
                    pmc += 1
                    pmask_idx[pgc] = pmc
            pairs_of_g[g] = lst
    npairs = pgc

    mmpos = {}

    from contextlib import ExitStack
    with ExitStack() as _st:
        _sb = lambda n, sh, dt: _st.enter_context(env["nc"].sbuf_tensor(n + sfx, sh, dt))
        _ps = lambda n, sh, dt: _st.enter_context(env["nc"].psum_tensor(n + sfx, sh, dt))
        kwb = [_sb(f"kwb{j}", [128, NCC * 128], BF16) for j in range(nkv)]
        vwb = [_sb(f"vwb{j}", [128, NCC * 128], BF16) for j in range(nkv)]
        kscb = _sb("kscb", [128, QT], BF16)
        vscb = _sb("vscb", [128, QT], BF16)
        sqb = _sb("sqb", [128, QT], BF16)
        scr1 = _sb("scr1", [128, QT], F32)
        scr2 = _sb("scr2", [128, QT], F32)
        kraw = _ps("kraw", [128, QT], F32)
        rotb = _ps("rotb", [128, QT], F32)
        vraw = _ps("vraw", [128, QT], F32)
        vtb = _ps("vtb", [128, QT], BF16)
        NIDX = 8 * nkv

        def kv_idx(tt8, j):
            return tt8 * nkv + j + 1

        @block.tensor
        def _(t):
            for tt8 in range(8):
                for j in range(nkv):
                    idx = kv_idx(tt8, j)
                    t.wait_ge(S.h["kw0"], base["kw0"] + 16 * nkv)
                    t.wait_ge(S.h[f"hq{tt8 % 2}"],
                              base[f"hq{tt8 % 2}"] + 16 * NCC * (tt8 // 2 + 1))
                    if idx >= 2:
                        t.wait_ge(S.h["kev1"], base["kev1"] + idx - 1)
                    for cc in range(NCC):
                        ins = t.matmul(kraw[:, :], kwb[j][:, cc * 128:(cc + 1) * 128],
                                       hq[tt8 % 2][:, cc * QT:(cc + 1) * QT],
                                       start=(cc == 0), stop=(cc == NCC - 1))
                    ins.then_inc(*S.inc("mm", 1))
                    mmpos[("kraw", idx)] = S.val("mm")
                    t.wait_ge(S.h["kev1"], base["kev1"] + idx)
                    if idx >= 2:
                        t.wait_ge(S.h["kmax"], base["kmax"] + idx - 1)
                    ins = t.matmul(rotb[:, :], R_sb[:, :], kscb[:, :],
                                   start=True, stop=True)
                    ins.then_inc(*S.inc("mm", 1))
                    mmpos[("rot", idx)] = S.val("mm")
                    t.wait_ge(S.h["kev3"], base["kev3"] + idx)
                    if idx >= 2:
                        t.wait_ge(S.h["kmax"], base["kmax"] + idx - 1)
                    ins = t.matmul(rotb[0:1, :], ones[:, 0:1], sqb[:, :],
                                   start=True, stop=True)
                    ins.then_inc(*S.inc("mm", 1))
                    mmpos[("kn2", idx)] = S.val("mm")
                    t.wait_ge(S.h["vw0"], base["vw0"] + 16 * nkv)
                    if idx >= 2:
                        t.wait_ge(S.h["vev1"], base["vev1"] + idx - 1)
                    for cc in range(NCC):
                        ins = t.matmul(vraw[:, :], vwb[j][:, cc * 128:(cc + 1) * 128],
                                       hq[tt8 % 2][:, cc * QT:(cc + 1) * QT],
                                       start=(cc == 0), stop=(cc == NCC - 1))
                    ins.then_inc(*S.inc("mm", 1))
                    mmpos[("vraw", idx)] = S.val("mm")
                    t.wait_ge(S.h["vev1"], base["vev1"] + idx)
                    if idx >= 2:
                        t.wait_ge(S.h["vev2"], base["vev2"] + idx - 1)
                    for s_ in range(4):
                        ins = t.transpose(vtb[:, s_ * 128:(s_ + 1) * 128],
                                          vscb[:, s_ * 128:(s_ + 1) * 128],
                                          ident[:, :])
                    ins.then_inc(*S.inc("mm", 1))
                    mmpos[("vt", idx)] = S.val("mm")

        @block.gpsimd
        def _(g):
            for j in range(nkv):
                g.dma_start(out=kwb[j][:, :], in_=WkT[j, :, :])\
                    .then_inc(*S.inc("kw0", 16))
                g.dma_start(out=vwb[j][:, :], in_=WvT[j, :, :])\
                    .then_inc(*S.inc("vw0", 16))

        @block.sync
        def _(s):
            for tt8 in range(8):
                par = tt8 % 2
                if tt8 >= 2:
                    s.wait_ge(S.h["mm"], mmpos[("vraw", kv_idx(tt8 - 2, nkv - 1))])
                for cc in range(NCC):
                    s.dma_start_transpose(
                        hq[par][:, cc * QT:(cc + 1) * QT],
                        h_src[tt8 * QT:(tt8 + 1) * QT, cc * 128:(cc + 1) * 128],
                    ).then_inc(*S.inc(f"hq{par}", 16))
                if tt8 >= 2:
                    s.wait_ge(S.h["kev2"], base["kev2"] + kv_idx(tt8 - 2, nkv - 1))
                s.dma_start(out=tabc[par][:, :], in_=cosT[:, tt8 * QT:(tt8 + 1) * QT])\
                    .then_inc(*S.inc(f"tabc{par}", 16))
                s.dma_start(out=tabs[par][:, :], in_=sinT[:, tt8 * QT:(tt8 + 1) * QT])\
                    .then_inc(*S.inc(f"tabs{par}", 16))

        @block.vector
        def _(v):
            v.memset(mk2[0:1, 0:8], 0.0)
            v.drain()
            for tt8 in range(8):
                par = tt8 % 2
                for j in range(nkv):
                    idx = kv_idx(tt8, j)
                    v.wait_ge(S.h["mm"], mmpos[("kraw", idx)])
                    if idx >= 2:
                        v.wait_ge(S.h["mm"], mmpos[("rot", idx - 1)])
                    if kind == "a":
                        v.tensor_scalar(kscb[:, :], kraw[:, :], bk[:, j:j + 1],
                                        qk_scale, ALU.add, ALU.mult)\
                            .then_inc(*S.inc("kev1", 1))
                    else:
                        v.tensor_scalar(kscb[:, :], kraw[:, :], qk_scale, None,
                                        ALU.mult).then_inc(*S.inc("kev1", 1))
                    v.drain()
                    v.wait_ge(S.h[f"tabc{par}"], base[f"tabc{par}"] + 16 * (tt8 // 2 + 1))
                    v.wait_ge(S.h[f"tabs{par}"], base[f"tabs{par}"] + 16 * (tt8 // 2 + 1))
                    v.tensor_tensor(scr1[:, :], kscb[:, :], tabc[par][:, :], ALU.mult)
                    v.wait_ge(S.h["mm"], mmpos[("rot", idx)])
                    v.tensor_tensor(scr2[:, :], rotb[:, :], tabs[par][:, :], ALU.mult)
                    v.drain()
                    v.tensor_tensor(ksb[:, j * T + tt8 * QT: j * T + (tt8 + 1) * QT],
                                    scr1[:, :], scr2[:, :], ALU.add)\
                        .then_inc(*S.inc("kev2", 1))
                    v.drain()
                    kk = ksb[:, j * T + tt8 * QT: j * T + (tt8 + 1) * QT]
                    v.tensor_tensor(sqb[:, :], kk, kk, ALU.mult)\
                        .then_inc(*S.inc("kev3", 1))
                    v.drain()
                    v.wait_ge(S.h["mm"], mmpos[("kn2", idx)])
                    v.tensor_reduce(scr1[0:1, 0:1], rotb[0:1, :], AX.X, ALU.max)
                    v.drain()
                    v.tensor_tensor(mk2[0:1, j:j + 1], mk2[0:1, j:j + 1],
                                    scr1[0:1, 0:1], ALU.max)\
                        .then_inc(*S.inc("kmax", 1))
                    v.drain()
                    v.wait_ge(S.h["mm"], mmpos[("vraw", idx)])
                    if idx >= 2:
                        v.wait_ge(S.h["mm"], mmpos[("vt", idx - 1)])
                    if kind == "a":
                        v.tensor_scalar(vscb[:, :], vraw[:, :], bv[:, j:j + 1],
                                        None, ALU.add).then_inc(*S.inc("vev1", 1))
                    else:
                        v.tensor_copy(vscb[:, :], vraw[:, :])\
                            .then_inc(*S.inc("vev1", 1))
                    v.wait_ge(S.h["mm"], mmpos[("vt", idx)])
                    v.tensor_copy(
                        vsb[:, (j * 32 + tt8 * 4) * 128:(j * 32 + tt8 * 4 + 4) * 128],
                        vtb[:, :]).then_inc(*S.inc("vev2", 1))
            for j in range(nkv):
                ins = v.tensor_single_scalar(mk2[0:1, j:j + 1], mk2[0:1, j:j + 1],
                                             1.0 / r_imm, ALU.add)
            ins.then_inc(*S.inc("kmax", 1))
            v.drain()

        emit_barrier(block, env, ["kev1", "kev2", "kev3", "kmax", "vev1", "vev2",
                                  "kw0", "vw0", "tabc0", "tabc1",
                                  "tabs0", "tabs1", "hq0", "hq1", "mm"])

    with ExitStack() as _st:
        _sb = lambda n, sh, dt: _st.enter_context(env["nc"].sbuf_tensor(n + sfx, sh, dt))
        _ps = lambda n, sh, dt: _st.enter_context(env["nc"].psum_tensor(n + sfx, sh, dt))
        qwb0 = _sb("qwb0", [128, NCC * 128], BF16)
        qwb1 = _sb("qwb1", [128, NCC * 128], BF16)
        owb0 = _sb("owb0", [128, NH * 128], BF16)
        owb1 = _sb("owb1", [128, NH * 128], BF16)
        qscb = _sb("qscb", [128, QT], BF16)
        qsb0 = _sb("qsb0", [128, QT], BF16)
        qsb1 = _sb("qsb1", [128, QT], BF16)
        sqb = _sb("sqbq", [128, QT], BF16)
        scr1 = _sb("scr1q", [128, QT], F32)
        scr2 = _sb("scr2q", [128, QT], F32)
        psb0 = _sb("psb0", [128, QT], BF16)
        psb1 = _sb("psb1", [128, QT], BF16)
        argb0 = _sb("argb0", [128, QT], F32)
        argb1 = _sb("argb1", [128, QT], F32)
        crow0 = _sb("crow0", [128, QT], BF16)
        crow1 = _sb("crow1", [128, QT], BF16)
        Csb0 = _sb("Csb0", [128, QT], BF16)
        Csb1 = _sb("Csb1", [128, QT], BF16)
        Rsb0 = _sb("Rsb0", [128, QT], F32)
        Rsb1 = _sb("Rsb1", [128, QT], F32)
        osb = _sb("osb", [128, NH * QT], BF16)
        ysb2a = _sb("ysb2a", [128, QT], BF16)
        ysb2b = _sb("ysb2b", [128, QT], BF16)
        qraw = _ps("qraw", [128, QT], F32)
        Tb = _ps("Tb", [128, QT], F32)
        sb0 = _ps("sb0", [128, QT], F32)
        sb1 = _ps("sb1", [128, QT], F32)
        ob0 = _ps("ob0", [128, QT], F32)
        ob1 = _ps("ob1", [128, QT], F32)
        lb0 = _ps("lb0", [128, QT], F32)
        lb1 = _ps("lb1", [128, QT], F32)
        qwb, owb = [qwb0, qwb1], [owb0, owb1]
        qsb, psb, argb = [qsb0, qsb1], [psb0, psb1], [argb0, argb1]
        crowb, Csb, Rsb2 = [crow0, crow1], [Csb0, Csb1], [Rsb0, Rsb1]
        ysb2 = [ysb2a, ysb2b]
        sbk, obk, lbk = [sb0, sb1], [ob0, ob1], [lb0, lb1]

        @block.tensor
        def _(t):
            for qi in range(NQT):
                for h in range(NH):
                    g = qi * NH + h
                    j = h % nkv
                    par = g % 2
                    t.wait_ge(S.h[f"qw{par}"], base[f"qw{par}"] + 16 * (g // 2 + 1))
                    t.wait_ge(S.h[f"hq{qi % 2}"],
                              base[f"hq{qi % 2}"] + 16 * NCC * (4 + qi // 2 + 1))
                    if g >= 2:
                        t.wait_ge(S.h["qev1"], base["qev1"] + g - 1)
                    for cc in range(NCC):
                        ins = t.matmul(qraw[:, :], qwb[par][:, cc * 128:(cc + 1) * 128],
                                       hq[qi % 2][:, cc * QT:(cc + 1) * QT],
                                       start=(cc == 0), stop=(cc == NCC - 1))
                    ins.then_inc(*S.inc("mm", 1))
                    mmpos[("qraw", g)] = S.val("mm")
                    t.wait_ge(S.h["qev1"], base["qev1"] + g + 1)
                    if g >= 1:
                        t.wait_ge(S.h["cev"], base["cev"] + g)
                    ins = t.matmul(Tb[:, :], R_sb[:, :], qscb[:, :],
                                   start=True, stop=True)
                    ins.then_inc(*S.inc("mm", 1))
                    mmpos[("rotq", g)] = S.val("mm")
                    t.wait_ge(S.h["qev3"], base["qev3"] + g + 1)
                    ins = t.matmul(Tb[0:1, :], ones[:, 0:1], sqb[:, :],
                                   start=True, stop=True)
                    ins.then_inc(*S.inc("mm", 1))
                    mmpos[("qn2", g)] = S.val("mm")
                    t.wait_ge(S.h["cr"], base["cr"] + g + 1)
                    ins = t.matmul(Tb[:, :], ones[0:1, :], crowb[par][0:1, :],
                                   start=True, stop=True)
                    ins.then_inc(*S.inc("mm", 1))
                    mmpos[("cbc", g)] = S.val("mm")
                    lst = chunks[qi]
                    for n, (K0, slot) in enumerate(lst):
                        pg = pair_of[(g, n)]
                        t.wait_ge(S.h["qev2"], base["qev2"] + g + 1)
                        if pg >= 3:
                            t.wait_ge(S.h["varg"], base["varg"] + pg - 2)
                        if qi >= 1 and g == qi * NH and n < 2:
                            t.wait_ge(S.h["yev"], base["yev"] + qi * NCC)
                        ins = t.matmul(sbk[pg % 2][:, :],
                                       ksb[:, j * T + K0: j * T + K0 + 128],
                                       qsb[par][:, :], start=True, stop=True)
                        ins.then_inc(*S.inc("mm", 1))
                        mmpos[("s", pg)] = S.val("mm")
                        t.wait_ge(S.h["pexp"], base["pexp"] + pg)
                        if slot is not None:
                            t.wait_ge(S.h["pmask"], base["pmask"] + pmask_idx[pg])
                        if n == 0 and g >= 2:
                            t.wait_ge(S.h["lr"], base["lr"] + g - 1)
                            t.wait_ge(S.h["oev"], base["oev"] + g - 1)
                        t.matmul(lbk[par][:, :], ones[:, :], psb[pg % 2][:, :],
                                 start=(n == 0), stop=(n == len(lst) - 1))
                        ins = t.matmul(obk[par][:, :],
                                       vsb[:, (j * 32 + (K0 // 128)) * 128:
                                           (j * 32 + (K0 // 128) + 1) * 128],
                                       psb[pg % 2][:, :],
                                       start=(n == 0), stop=(n == len(lst) - 1))
                        ins.then_inc(*S.inc("mm", 1))
                        mmpos[("o", pg)] = S.val("mm")
                for co in range(NCC):
                    oc = qi * NCC + co
                    t.wait_ge(S.h[f"ow{oc % 2}"], base[f"ow{oc % 2}"] + 16 * (oc // 2 + 1))
                    t.wait_ge(S.h["oev"], base["oev"] + (qi + 1) * NH)
                    t.wait_ge(S.h["varg"], base["varg"] + pair_of[(qi * NH + NH - 1,
                                                                   len(chunks[qi]) - 1)])
                    if oc >= 2:
                        t.wait_ge(S.h["yev"], base["yev"] + oc - 1)
                    for hh in range(NH):
                        ins = t.matmul(sbk[co % 2][:, :],
                                       owb[oc % 2][:, hh * 128:(hh + 1) * 128],
                                       osb[:, hh * QT:(hh + 1) * QT],
                                       start=(hh == 0), stop=(hh == NH - 1))
                    ins.then_inc(*S.inc("mm", 1))
                    mmpos[("yq", qi, co)] = S.val("mm")

        @block.gpsimd
        def _(g_):
            for qi in range(NQT):
                for h in range(NH):
                    g = qi * NH + h
                    if g >= 2:
                        g_.wait_ge(S.h["mm"], mmpos[("qraw", g - 2)])
                    g_.dma_start(out=qwb[g % 2][:, :], in_=WqT[h, :, :])\
                        .then_inc(*S.inc(f"qw{g % 2}", 16))
                for co in range(NCC):
                    oc = qi * NCC + co
                    if oc >= 2:
                        g_.wait_ge(S.h["mm"], mmpos[("yq", (oc - 2) // NCC,
                                                     (oc - 2) % NCC)])
                    g_.dma_start(out=owb[oc % 2][:, :], in_=WoT[co, :, :])\
                        .then_inc(*S.inc(f"ow{oc % 2}", 16))

        @block.sync
        def _(s):
            for qi in range(NQT):
                par = qi % 2
                if qi >= 2:
                    s.wait_ge(S.h["mm"], mmpos[("qraw", (qi - 2) * NH + NH - 1)])
                for cc in range(NCC):
                    s.dma_start_transpose(
                        hq[par][:, cc * QT:(cc + 1) * QT],
                        h_src[qi * QT:(qi + 1) * QT, cc * 128:(cc + 1) * 128],
                    ).then_inc(*S.inc(f"hq{par}", 16))
                if qi >= 2:
                    s.wait_ge(S.h["qev2"], base["qev2"] + (qi - 2) * NH + NH)
                s.dma_start(out=tabc[par][:, :], in_=cosT[:, qi * QT:(qi + 1) * QT])\
                    .then_inc(*S.inc(f"tabc{par}", 16))
                s.dma_start(out=tabs[par][:, :], in_=sinT[:, qi * QT:(qi + 1) * QT])\
                    .then_inc(*S.inc(f"tabs{par}", 16))

        @block.vector
        def _(v):
            for qi in range(NQT):
                par8 = qi % 2
                for h in range(NH):
                    g = qi * NH + h
                    j = h % nkv
                    par = g % 2
                    v.wait_ge(S.h["mm"], mmpos[("qraw", g)])
                    if g >= 1:
                        v.wait_ge(S.h["mm"], mmpos[("rotq", g - 1)])
                    if g >= 2:
                        v.wait_ge(S.h["mm"], mmpos[("s", pairs_of_g[g - 2][-1])])
                    if h == 0 and qi >= 1:
                        v.wait_ge(S.h["mm"], mmpos[("yq", qi - 1, NCC - 1)])
                    if kind == "a":
                        v.tensor_scalar(qscb[:, :], qraw[:, :], bq[:, h:h + 1],
                                        qk_scale, ALU.add, ALU.mult)\
                            .then_inc(*S.inc("qev1", 1))
                    else:
                        v.tensor_scalar(qscb[:, :], qraw[:, :], qk_scale, None,
                                        ALU.mult).then_inc(*S.inc("qev1", 1))
                    v.drain()
                    v.wait_ge(S.h[f"tabc{par8}"],
                              base[f"tabc{par8}"] + 16 * (4 + qi // 2 + 1))
                    v.wait_ge(S.h[f"tabs{par8}"],
                              base[f"tabs{par8}"] + 16 * (4 + qi // 2 + 1))
                    v.tensor_tensor(scr1[:, :], qscb[:, :], tabc[par8][:, :], ALU.mult)
                    v.wait_ge(S.h["mm"], mmpos[("rotq", g)])
                    v.tensor_tensor(scr2[:, :], Tb[:, :], tabs[par8][:, :], ALU.mult)
                    v.drain()
                    v.tensor_tensor(qsb[par][:, :], scr1[:, :], scr2[:, :], ALU.add)\
                        .then_inc(*S.inc("qev2", 1))
                    v.drain()
                    v.tensor_tensor(sqb[:, :], qsb[par][:, :], qsb[par][:, :],
                                    ALU.mult).then_inc(*S.inc("qev3", 1))
                    v.drain()
                    v.wait_ge(S.h["mm"], mmpos[("qn2", g)])
                    v.tensor_scalar(crowb[par][0:1, :], Tb[0:1, :], mk2[0:1, j:j + 1],
                                    r_imm / 2, ALU.add, ALU.mult)\
                        .then_inc(*S.inc("cr", 1))
                    for n, (K0, slot) in enumerate(chunks[qi]):
                        pg = pair_of[(g, n)]
                        v.wait_ge(S.h["mm"], mmpos[("s", pg)])
                        v.wait_ge(S.h["cev"], base["cev"] + g + 1)
                        if pg >= 3:
                            v.wait_ge(S.h["pexp"], base["pexp"] + pg - 2)
                        v.scalar_tensor_tensor(argb[pg % 2][:, :], sbk[pg % 2][:, :],
                                               r_imm, Csb[par][:, :],
                                               ALU.mult, ALU.subtract)\
                            .then_inc(*S.inc("varg", 1))
                        if slot is not None:
                            v.wait_ge(S.h["pexp"], base["pexp"] + pg)
                            v.tensor_tensor(psb[pg % 2][:, :], psb[pg % 2][:, :],
                                            masks[:, slot * QT:(slot + 1) * QT],
                                            ALU.mult).then_inc(*S.inc("pmask", 1))
                    v.wait_ge(S.h["lr"], base["lr"] + g + 1)
                    v.wait_ge(S.h["mm"], mmpos[("o", pairs_of_g[g][-1])])
                    v.tensor_tensor(osb[:, h * QT:(h + 1) * QT], obk[par][:, :],
                                    Rsb2[par][:, :], ALU.mult)\
                        .then_inc(*S.inc("oev", 1))
                for co in range(NCC):
                    oc = qi * NCC + co
                    v.wait_ge(S.h["mm"], mmpos[("yq", qi, co)])
                    if oc >= 2:
                        v.wait_ge(S.h[f"yst{oc % 2}"],
                                  base[f"yst{oc % 2}"] + 16 * (oc // 2))
                    v.tensor_scalar(ysb2[oc % 2][:, :], sbk[co % 2][:, :],
                                    bo[:, co:co + 1], None, ALU.add)\
                        .then_inc(*S.inc("yev", 1))

        @block.scalar
        def _(sc):
            for qi in range(NQT):
                for h in range(NH):
                    g = qi * NH + h
                    par = g % 2
                    sc.wait_ge(S.h["mm"], mmpos[("cbc", g)])
                    if g >= 2:
                        sc.wait_ge(S.h["varg"],
                                   base["varg"] + pairs_of_g[g - 2][-1])
                    sc.copy(Csb[par][:, :], Tb[:, :]).then_inc(*S.inc("cev", 1))
                    for n, (K0, slot) in enumerate(chunks[qi]):
                        pg = pair_of[(g, n)]
                        sc.wait_ge(S.h["varg"], base["varg"] + pg)
                        if pg >= 3:
                            sc.wait_ge(S.h["mm"], mmpos[("o", pg - 2)])
                        sc.activation(psb[pg % 2][:, :], argb[pg % 2][:, :], AF.Exp)\
                            .then_inc(*S.inc("pexp", 1))
                    sc.wait_ge(S.h["mm"], mmpos[("o", pairs_of_g[g][-1])])
                    if g >= 2:
                        sc.wait_ge(S.h["oev"], base["oev"] + g - 1)
                    sc.activation(Rsb2[par][:, :], lbk[par][:, :], AF.Ln)
                    sc.drain()
                    sc.activation(Rsb2[par][:, :], Rsb2[par][:, :], AF.Exp, scale=-1.0)\
                        .then_inc(*S.inc("lr", 1))
                for co in range(NCC):
                    oc = qi * NCC + co
                    sc.wait_ge(S.h["yev"], base["yev"] + oc + 1)
                    sc.dma_start(out=y_dst[co * 128:(co + 1) * 128,
                                           qi * QT:(qi + 1) * QT],
                                 in_=ysb2[oc % 2][:, :])\
                        .then_inc(*S.inc(f"yst{oc % 2}", 16))

        emit_barrier(block, env, ["qev1", "qev2", "qev3", "cr", "cev", "varg",
                                  "pexp", "pmask", "lr", "oev", "yev",
                                  "yst0", "yst1", "qw0", "qw1", "ow0", "ow1",
                                  "hq0", "hq1", "tabc0", "tabc1", "tabs0",
                                  "tabs1", "mm"])


def emit_barrier(block, env, waits):
    S = env["sems"]
    vals = [(S.h[nm], S.val(nm)) for nm in waits]

    @block.sync
    def _(s):
        for h, v in vals:
            s.wait_ge(h, v)
        s.sem_inc(S.h["bar"], 1)

    S.n["bar"] += 1
    tgt = S.val("bar")
    for eng in ("tensor", "vector", "scalar", "gpsimd"):
        getattr(block, eng)(lambda e: e.wait_ge(S.h["bar"], tgt))


def build_program(sublayers=("attn_a", "ffn", "attn_b", "ffn"), h_out=False):
    nc = bass.Bass("TRN2", target_bir_lowering=False)
    ap = lambda t: t.ap()

    x_in = ap(nc.dram_tensor("x", [T, C], F32, kind="ExternalInput"))
    out = ap(nc.dram_tensor("out", [T, C], F32, kind="ExternalOutput"))

    gWT = ap(nc.dram_tensor("gWT", [NI, 128, NCC * 128], F32, kind="ExternalInput"))
    uWT = ap(nc.dram_tensor("uWT", [NI, 128, NCC * 128], F32, kind="ExternalInput"))
    dWT3 = ap(nc.dram_tensor("dWT3", [NCC, 128, NI * 128], F32, kind="ExternalInput"))
    gS_d = ap(nc.dram_tensor("gS", [128, NI], F32, kind="ExternalInput"))
    uS_d = ap(nc.dram_tensor("uS", [128, NI], F32, kind="ExternalInput"))
    dS_d = ap(nc.dram_tensor("dS", [128, NCC], F32, kind="ExternalInput"))

    aWqT = ap(nc.dram_tensor("aWqT", [NH, 128, NCC * 128], F32, kind="ExternalInput"))
    aWkT = ap(nc.dram_tensor("aWkT", [NKV_A, 128, NCC * 128], F32, kind="ExternalInput"))
    aWvT = ap(nc.dram_tensor("aWvT", [NKV_A, 128, NCC * 128], F32, kind="ExternalInput"))
    aWoT = ap(nc.dram_tensor("aWoT", [NCC, 128, NH * 128], F32, kind="ExternalInput"))
    bWqT = ap(nc.dram_tensor("bWqT", [NH, 128, NCC * 128], F32, kind="ExternalInput"))
    bWkT = ap(nc.dram_tensor("bWkT", [1, 128, NCC * 128], F32, kind="ExternalInput"))
    bWvT = ap(nc.dram_tensor("bWvT", [1, 128, NCC * 128], F32, kind="ExternalInput"))
    bWoT = ap(nc.dram_tensor("bWoT", [NCC, 128, NH * 128], F32, kind="ExternalInput"))
    bqA_d = ap(nc.dram_tensor("bqA", [128, NH], F32, kind="ExternalInput"))
    bkA_d = ap(nc.dram_tensor("bkA", [128, NKV_A], F32, kind="ExternalInput"))
    bvA_d = ap(nc.dram_tensor("bvA", [128, NKV_A], F32, kind="ExternalInput"))
    boA_d = ap(nc.dram_tensor("boA", [128, NCC], F32, kind="ExternalInput"))
    boB_d = ap(nc.dram_tensor("boB", [128, NCC], F32, kind="ExternalInput"))
    cosA_d = ap(nc.dram_tensor("cosA", [128, T], F32, kind="ExternalInput"))
    sinA_d = ap(nc.dram_tensor("sinA", [128, T], F32, kind="ExternalInput"))
    cosB_d = ap(nc.dram_tensor("cosB", [128, T], F32, kind="ExternalInput"))
    sinB_d = ap(nc.dram_tensor("sinB", [128, T], F32, kind="ExternalInput"))
    RaI = ap(nc.dram_tensor("RaI", [128, 128], F32, kind="ExternalInput"))
    RbI = ap(nc.dram_tensor("RbI", [128, 128], F32, kind="ExternalInput"))
    masksI = ap(nc.dram_tensor("masksI", [128, 8 * QT], F32, kind="ExternalInput"))
    identI = ap(nc.dram_tensor("identI", [128, 128], F32, kind="ExternalInput"))
    onesI = ap(nc.dram_tensor("onesI", [128, 128], F32, kind="ExternalInput"))

    xs_a = ap(nc.dram_tensor("xs_a", [T, C], F32))
    xs_b = ap(nc.dram_tensor("xs_b", [T, C], F32))
    h_sc = ap(nc.dram_tensor("h_sc", [T, C], BF16,
                             kind="ExternalOutput" if h_out else "Internal"))
    y_sc = ap(nc.dram_tensor("y_sc", [C, T], BF16))

    from contextlib import ExitStack
    with ExitStack() as stack:
        S = Sems(nc, stack, SEM_NAMES)
        sb = lambda name, shape, dt: stack.enter_context(nc.sbuf_tensor(name, shape, dt))
        B = {}
        B["xt"] = [sb(f"xt{i}", [128, C], F32) for i in range(2)]
        B["yt"] = [sb(f"yt{i}", [128, C], BF16) for i in range(2)]
        B["ht"] = [sb(f"ht{i}", [128, C], BF16) for i in range(2)]
        B["st8"] = sb("st8", [128, 16], F32)
        B["bn6"] = sb("bn6", [128, 48], F32)
        B["sqs"] = sb("sqs", [128, 16], F32)
        B["gSs"] = sb("gSs", [128, NI], F32)
        B["uSs"] = sb("uSs", [128, NI], F32)
        B["dSs"] = sb("dSs", [128, NCC], F32)
        B["masks"] = sb("masks", [128, 8 * QT], BF16)
        B["ident"] = sb("ident", [128, 128], BF16)
        B["ones"] = sb("ones", [128, 128], BF16)
        B["Ra"] = sb("Ra", [128, 128], BF16)
        B["Rb"] = sb("Rb", [128, 128], BF16)
        B["bqA"] = sb("bqA_s", [128, NH], F32)
        B["bkA"] = sb("bkA_s", [128, NKV_A], F32)
        B["bvA"] = sb("bvA_s", [128, NKV_A], F32)
        B["boA"] = sb("boA_s", [128, NCC], F32)
        B["boB"] = sb("boB_s", [128, NCC], F32)

        env = {"nc": nc, "sems": S, "bufs": B}

        with nc.Block() as block:
            @block.gpsimd
            def _(g):
                for dst, srcd in [(B["gSs"], gS_d), (B["uSs"], uS_d), (B["dSs"], dS_d),
                                  (B["masks"], masksI), (B["ident"], identI),
                                  (B["ones"], onesI), (B["Ra"], RaI),
                                  (B["Rb"], RbI), (B["bqA"], bqA_d), (B["bkA"], bkA_d),
                                  (B["bvA"], bvA_d), (B["boA"], boA_d),
                                  (B["boB"], boB_d)]:
                    g.dma_start(out=dst[:, :], in_=srcd[:, :]).then_inc(*S.inc("aux", 16))

            auxtot = S.val("aux")
            for eng in ("tensor", "vector", "scalar"):
                getattr(block, eng)(lambda e: e.wait_ge(S.h["aux"], auxtot))

            x_cur = x_in
            x_scr = [xs_a, xs_b]
            y_prev = None
            for k, sl in enumerate(sublayers):
                x_next = x_scr[k % 2]
                emit_ln_phase(block, env, x_cur, x_next, y_prev, True, h_sc)
                emit_barrier(block, env, ["xld0", "xld1", "yld0", "yld1", "xst0",
                                          "xst1", "hst0", "hst1", "vln"])

                if sl == "ffn":
                    with (
                        nc.sbuf_tensor(f"hsb_{k}", [128, NCC * GROUP], BF16) as hsb,
                        nc.sbuf_tensor(f"zsb_{k}", [128, NI * GROUP], BF16) as zsb,
                        nc.sbuf_tensor(f"gw0_{k}", [128, C], BF16) as gw0,
                        nc.sbuf_tensor(f"gw1_{k}", [128, C], BF16) as gw1,
                        nc.sbuf_tensor(f"uw0_{k}", [128, C], BF16) as uw0,
                        nc.sbuf_tensor(f"uw1_{k}", [128, C], BF16) as uw1,
                        nc.sbuf_tensor(f"dw0_{k}", [128, INTER], BF16) as dw0,
                        nc.sbuf_tensor(f"dw1_{k}", [128, INTER], BF16) as dw1,
                        nc.sbuf_tensor(f"silu0_{k}", [128, QT], BF16) as silu0,
                        nc.sbuf_tensor(f"silu1_{k}", [128, QT], BF16) as silu1,
                        nc.sbuf_tensor(f"ysb0_{k}", [128, QT], BF16) as ysb0,
                        nc.sbuf_tensor(f"ysb1_{k}", [128, QT], BF16) as ysb1,
                        nc.psum_tensor(f"pg0_{k}", [128, QT], F32) as pg0,
                        nc.psum_tensor(f"pg1_{k}", [128, QT], F32) as pg1,
                        nc.psum_tensor(f"pu0_{k}", [128, QT], F32) as pu0,
                        nc.psum_tensor(f"pu1_{k}", [128, QT], F32) as pu1,
                        nc.psum_tensor(f"py0_{k}", [128, QT], F32) as py0,
                        nc.psum_tensor(f"py1_{k}", [128, QT], F32) as py1,
                    ):
                        B.update(hsb=hsb, zsb=zsb, gw=[gw0, gw1], uw=[uw0, uw1],
                                 dw=[dw0, dw1], silu=[silu0, silu1],
                                 ysb=[ysb0, ysb1], pg=[pg0, pg1], pu=[pu0, pu1],
                                 py=[py0, py1])
                        w = {"gWT": gWT, "uWT": uWT, "dWT3": dWT3,
                             "gS": B["gSs"], "uS": B["uSs"], "dS": B["dSs"]}
                        emit_ffn_phase(block, env, h_sc, y_sc, w)
                        emit_barrier(block, env,
                                     ["hld", "gld0", "gld1", "uld0", "uld1",
                                      "dwld0", "dwld1", "yst0", "yst1", "yev",
                                      "zst", "act", "mm"])
                elif sl in ("attn_a", "attn_b"):
                    nkv = NKV_A if sl == "attn_a" else 1
                    with (
                        nc.sbuf_tensor(f"ksb{k}", [128, nkv * T], BF16) as ksb,
                        nc.sbuf_tensor(f"vsb{k}", [128, nkv * T], BF16) as vsb,
                        nc.sbuf_tensor(f"hqA{k}", [128, NCC * QT], BF16) as hqA,
                        nc.sbuf_tensor(f"hqB{k}", [128, NCC * QT], BF16) as hqB,
                        nc.sbuf_tensor(f"tabcA{k}", [128, QT], F32) as tabcA,
                        nc.sbuf_tensor(f"tabcB{k}", [128, QT], F32) as tabcB,
                        nc.sbuf_tensor(f"tabsA{k}", [128, QT], F32) as tabsA,
                        nc.sbuf_tensor(f"tabsB{k}", [128, QT], F32) as tabsB,
                        nc.sbuf_tensor(f"mk2_{k}", [128, 8], F32) as mk2,
                    ):
                        B.update(ksb=ksb, vsb=vsb, hq=[hqA, hqB],
                                 tabc=[tabcA, tabcB], tabs=[tabsA, tabsB], mk2=mk2)
                        if sl == "attn_a":
                            w = {"WqT": aWqT, "WkT": aWkT, "WvT": aWvT,
                                 "WoT": aWoT, "cosA": cosA_d, "sinA": sinA_d,
                                 "cosB": cosB_d, "sinB": sinB_d}
                        else:
                            w = {"WqT": bWqT, "WkT": bWkT, "WvT": bWvT,
                                 "WoT": bWoT, "cosA": cosA_d, "sinA": sinA_d,
                                 "cosB": cosB_d, "sinB": sinB_d}
                        env["sfx"] = f"_{k}"
                        emit_attn_phase(block, env, h_sc, y_sc, w,
                                        "a" if sl == "attn_a" else "b")
                elif sl == "ln":
                    pass
                else:
                    raise NotImplementedError(sl)

                x_cur = x_next
                if sl != "ln":
                    y_prev = y_sc

            emit_ln_phase(block, env, x_cur, out, y_prev, False, None)

            @block.sync
            def _(s):
                s.wait_ge(S.h["xst0"], S.val("xst0"))
                s.wait_ge(S.h["xst1"], S.val("xst1"))

    return nc


def prep_ffn_inputs(gate_w, up_w, down_w, gate_s, up_s, down_s):
    d = {}
    d["gWT"] = np.ascontiguousarray(
        np.asarray(gate_w).reshape(NI, 128, NCC, 128).transpose(0, 3, 2, 1)
        .reshape(NI, 128, NCC * 128))
    d["uWT"] = np.ascontiguousarray(
        np.asarray(up_w).reshape(NI, 128, NCC, 128).transpose(0, 3, 2, 1)
        .reshape(NI, 128, NCC * 128))
    d["dWT3"] = np.ascontiguousarray(
        np.asarray(down_w).reshape(NCC, 128, NI, 128).transpose(0, 3, 2, 1)
        .reshape(NCC, 128, NI * 128))
    d["gS"] = np.ascontiguousarray(np.asarray(gate_s).reshape(NI, 128).T)
    d["uS"] = np.ascontiguousarray(np.asarray(up_s).reshape(NI, 128).T)
    d["dS"] = np.ascontiguousarray(np.asarray(down_s).reshape(NCC, 128).T)
    return d




def prep_attn_inputs(inputs):
    d = {}

    def tile_in(Wname, nheads):
        Wm = np.asarray(inputs[Wname])
        return np.ascontiguousarray(
            Wm.reshape(NCC, 128, nheads, 128).transpose(2, 1, 0, 3)
            .reshape(nheads, 128, NCC * 128))

    def tile_out(Wname):
        Wm = np.asarray(inputs[Wname])
        return np.ascontiguousarray(
            Wm.reshape(NH, 128, NCC, 128).transpose(2, 1, 0, 3)
            .reshape(NCC, 128, NH * 128))

    d["aWqT"] = tile_in("L0_attn_a_Wq", NH)
    d["aWkT"] = tile_in("L0_attn_a_Wk", NKV_A)
    d["aWvT"] = tile_in("L0_attn_a_Wv", NKV_A)
    d["aWoT"] = tile_out("L0_attn_a_Wo")
    d["bWqT"] = tile_in("L0_attn_b_Wq", NH)
    d["bWkT"] = tile_in("L0_attn_b_Wk", 1)
    d["bWvT"] = tile_in("L0_attn_b_Wv", 1)
    d["bWoT"] = tile_out("L0_attn_b_Wo")

    d["bqA"] = np.ascontiguousarray(np.asarray(inputs["L0_attn_a_bq"]).reshape(NH, 128).T)
    d["bkA"] = np.ascontiguousarray(np.asarray(inputs["L0_attn_a_bk"]).reshape(NKV_A, 128).T)
    d["bvA"] = np.ascontiguousarray(np.asarray(inputs["L0_attn_a_bv"]).reshape(NKV_A, 128).T)
    d["boA"] = np.ascontiguousarray(np.asarray(inputs["L0_attn_a_bo"]).reshape(NCC, 128).T)
    d["boB"] = np.ascontiguousarray(np.asarray(inputs["L0_attn_b_bo"]).reshape(NCC, 128).T)

    t = np.arange(T, dtype=np.float64)
    th_a = 10000.0 ** (-np.arange(0, HD, 2, dtype=np.float64) / HD)
    ang_a = np.outer(th_a, t)
    cosA = np.empty((128, T), np.float32)
    sinA = np.empty((128, T), np.float32)
    cosA[0::2] = np.cos(ang_a); cosA[1::2] = np.cos(ang_a)
    sinA[0::2] = -np.sin(ang_a); sinA[1::2] = np.sin(ang_a)
    d["cosA"], d["sinA"] = np.ascontiguousarray(cosA), np.ascontiguousarray(sinA)
    th_b = np.exp(-np.log(10000.0) * (np.arange(0, 64, 2, dtype=np.float64) / 64.0))
    ang_b = np.outer(th_b, t)
    cosB = np.ones((128, T), np.float32)
    sinB = np.zeros((128, T), np.float32)
    cosB[0:32] = np.cos(ang_b); cosB[32:64] = np.cos(ang_b)
    sinB[0:32] = -np.sin(ang_b); sinB[32:64] = np.sin(ang_b)
    d["cosB"], d["sinB"] = np.ascontiguousarray(cosB), np.ascontiguousarray(sinB)

    Ra = np.zeros((128, 128), np.float32)
    for i in range(64):
        Ra[2 * i, 2 * i + 1] = 1.0
        Ra[2 * i + 1, 2 * i] = 1.0
    Rb = np.zeros((128, 128), np.float32)
    for dd in range(32):
        Rb[dd, dd + 32] = 1.0
        Rb[dd + 32, dd] = 1.0
    d["RaI"], d["RbI"] = Ra, Rb
    d["identI"] = np.eye(128, dtype=np.float32)
    d["onesI"] = np.ones((128, 128), np.float32)

    kk = np.arange(128)[:, None]
    tt = np.arange(QT)[None, :]
    masks = np.zeros((128, 8 * QT), np.float32)
    for m in range(4):
        dd = m * 128
        masks[:, m * QT:(m + 1) * QT] = (kk + dd <= tt).astype(np.float32)
    for m in range(4):
        dd = -WINDOW + m * 128
        masks[:, (4 + m) * QT:(5 + m) * QT] = ((tt - kk) <= dd + WINDOW - 1).astype(np.float32)
    d["masksI"] = masks
    return d


SUBLAYERS = ("attn_a", "ffn", "attn_b", "ffn")


def make_in_maps(inputs):
    x = np.asarray(inputs["L0_attn_a_x"])
    wd = prep_ffn_inputs(inputs["L0_ffn_gate_w"], inputs["L0_ffn_up_w"],
                         inputs["L0_ffn_down_w"], inputs["L0_ffn_gate_s"],
                         inputs["L0_ffn_up_s"], inputs["L0_ffn_down_s"])
    wd.update(prep_attn_inputs(inputs))
    return [dict(wd, x=np.ascontiguousarray(x[b])) for b in range(8)]


def kernel(**inputs):
    nc = build_program(SUBLAYERS)
    in_maps = make_in_maps(inputs)
    res = run_bass_kernel_spmd(nc, in_maps, core_ids=list(range(8)))
    return np.stack([r["out"] for r in res.results], axis=0)


# version 02
# speedup vs baseline: 1.1892x; 1.1892x over previous
import numpy as np
import ml_dtypes
import concourse.bass as bass
import concourse.mybir as mybir
from concourse.bass_utils import run_bass_kernel_spmd

F32 = mybir.dt.float32
BF16 = mybir.dt.bfloat16
AF = mybir.ActivationFunctionType
ALU = mybir.AluOpType
AX = mybir.AxisListType
bfdt = ml_dtypes.bfloat16

T = 4096
C = 2048
HD = 128
NH = 16
NKV_A = 4
INTER = 5632
NI = INTER // 128
NCC = C // 128
QT = 512
NQT = T // QT
GROUP = 1024
NG = T // GROUP
TPG = GROUP // QT
NTT = 32
EPS = 1e-5
WINDOW = 512


class Sems:
    def __init__(self, nc, stack, names):
        self.h = {}
        self.n = {}
        for nm in names:
            self.h[nm] = stack.enter_context(nc.semaphore(nm))
            self.n[nm] = 0

    def inc(self, nm, k):
        self.n[nm] += k
        return self.h[nm], k

    def val(self, nm):
        return self.n[nm]


SEM_NAMES = [
    "xld0", "xld1", "yld0", "yld1", "hld", "hst0", "hst1", "xst",
    "yst0", "yst1", "xst0", "xst1",
    "vln", "sln", "happ",
    "gld0", "gld1", "uld0", "uld1", "dwld0", "dwld1",
    "mm", "act", "zst", "ymm", "yev",
    "bar", "aux", "aux2",
    "kw0", "kw1", "vw0", "vw1", "tabc0", "tabc1", "tabs0", "tabs1",
    "hq0", "hq1", "qw0", "qw1", "ow0", "ow1",
    "kev1", "kev2", "kev3", "kmax", "vev1", "vev2",
    "qev1", "qev2", "qev3", "cr", "cev", "varg", "pexp", "pmask", "lr", "oev",
]


def emit_ln_phase(block, env, x_src, x_dst, y_src, do_ln, h_dst):
    S, B = env["sems"], env["bufs"]
    xt, yt, ht = B["xt"], B["yt"], B["ht"]
    st8, bn6, sqs = B["st8"], B["bn6"], B["sqs"]
    have_y = y_src is not None
    b = {nm: S.val(nm) for nm in ("xld0", "xld1", "yld0", "yld1", "vln", "sln",
                                  "happ", "xst0", "xst1", "hst0", "hst1")}

    @block.sync
    def _(s):
        for i in range(NTT + 2):
            if 2 <= i:
                j = i - 2
                s.wait_ge(S.h["vln"], b["vln"] + j + 1)
                s.dma_start(out=x_dst[j * 128:(j + 1) * 128, :], in_=xt[j % 2][:, :])\
                    .then_inc(*S.inc(f"xst{j % 2}", 16))
                if do_ln:
                    s.dma_start(out=h_dst[j * 128:(j + 1) * 128, :], in_=ht[j % 2][:, :])\
                        .then_inc(*S.inc(f"hst{j % 2}", 16))
            if i < NTT:
                if i >= 2:
                    s.wait_ge(S.h[f"xst{i % 2}"], b[f"xst{i % 2}"] + 16 * (i // 2))
                s.dma_start(out=xt[i % 2][:, :], in_=x_src[i * 128:(i + 1) * 128, :])\
                    .then_inc(*S.inc(f"xld{i % 2}", 16))
                if have_y:
                    s.dma_start_transpose(yt[i % 2][:, :], y_src[:, i * 128:(i + 1) * 128])\
                        .then_inc(*S.inc(f"yld{i % 2}", 16))

    @block.vector
    def _(v):
        for i in range(NTT):
            p = i % 2
            v.wait_ge(S.h[f"xld{p}"], b[f"xld{p}"] + 16 * (i // 2 + 1))
            if have_y:
                v.wait_ge(S.h[f"yld{p}"], b[f"yld{p}"] + 16 * (i // 2 + 1))
            if do_ln:
                if have_y:
                    v.tensor_tensor(xt[p][:, :], xt[p][:, :], yt[p][:, :], ALU.add)
                    v.drain()
                if i >= 2:
                    v.wait_ge(S.h["happ"], b["happ"] + i - 1)
                for g in range(4):
                    v.bn_stats(bn6[:, p * 24 + 6 * g: p * 24 + 6 * g + 6],
                               xt[p][:, 512 * g:512 * (g + 1)])
                v.drain()
                b6 = bn6[:, p * 24: p * 24 + 24].rearrange("a (g s) -> a g s", s=3)
                col = st8[:, p * 8: p * 8 + 8]
                v.tensor_reduce(col[:, 0:1], b6[:, :, 1], AX.X, ALU.add)
                v.tensor_reduce(col[:, 1:2], b6[:, :, 2], AX.X, ALU.add)
                v.scalar_tensor_tensor(sqs[:, p * 8: p * 8 + 8],
                                       b6[:, :, 1], 1.0, b6[:, :, 1],
                                       ALU.bypass, ALU.mult,
                                       accum_out=col[:, 2:3])
                v.drain()
                v.tensor_scalar(col[:, 4:5], col[:, 0:1], 1.0 / 8, None, ALU.mult)
                v.tensor_scalar(col[:, 6:7], col[:, 2:3], 1.0 / 8, None, ALU.mult)
                v.drain()
                v.tensor_scalar(col[:, 5:6], col[:, 4:5], -1.0, None, ALU.mult)
                v.scalar_tensor_tensor(col[:, 6:7], col[:, 1:2], 1.0 / 2048,
                                       col[:, 6:7], ALU.mult, ALU.add)
                v.drain()
                v.scalar_tensor_tensor(col[:, 6:7], col[:, 4:5], col[:, 5:6],
                                       col[:, 6:7], ALU.mult, ALU.add)
                v.drain()
                v.tensor_single_scalar(col[:, 6:7], col[:, 6:7], EPS, ALU.add)\
                    .then_inc(*S.inc("sln", 1))
                v.wait_ge(S.h["happ"], b["happ"] + i + 1)
                if i >= 2:
                    v.wait_ge(S.h[f"hst{p}"], b[f"hst{p}"] + 16 * (i // 2))
                v.tensor_scalar(ht[p][:, :], xt[p][:, :], col[:, 4:5], col[:, 7:8],
                                ALU.subtract, ALU.mult).then_inc(*S.inc("vln", 1))
            elif have_y:
                v.tensor_tensor(xt[p][:, :], xt[p][:, :], yt[p][:, :], ALU.add)\
                    .then_inc(*S.inc("vln", 1))
            else:
                v.tensor_single_scalar(xt[p][:, :], xt[p][:, :], 0.0, ALU.add)\
                    .then_inc(*S.inc("vln", 1))

    if do_ln:
        @block.scalar
        def _(sc):
            for i in range(NTT):
                p = i % 2
                col = st8[:, p * 8: p * 8 + 8]
                sc.wait_ge(S.h["sln"], b["sln"] + i + 1)
                sc.activation(col[:, 7:8], col[:, 6:7], AF.Ln)
                sc.drain()
                sc.activation(col[:, 7:8], col[:, 7:8], AF.Exp, scale=-0.5)\
                    .then_inc(*S.inc("happ", 1))


def emit_ffn_phase(block, env, h_src, y_dst, w):
    S, B = env["sems"], env["bufs"]
    hsb, zsb = B["hsb"], B["zsb"]
    gw, uw, dw = B["gw"], B["uw"], B["dw"]
    silu, ysb = B["silu"], B["ysb"]
    pg, pu, py = B["pg"], B["pu"], B["py"]
    gS, uS, dS = w["gS"], w["uS"], w["dS"]

    b = {nm: S.val(nm) for nm in ("hld", "gld0", "gld1", "uld0", "uld1",
                                  "dwld0", "dwld1", "mm", "act", "zst",
                                  "ymm", "yev", "yst0", "yst1")}

    gWTr = w["gWT"]
    uWTr = w["uWT"]
    dWT3r = w["dWT3"]

    mm_g, mm_u, mm_y = {}, {}, {}
    mmc = 0
    for grp in range(NG):
        for i in range(NI):
            for tt in range(TPG):
                mmc += 1
                mm_g[(grp, i, tt)] = mmc
            for tt in range(TPG):
                mmc += 1
                mm_u[(grp, i, tt)] = mmc
        for co in range(NCC):
            for tt in range(TPG):
                mmc += 1
                mm_y[(grp, co, tt)] = mmc

    def chunk_gi(n):
        return divmod(n, NI)

    @block.gpsimd
    def _(g):
        for grp in range(NG):
            for i in range(NI):
                n = grp * NI + i
                par = n % 2
                if n >= 2:
                    pg_, pi_ = chunk_gi(n - 2)
                    g.wait_ge(S.h["mm"], b["mm"] + mm_g[(pg_, pi_, TPG - 1)])
                g.dma_start(out=gw[par][:, :], in_=gWTr[i, :, :])\
                    .then_inc(*S.inc(f"gld{par}", 16))
                if n >= 2:
                    pg_, pi_ = chunk_gi(n - 2)
                    g.wait_ge(S.h["mm"], b["mm"] + mm_u[(pg_, pi_, TPG - 1)])
                g.dma_start(out=uw[par][:, :], in_=uWTr[i, :, :])\
                    .then_inc(*S.inc(f"uld{par}", 16))
            for co in range(NCC):
                n = grp * NCC + co
                if n >= 2:
                    pg_, pco_ = divmod(n - 2, NCC)
                    g.wait_ge(S.h["mm"], b["mm"] + mm_y[(pg_, pco_, TPG - 1)])
                g.dma_start(out=dw[n % 2][:, :], in_=dWT3r[co, :, :])\
                    .then_inc(*S.inc(f"dwld{n % 2}", 16))

    @block.sync
    def _(s):
        for grp in range(NG):
            if grp >= 1:
                s.wait_ge(S.h["mm"], b["mm"] + mm_u[(grp - 1, NI - 1, TPG - 1)])
            for cc in range(NCC):
                for tt in range(TPG):
                    t0 = grp * GROUP + tt * QT
                    s.dma_start_transpose(
                        hsb[:, cc * GROUP + tt * QT: cc * GROUP + (tt + 1) * QT],
                        h_src[t0:t0 + QT, cc * 128:(cc + 1) * 128],
                    ).then_inc(*S.inc("hld", 16))

    @block.tensor
    def _(t):
        for grp in range(NG):
            t.wait_ge(S.h["hld"], b["hld"] + 16 * NCC * TPG * (grp + 1))
            for i in range(NI):
                n = grp * NI + i
                par = n % 2
                t.wait_ge(S.h[f"gld{par}"], b[f"gld{par}"] + 16 * (n // 2 + 1))
                for tt in range(TPG):
                    gi = n * TPG + tt
                    if n >= 1:
                        t.wait_ge(S.h["act"], b["act"] + gi - TPG + 1)
                    for cc in range(NCC):
                        ins = t.matmul(pg[tt][:, :], gw[par][:, cc * 128:(cc + 1) * 128],
                                       hsb[:, cc * GROUP + tt * QT: cc * GROUP + (tt + 1) * QT],
                                       start=(cc == 0), stop=(cc == NCC - 1))
                    ins.then_inc(*S.inc("mm", 1))
                t.wait_ge(S.h[f"uld{par}"], b[f"uld{par}"] + 16 * (n // 2 + 1))
                for tt in range(TPG):
                    gi = n * TPG + tt
                    if n >= 1:
                        t.wait_ge(S.h["zst"], b["zst"] + gi - TPG + 1)
                    for cc in range(NCC):
                        ins = t.matmul(pu[tt][:, :], uw[par][:, cc * 128:(cc + 1) * 128],
                                       hsb[:, cc * GROUP + tt * QT: cc * GROUP + (tt + 1) * QT],
                                       start=(cc == 0), stop=(cc == NCC - 1))
                    ins.then_inc(*S.inc("mm", 1))
            t.wait_ge(S.h["zst"], b["zst"] + (grp + 1) * NI * TPG)
            for co in range(NCC):
                n = grp * NCC + co
                t.wait_ge(S.h[f"dwld{n % 2}"], b[f"dwld{n % 2}"] + 16 * (n // 2 + 1))
                for tt in range(TPG):
                    yi = n * TPG + tt
                    if yi >= 2:
                        t.wait_ge(S.h["yev"], b["yev"] + yi - 1)
                    for ich in range(NI):
                        ins = t.matmul(py[yi % 2][:, :], dw[n % 2][:, ich * 128:(ich + 1) * 128],
                                       zsb[:, ich * GROUP + tt * QT: ich * GROUP + (tt + 1) * QT],
                                       start=(ich == 0), stop=(ich == NI - 1))
                    ins.then_inc(*S.inc("mm", 1))
                    t.sem_inc(*S.inc("ymm", 1))

    @block.scalar
    def _(sc):
        for grp in range(NG):
            for i in range(NI):
                for tt in range(TPG):
                    gi = (grp * NI + i) * TPG + tt
                    sc.wait_ge(S.h["mm"], b["mm"] + mm_g[(grp, i, tt)])
                    if gi >= 2:
                        sc.wait_ge(S.h["zst"], b["zst"] + gi - 1)
                    sc.activation(silu[gi % 2][:, :], pg[tt][:, :], AF.Silu,
                                  scale=gS[:, i:i + 1]).then_inc(*S.inc("act", 1))
            for co in range(NCC):
                for tt in range(TPG):
                    yi = (grp * NCC + co) * TPG + tt
                    t0 = grp * GROUP + tt * QT
                    sc.wait_ge(S.h["yev"], b["yev"] + yi + 1)
                    sc.dma_start(out=y_dst[co * 128:(co + 1) * 128, t0:t0 + QT],
                                 in_=ysb[yi % 2][:, :]).then_inc(*S.inc(f"yst{yi % 2}", 16))

    @block.vector
    def _(v):
        for grp in range(NG):
            for i in range(NI):
                for tt in range(TPG):
                    gi = (grp * NI + i) * TPG + tt
                    v.wait_ge(S.h["mm"], b["mm"] + mm_u[(grp, i, tt)])
                    v.wait_ge(S.h["act"], b["act"] + gi + 1)
                    if grp >= 1 and i == 0 and tt == 0:
                        v.wait_ge(S.h["mm"], b["mm"] + mm_y[(grp - 1, NCC - 1, TPG - 1)])
                    v.scalar_tensor_tensor(
                        zsb[:, i * GROUP + tt * QT: i * GROUP + (tt + 1) * QT],
                        pu[tt][:, :], uS[:, i:i + 1], silu[gi % 2][:, :],
                        ALU.mult, ALU.mult).then_inc(*S.inc("zst", 1))
            for co in range(NCC):
                for tt in range(TPG):
                    yi = (grp * NCC + co) * TPG + tt
                    v.wait_ge(S.h["mm"], b["mm"] + mm_y[(grp, co, tt)])
                    if yi >= 2:
                        v.wait_ge(S.h[f"yst{yi % 2}"], b[f"yst{yi % 2}"] + 16 * (yi // 2))
                    v.tensor_scalar(ysb[yi % 2][:, :], py[yi % 2][:, :],
                                    dS[:, co:co + 1], None, ALU.mult)\
                        .then_inc(*S.inc("yev", 1))




def attn_chunks(kind, qi):
    Q0 = qi * QT
    lo = 0 if kind == "a" else max(0, Q0 - WINDOW)
    out = []
    for K0 in range(lo, Q0 + QT - 128 + 1, 128):
        d = K0 - Q0
        if d >= 0:
            slot = d // 128
        elif kind == "b":
            slot = 4 + (d + WINDOW) // 128
        else:
            slot = None
        out.append((K0, slot))
    return out


def emit_attn_phase(block, env, h_src, y_dst, w, kind):
    sfx = env.get("sfx", "")
    S, B = env["sems"], env["bufs"]
    nkv = NKV_A if kind == "a" else 1
    r_imm = 1.0 if kind == "a" else float(np.sqrt(2.0))
    qk_scale = 0.5 if kind == "a" else 0.25

    masks = B["masks"]
    ident, ones = B["ident"], B["ones"]
    shiftc = B["shiftc"]
    R_sb = B["Ra"] if kind == "a" else B["Rb"]
    bq = B.get("bqA") if kind == "a" else None
    bk = B.get("bkA") if kind == "a" else None
    bv = B.get("bvA") if kind == "a" else None
    bo = B["boA"] if kind == "a" else B["boB"]
    cosT, sinT = (w["cosA"], w["sinA"]) if kind == "a" else (w["cosB"], w["sinB"])
    WqT, WkT, WvT, WoT = w["WqT"], w["WkT"], w["WvT"], w["WoT"]

    hq = B["hq"]
    ksb, vsb = B["ksb"], B["vsb"]
    tabc, tabs = B["tabc"], B["tabs"]

    base = {nm: S.val(nm) for nm in (
        "mm", "kw0", "vw0", "tabc0", "tabc1", "tabs0", "tabs1",
        "hq0", "hq1", "qw0", "qw1", "ow0", "ow1",
        "kev1", "kev2", "vev1", "vev2",
        "qev1", "qev2", "pexp", "pmask", "lr",
        "oev", "yev", "yst0", "yst1")}

    chunks = {qi: attn_chunks(kind, qi) for qi in range(NQT)}
    pair_of = {}
    pairs_of_g = {}
    pmask_idx = {}
    pgc = 0
    pmc = 0
    for qi in range(NQT):
        for h in range(NH):
            g = qi * NH + h
            lst = []
            for n, (K0, slot) in enumerate(chunks[qi]):
                pgc += 1
                pair_of[(g, n)] = pgc
                lst.append(pgc)
                if slot is not None:
                    pmc += 1
                    pmask_idx[pgc] = pmc
            pairs_of_g[g] = lst

    mmpos = {}

    from contextlib import ExitStack

    with ExitStack() as _st:
        _sb = lambda n, sh, dt: _st.enter_context(env["nc"].sbuf_tensor(n + sfx, sh, dt))
        _ps = lambda n, sh, dt: _st.enter_context(env["nc"].psum_tensor(n + sfx, sh, dt))
        kwb = [_sb(f"kwb{j}", [128, NCC * 128], BF16) for j in range(nkv)]
        vwb = [_sb(f"vwb{j}", [128, NCC * 128], BF16) for j in range(nkv)]
        kscb = _sb("kscb", [128, QT], BF16)
        vscb = _sb("vscb", [128, QT], BF16)
        scr1 = _sb("scr1", [128, QT], F32)
        scr2 = _sb("scr2", [128, QT], F32)
        kraw = _ps("kraw", [128, QT], F32)
        rotb = _ps("rotb", [128, QT], F32)
        vraw = _ps("vraw", [128, QT], F32)
        vtb = _ps("vtb", [128, QT], BF16)

        def kv_idx(tt8, j):
            return tt8 * nkv + j + 1

        @block.tensor
        def _(t):
            for tt8 in range(8):
                for j in range(nkv):
                    idx = kv_idx(tt8, j)
                    t.wait_ge(S.h["kw0"], base["kw0"] + 16 * nkv)
                    t.wait_ge(S.h[f"hq{tt8 % 2}"],
                              base[f"hq{tt8 % 2}"] + 16 * NCC * (tt8 // 2 + 1))
                    if idx >= 2:
                        t.wait_ge(S.h["kev1"], base["kev1"] + idx - 1)
                    for cc in range(NCC):
                        ins = t.matmul(kraw[:, :], kwb[j][:, cc * 128:(cc + 1) * 128],
                                       hq[tt8 % 2][:, cc * QT:(cc + 1) * QT],
                                       start=(cc == 0), stop=(cc == NCC - 1))
                    ins.then_inc(*S.inc("mm", 1))
                    mmpos[("kraw", idx)] = S.val("mm")
                    t.wait_ge(S.h["kev1"], base["kev1"] + idx)
                    if idx >= 2:
                        t.wait_ge(S.h["kev2"], base["kev2"] + idx - 1)
                    ins = t.matmul(rotb[:, :], R_sb[:, :], kscb[:, :],
                                   start=True, stop=True)
                    ins.then_inc(*S.inc("mm", 1))
                    mmpos[("rot", idx)] = S.val("mm")
                    t.wait_ge(S.h["vw0"], base["vw0"] + 16 * nkv)
                    if idx >= 2:
                        t.wait_ge(S.h["vev1"], base["vev1"] + idx - 1)
                    for cc in range(NCC):
                        ins = t.matmul(vraw[:, :], vwb[j][:, cc * 128:(cc + 1) * 128],
                                       hq[tt8 % 2][:, cc * QT:(cc + 1) * QT],
                                       start=(cc == 0), stop=(cc == NCC - 1))
                    ins.then_inc(*S.inc("mm", 1))
                    mmpos[("vraw", idx)] = S.val("mm")
                    t.wait_ge(S.h["vev1"], base["vev1"] + idx)
                    if idx >= 2:
                        t.wait_ge(S.h["vev2"], base["vev2"] + idx - 1)
                    for s_ in range(4):
                        ins = t.transpose(vtb[:, s_ * 128:(s_ + 1) * 128],
                                          vscb[:, s_ * 128:(s_ + 1) * 128],
                                          ident[:, :])
                    ins.then_inc(*S.inc("mm", 1))
                    mmpos[("vt", idx)] = S.val("mm")

        @block.gpsimd
        def _(g):
            for j in range(nkv):
                g.dma_start(out=kwb[j][:, :], in_=WkT[j, :, :])\
                    .then_inc(*S.inc("kw0", 16))
                g.dma_start(out=vwb[j][:, :], in_=WvT[j, :, :])\
                    .then_inc(*S.inc("vw0", 16))

        @block.sync
        def _(s):
            for tt8 in range(8):
                par = tt8 % 2
                if tt8 >= 2:
                    s.wait_ge(S.h["mm"], mmpos[("vraw", kv_idx(tt8 - 2, nkv - 1))])
                for cc in range(NCC):
                    s.dma_start_transpose(
                        hq[par][:, cc * QT:(cc + 1) * QT],
                        h_src[tt8 * QT:(tt8 + 1) * QT, cc * 128:(cc + 1) * 128],
                    ).then_inc(*S.inc(f"hq{par}", 16))
                if tt8 >= 2:
                    s.wait_ge(S.h["kev2"], base["kev2"] + kv_idx(tt8 - 2, nkv - 1))
                s.dma_start(out=tabc[par][:, :], in_=cosT[:, tt8 * QT:(tt8 + 1) * QT])\
                    .then_inc(*S.inc(f"tabc{par}", 16))
                s.dma_start(out=tabs[par][:, :], in_=sinT[:, tt8 * QT:(tt8 + 1) * QT])\
                    .then_inc(*S.inc(f"tabs{par}", 16))

        @block.vector
        def _(v):
            for tt8 in range(8):
                par = tt8 % 2
                for j in range(nkv):
                    idx = kv_idx(tt8, j)
                    v.wait_ge(S.h["mm"], mmpos[("kraw", idx)])
                    if idx >= 2:
                        v.wait_ge(S.h["mm"], mmpos[("rot", idx - 1)])
                    if kind == "a":
                        v.tensor_scalar(kscb[:, :], kraw[:, :], bk[:, j:j + 1],
                                        qk_scale, ALU.add, ALU.mult)\
                            .then_inc(*S.inc("kev1", 1))
                    else:
                        v.tensor_scalar(kscb[:, :], kraw[:, :], qk_scale, None,
                                        ALU.mult).then_inc(*S.inc("kev1", 1))
                    v.drain()
                    v.wait_ge(S.h[f"tabc{par}"], base[f"tabc{par}"] + 16 * (tt8 // 2 + 1))
                    v.wait_ge(S.h[f"tabs{par}"], base[f"tabs{par}"] + 16 * (tt8 // 2 + 1))
                    v.tensor_tensor(scr1[:, :], kscb[:, :], tabc[par][:, :], ALU.mult)
                    v.wait_ge(S.h["mm"], mmpos[("rot", idx)])
                    v.tensor_tensor(scr2[:, :], rotb[:, :], tabs[par][:, :], ALU.mult)
                    v.drain()
                    v.tensor_tensor(ksb[:, j * T + tt8 * QT: j * T + (tt8 + 1) * QT],
                                    scr1[:, :], scr2[:, :], ALU.add)\
                        .then_inc(*S.inc("kev2", 1))
                    v.drain()
                    v.wait_ge(S.h["mm"], mmpos[("vraw", idx)])
                    if idx >= 2:
                        v.wait_ge(S.h["mm"], mmpos[("vt", idx - 1)])
                    if kind == "a":
                        v.tensor_scalar(vscb[:, :], vraw[:, :], bv[:, j:j + 1],
                                        None, ALU.add).then_inc(*S.inc("vev1", 1))
                    else:
                        v.tensor_copy(vscb[:, :], vraw[:, :])\
                            .then_inc(*S.inc("vev1", 1))
                    v.wait_ge(S.h["mm"], mmpos[("vt", idx)])
                    v.tensor_copy(
                        vsb[:, (j * 32 + tt8 * 4) * 128:(j * 32 + tt8 * 4 + 4) * 128],
                        vtb[:, :]).then_inc(*S.inc("vev2", 1))

        emit_barrier(block, env, ["kev1", "kev2", "vev1", "vev2",
                                  "kw0", "vw0", "tabc0", "tabc1",
                                  "tabs0", "tabs1", "hq0", "hq1", "mm"])

    NPB = 4
    with ExitStack() as _st:
        _sb = lambda n, sh, dt: _st.enter_context(env["nc"].sbuf_tensor(n + sfx, sh, dt))
        _ps = lambda n, sh, dt: _st.enter_context(env["nc"].psum_tensor(n + sfx, sh, dt))
        qwb = [_sb(f"qwb{i}", [128, NCC * 128], BF16) for i in range(2)]
        owb = [_sb(f"owb{i}", [128, NH * 128], BF16) for i in range(2)]
        qscb = _sb("qscb", [128, QT], BF16)
        qsb = [_sb(f"qsb{i}", [128, QT], BF16) for i in range(2)]
        scr1 = _sb("scr1q", [128, QT], F32)
        scr2 = _sb("scr2q", [128, QT], F32)
        psb = [_sb(f"psb{i}", [128, QT], BF16) for i in range(NPB)]
        Rsb2 = [_sb(f"Rsb{i}", [128, QT], F32) for i in range(2)]
        osb = _sb("osb", [128, NH * QT], BF16)
        ysb2 = [_sb(f"ysb2{i}", [128, QT], BF16) for i in range(2)]
        qrot = _ps("qrot", [128, QT], F32)
        sbk = [_ps(f"sb{i}", [128, QT], F32) for i in range(NPB)]
        obk = [_ps(f"ob{i}", [128, QT], F32) for i in range(2)]
        lbk = _ps("lb", [128, QT], F32)

        @block.tensor
        def _(t):
            for qi in range(NQT):
                lst = chunks[qi]
                for h in range(NH):
                    g = qi * NH + h
                    j = h % nkv
                    par = g % 2
                    t.wait_ge(S.h[f"qw{par}"], base[f"qw{par}"] + 16 * (g // 2 + 1))
                    t.wait_ge(S.h[f"hq{qi % 2}"],
                              base[f"hq{qi % 2}"] + 16 * NCC * (4 + qi // 2 + 1))
                    if g >= 1:
                        t.wait_ge(S.h["qev2"], base["qev2"] + g)
                    for cc in range(NCC):
                        ins = t.matmul(qrot[:, :], qwb[par][:, cc * 128:(cc + 1) * 128],
                                       hq[qi % 2][:, cc * QT:(cc + 1) * QT],
                                       start=(cc == 0), stop=(cc == NCC - 1))
                    ins.then_inc(*S.inc("mm", 1))
                    mmpos[("qraw", g)] = S.val("mm")
                    t.wait_ge(S.h["qev1"], base["qev1"] + g + 1)
                    ins = t.matmul(qrot[:, :], R_sb[:, :], qscb[:, :],
                                   start=True, stop=True)
                    ins.then_inc(*S.inc("mm", 1))
                    mmpos[("rotq", g)] = S.val("mm")
                    for n, (K0, slot) in enumerate(lst):
                        pg = pair_of[(g, n)]
                        t.wait_ge(S.h["qev2"], base["qev2"] + g + 1)
                        if pg > NPB:
                            t.wait_ge(S.h["pexp"], base["pexp"] + pg - NPB)
                        if qi >= 1 and h == 0 and n < NPB:
                            t.wait_ge(S.h["yev"], base["yev"] + qi * NCC)
                        ins = t.matmul(sbk[pg % NPB][:, :],
                                       ksb[:, j * T + K0: j * T + K0 + 128],
                                       qsb[par][:, :], start=True, stop=True)
                        ins.then_inc(*S.inc("mm", 1))
                        mmpos[("s", pg)] = S.val("mm")
                        t.wait_ge(S.h["pexp"], base["pexp"] + pg)
                        if slot is not None:
                            t.wait_ge(S.h["pmask"], base["pmask"] + pmask_idx[pg])
                        if n == 0:
                            if g >= 1:
                                t.wait_ge(S.h["lr"], base["lr"] + g)
                            if g >= 2:
                                t.wait_ge(S.h["oev"], base["oev"] + g - 1)
                        t.matmul(lbk[:, :], ones[:, :], psb[pg % NPB][:, :],
                                 start=(n == 0), stop=(n == len(lst) - 1))
                        ins = t.matmul(obk[par][:, :],
                                       vsb[:, (j * 32 + (K0 // 128)) * 128:
                                           (j * 32 + (K0 // 128) + 1) * 128],
                                       psb[pg % NPB][:, :],
                                       start=(n == 0), stop=(n == len(lst) - 1))
                        ins.then_inc(*S.inc("mm", 1))
                        mmpos[("o", pg)] = S.val("mm")
                for co in range(NCC):
                    oc = qi * NCC + co
                    t.wait_ge(S.h[f"ow{oc % 2}"], base[f"ow{oc % 2}"] + 16 * (oc // 2 + 1))
                    t.wait_ge(S.h["oev"], base["oev"] + (qi + 1) * NH)
                    t.wait_ge(S.h["pexp"],
                              base["pexp"] + pair_of[(qi * NH + NH - 1, len(lst) - 1)])
                    if oc >= NPB:
                        t.wait_ge(S.h["yev"], base["yev"] + oc - NPB + 1)
                    for hh in range(NH):
                        ins = t.matmul(sbk[co % NPB][:, :],
                                       owb[oc % 2][:, hh * 128:(hh + 1) * 128],
                                       osb[:, hh * QT:(hh + 1) * QT],
                                       start=(hh == 0), stop=(hh == NH - 1))
                    ins.then_inc(*S.inc("mm", 1))
                    mmpos[("yq", qi, co)] = S.val("mm")

        @block.gpsimd
        def _(g_):
            for qi in range(NQT):
                for h in range(NH):
                    g = qi * NH + h
                    if g >= 2:
                        g_.wait_ge(S.h["mm"], mmpos[("qraw", g - 2)])
                    g_.dma_start(out=qwb[g % 2][:, :], in_=WqT[h, :, :])\
                        .then_inc(*S.inc(f"qw{g % 2}", 16))
                for co in range(NCC):
                    oc = qi * NCC + co
                    if oc >= 2:
                        g_.wait_ge(S.h["mm"], mmpos[("yq", (oc - 2) // NCC,
                                                     (oc - 2) % NCC)])
                    g_.dma_start(out=owb[oc % 2][:, :], in_=WoT[co, :, :])\
                        .then_inc(*S.inc(f"ow{oc % 2}", 16))

        @block.sync
        def _(s):
            for qi in range(NQT):
                par = qi % 2
                if qi >= 2:
                    s.wait_ge(S.h["mm"], mmpos[("qraw", (qi - 2) * NH + NH - 1)])
                for cc in range(NCC):
                    s.dma_start_transpose(
                        hq[par][:, cc * QT:(cc + 1) * QT],
                        h_src[qi * QT:(qi + 1) * QT, cc * 128:(cc + 1) * 128],
                    ).then_inc(*S.inc(f"hq{par}", 16))
                if qi >= 2:
                    s.wait_ge(S.h["qev2"], base["qev2"] + (qi - 2) * NH + NH)
                s.dma_start(out=tabc[par][:, :], in_=cosT[:, qi * QT:(qi + 1) * QT])\
                    .then_inc(*S.inc(f"tabc{par}", 16))
                s.dma_start(out=tabs[par][:, :], in_=sinT[:, qi * QT:(qi + 1) * QT])\
                    .then_inc(*S.inc(f"tabs{par}", 16))

        @block.vector
        def _(v):
            for qi in range(NQT):
                par8 = qi % 2
                for h in range(NH):
                    g = qi * NH + h
                    par = g % 2
                    v.wait_ge(S.h["mm"], mmpos[("qraw", g)])
                    if g >= 1:
                        v.wait_ge(S.h["mm"], mmpos[("rotq", g - 1)])
                    if g >= 2:
                        v.wait_ge(S.h["mm"], mmpos[("s", pairs_of_g[g - 2][-1])])
                    if h == 0 and qi >= 1:
                        v.wait_ge(S.h["mm"], mmpos[("yq", qi - 1, NCC - 1)])
                    if kind == "a":
                        v.tensor_scalar(qscb[:, :], qrot[:, :], bq[:, h:h + 1],
                                        qk_scale, ALU.add, ALU.mult)\
                            .then_inc(*S.inc("qev1", 1))
                    else:
                        v.tensor_scalar(qscb[:, :], qrot[:, :], qk_scale, None,
                                        ALU.mult).then_inc(*S.inc("qev1", 1))
                    v.drain()
                    v.wait_ge(S.h[f"tabc{par8}"],
                              base[f"tabc{par8}"] + 16 * (4 + qi // 2 + 1))
                    v.wait_ge(S.h[f"tabs{par8}"],
                              base[f"tabs{par8}"] + 16 * (4 + qi // 2 + 1))
                    v.tensor_tensor(scr1[:, :], qscb[:, :], tabc[par8][:, :], ALU.mult)
                    v.wait_ge(S.h["mm"], mmpos[("rotq", g)])
                    v.tensor_tensor(scr2[:, :], qrot[:, :], tabs[par8][:, :], ALU.mult)
                    v.drain()
                    v.tensor_tensor(qsb[par][:, :], scr1[:, :], scr2[:, :], ALU.add)\
                        .then_inc(*S.inc("qev2", 1))
                    v.drain()
                    for n, (K0, slot) in enumerate(chunks[qi]):
                        pg = pair_of[(g, n)]
                        if slot is not None:
                            v.wait_ge(S.h["pexp"], base["pexp"] + pg)
                            v.tensor_tensor(psb[pg % NPB][:, :], psb[pg % NPB][:, :],
                                            masks[:, slot * QT:(slot + 1) * QT],
                                            ALU.mult).then_inc(*S.inc("pmask", 1))
                    v.wait_ge(S.h["lr"], base["lr"] + g + 1)
                    v.wait_ge(S.h["mm"], mmpos[("o", pairs_of_g[g][-1])])
                    v.tensor_tensor(osb[:, h * QT:(h + 1) * QT], obk[par][:, :],
                                    Rsb2[par][:, :], ALU.mult)\
                        .then_inc(*S.inc("oev", 1))
                for co in range(NCC):
                    oc = qi * NCC + co
                    v.wait_ge(S.h["mm"], mmpos[("yq", qi, co)])
                    if oc >= 2:
                        v.wait_ge(S.h[f"yst{oc % 2}"],
                                  base[f"yst{oc % 2}"] + 16 * (oc // 2))
                    v.tensor_scalar(ysb2[oc % 2][:, :], sbk[co % NPB][:, :],
                                    bo[:, co:co + 1], None, ALU.add)\
                        .then_inc(*S.inc("yev", 1))

        @block.scalar
        def _(sc):
            for qi in range(NQT):
                for h in range(NH):
                    g = qi * NH + h
                    par = g % 2
                    for n, (K0, slot) in enumerate(chunks[qi]):
                        pg = pair_of[(g, n)]
                        sc.wait_ge(S.h["mm"], mmpos[("s", pg)])
                        if pg > NPB:
                            sc.wait_ge(S.h["mm"], mmpos[("o", pg - NPB)])
                        sc.activation(psb[pg % NPB][:, :], sbk[pg % NPB][:, :],
                                      AF.Exp, bias=shiftc[:, 0:1], scale=r_imm)\
                            .then_inc(*S.inc("pexp", 1))
                    sc.wait_ge(S.h["mm"], mmpos[("o", pairs_of_g[g][-1])])
                    if g >= 2:
                        sc.wait_ge(S.h["oev"], base["oev"] + g - 1)
                    sc.activation(Rsb2[par][:, :], lbk[:, :], AF.Ln)
                    sc.drain()
                    sc.activation(Rsb2[par][:, :], Rsb2[par][:, :], AF.Exp, scale=-1.0)\
                        .then_inc(*S.inc("lr", 1))
                for co in range(NCC):
                    oc = qi * NCC + co
                    sc.wait_ge(S.h["yev"], base["yev"] + oc + 1)
                    sc.dma_start(out=y_dst[co * 128:(co + 1) * 128,
                                           qi * QT:(qi + 1) * QT],
                                 in_=ysb2[oc % 2][:, :])\
                        .then_inc(*S.inc(f"yst{oc % 2}", 16))

        emit_barrier(block, env, ["qev1", "qev2", "pexp", "pmask", "lr", "oev",
                                  "yev", "yst0", "yst1", "qw0", "qw1", "ow0",
                                  "ow1", "hq0", "hq1", "tabc0", "tabc1",
                                  "tabs0", "tabs1", "mm"])


def emit_barrier(block, env, waits):
    S = env["sems"]
    vals = [(S.h[nm], S.val(nm)) for nm in waits]

    @block.sync
    def _(s):
        for h, v in vals:
            s.wait_ge(h, v)
        s.sem_inc(S.h["bar"], 1)

    S.n["bar"] += 1
    tgt = S.val("bar")
    for eng in ("tensor", "vector", "scalar", "gpsimd"):
        getattr(block, eng)(lambda e: e.wait_ge(S.h["bar"], tgt))


def build_program(sublayers=("attn_a", "ffn", "attn_b", "ffn"), h_out=False):
    nc = bass.Bass("TRN2", target_bir_lowering=False)
    ap = lambda t: t.ap()

    x_in = ap(nc.dram_tensor("x", [T, C], F32, kind="ExternalInput"))
    out = ap(nc.dram_tensor("out", [T, C], F32, kind="ExternalOutput"))

    gWT = ap(nc.dram_tensor("gWT", [NI, 128, NCC * 128], F32, kind="ExternalInput"))
    uWT = ap(nc.dram_tensor("uWT", [NI, 128, NCC * 128], F32, kind="ExternalInput"))
    dWT3 = ap(nc.dram_tensor("dWT3", [NCC, 128, NI * 128], F32, kind="ExternalInput"))
    gS_d = ap(nc.dram_tensor("gS", [128, NI], F32, kind="ExternalInput"))
    uS_d = ap(nc.dram_tensor("uS", [128, NI], F32, kind="ExternalInput"))
    dS_d = ap(nc.dram_tensor("dS", [128, NCC], F32, kind="ExternalInput"))

    aWqT = ap(nc.dram_tensor("aWqT", [NH, 128, NCC * 128], F32, kind="ExternalInput"))
    aWkT = ap(nc.dram_tensor("aWkT", [NKV_A, 128, NCC * 128], F32, kind="ExternalInput"))
    aWvT = ap(nc.dram_tensor("aWvT", [NKV_A, 128, NCC * 128], F32, kind="ExternalInput"))
    aWoT = ap(nc.dram_tensor("aWoT", [NCC, 128, NH * 128], F32, kind="ExternalInput"))
    bWqT = ap(nc.dram_tensor("bWqT", [NH, 128, NCC * 128], F32, kind="ExternalInput"))
    bWkT = ap(nc.dram_tensor("bWkT", [1, 128, NCC * 128], F32, kind="ExternalInput"))
    bWvT = ap(nc.dram_tensor("bWvT", [1, 128, NCC * 128], F32, kind="ExternalInput"))
    bWoT = ap(nc.dram_tensor("bWoT", [NCC, 128, NH * 128], F32, kind="ExternalInput"))
    bqA_d = ap(nc.dram_tensor("bqA", [128, NH], F32, kind="ExternalInput"))
    bkA_d = ap(nc.dram_tensor("bkA", [128, NKV_A], F32, kind="ExternalInput"))
    bvA_d = ap(nc.dram_tensor("bvA", [128, NKV_A], F32, kind="ExternalInput"))
    boA_d = ap(nc.dram_tensor("boA", [128, NCC], F32, kind="ExternalInput"))
    boB_d = ap(nc.dram_tensor("boB", [128, NCC], F32, kind="ExternalInput"))
    cosA_d = ap(nc.dram_tensor("cosA", [128, T], F32, kind="ExternalInput"))
    sinA_d = ap(nc.dram_tensor("sinA", [128, T], F32, kind="ExternalInput"))
    cosB_d = ap(nc.dram_tensor("cosB", [128, T], F32, kind="ExternalInput"))
    sinB_d = ap(nc.dram_tensor("sinB", [128, T], F32, kind="ExternalInput"))
    RaI = ap(nc.dram_tensor("RaI", [128, 128], F32, kind="ExternalInput"))
    RbI = ap(nc.dram_tensor("RbI", [128, 128], F32, kind="ExternalInput"))
    masksI = ap(nc.dram_tensor("masksI", [128, 8 * QT], F32, kind="ExternalInput"))
    identI = ap(nc.dram_tensor("identI", [128, 128], F32, kind="ExternalInput"))
    onesI = ap(nc.dram_tensor("onesI", [128, 128], F32, kind="ExternalInput"))

    xs_a = ap(nc.dram_tensor("xs_a", [T, C], F32))
    xs_b = ap(nc.dram_tensor("xs_b", [T, C], F32))
    h_sc = ap(nc.dram_tensor("h_sc", [T, C], BF16,
                             kind="ExternalOutput" if h_out else "Internal"))
    y_sc = ap(nc.dram_tensor("y_sc", [C, T], BF16))

    from contextlib import ExitStack
    with ExitStack() as stack:
        S = Sems(nc, stack, SEM_NAMES)
        sb = lambda name, shape, dt: stack.enter_context(nc.sbuf_tensor(name, shape, dt))
        B = {}
        B["xt"] = [sb(f"xt{i}", [128, C], F32) for i in range(2)]
        B["yt"] = [sb(f"yt{i}", [128, C], BF16) for i in range(2)]
        B["ht"] = [sb(f"ht{i}", [128, C], BF16) for i in range(2)]
        B["st8"] = sb("st8", [128, 16], F32)
        B["bn6"] = sb("bn6", [128, 48], F32)
        B["sqs"] = sb("sqs", [128, 16], F32)
        B["gSs"] = sb("gSs", [128, NI], F32)
        B["uSs"] = sb("uSs", [128, NI], F32)
        B["dSs"] = sb("dSs", [128, NCC], F32)
        B["masks"] = sb("masks", [128, 8 * QT], BF16)
        B["ident"] = sb("ident", [128, 128], BF16)
        B["ones"] = sb("ones", [128, 128], BF16)
        B["Ra"] = sb("Ra", [128, 128], BF16)
        B["Rb"] = sb("Rb", [128, 128], BF16)
        B["bqA"] = sb("bqA_s", [128, NH], F32)
        B["bkA"] = sb("bkA_s", [128, NKV_A], F32)
        B["bvA"] = sb("bvA_s", [128, NKV_A], F32)
        B["boA"] = sb("boA_s", [128, NCC], F32)
        B["boB"] = sb("boB_s", [128, NCC], F32)
        B["shiftc"] = sb("shiftc", [128, 1], F32)

        env = {"nc": nc, "sems": S, "bufs": B}

        with nc.Block() as block:
            @block.gpsimd
            def _(g):
                for dst, srcd in [(B["gSs"], gS_d), (B["uSs"], uS_d), (B["dSs"], dS_d),
                                  (B["masks"], masksI), (B["ident"], identI),
                                  (B["ones"], onesI), (B["Ra"], RaI),
                                  (B["Rb"], RbI), (B["bqA"], bqA_d), (B["bkA"], bkA_d),
                                  (B["bvA"], bvA_d), (B["boA"], boA_d),
                                  (B["boB"], boB_d)]:
                    g.dma_start(out=dst[:, :], in_=srcd[:, :]).then_inc(*S.inc("aux", 16))
                g.memset(B["shiftc"][:, :], -20.0).then_inc(*S.inc("aux2", 1))

            auxtot = S.val("aux")
            aux2tot = S.val("aux2")
            for eng in ("tensor", "vector", "scalar"):
                def _w(e):
                    e.wait_ge(S.h["aux"], auxtot)
                    e.wait_ge(S.h["aux2"], aux2tot)
                getattr(block, eng)(_w)

            x_cur = x_in
            x_scr = [xs_a, xs_b]
            y_prev = None
            for k, sl in enumerate(sublayers):
                x_next = x_scr[k % 2]
                emit_ln_phase(block, env, x_cur, x_next, y_prev, True, h_sc)
                emit_barrier(block, env, ["xld0", "xld1", "yld0", "yld1", "xst0",
                                          "xst1", "hst0", "hst1", "vln"])

                if sl == "ffn":
                    with (
                        nc.sbuf_tensor(f"hsb_{k}", [128, NCC * GROUP], BF16) as hsb,
                        nc.sbuf_tensor(f"zsb_{k}", [128, NI * GROUP], BF16) as zsb,
                        nc.sbuf_tensor(f"gw0_{k}", [128, C], BF16) as gw0,
                        nc.sbuf_tensor(f"gw1_{k}", [128, C], BF16) as gw1,
                        nc.sbuf_tensor(f"uw0_{k}", [128, C], BF16) as uw0,
                        nc.sbuf_tensor(f"uw1_{k}", [128, C], BF16) as uw1,
                        nc.sbuf_tensor(f"dw0_{k}", [128, INTER], BF16) as dw0,
                        nc.sbuf_tensor(f"dw1_{k}", [128, INTER], BF16) as dw1,
                        nc.sbuf_tensor(f"silu0_{k}", [128, QT], BF16) as silu0,
                        nc.sbuf_tensor(f"silu1_{k}", [128, QT], BF16) as silu1,
                        nc.sbuf_tensor(f"ysb0_{k}", [128, QT], BF16) as ysb0,
                        nc.sbuf_tensor(f"ysb1_{k}", [128, QT], BF16) as ysb1,
                        nc.psum_tensor(f"pg0_{k}", [128, QT], F32) as pg0,
                        nc.psum_tensor(f"pg1_{k}", [128, QT], F32) as pg1,
                        nc.psum_tensor(f"pu0_{k}", [128, QT], F32) as pu0,
                        nc.psum_tensor(f"pu1_{k}", [128, QT], F32) as pu1,
                        nc.psum_tensor(f"py0_{k}", [128, QT], F32) as py0,
                        nc.psum_tensor(f"py1_{k}", [128, QT], F32) as py1,
                    ):
                        B.update(hsb=hsb, zsb=zsb, gw=[gw0, gw1], uw=[uw0, uw1],
                                 dw=[dw0, dw1], silu=[silu0, silu1],
                                 ysb=[ysb0, ysb1], pg=[pg0, pg1], pu=[pu0, pu1],
                                 py=[py0, py1])
                        w = {"gWT": gWT, "uWT": uWT, "dWT3": dWT3,
                             "gS": B["gSs"], "uS": B["uSs"], "dS": B["dSs"]}
                        emit_ffn_phase(block, env, h_sc, y_sc, w)
                        emit_barrier(block, env,
                                     ["hld", "gld0", "gld1", "uld0", "uld1",
                                      "dwld0", "dwld1", "yst0", "yst1", "yev",
                                      "zst", "act", "mm"])
                elif sl in ("attn_a", "attn_b"):
                    nkv = NKV_A if sl == "attn_a" else 1
                    with (
                        nc.sbuf_tensor(f"ksb{k}", [128, nkv * T], BF16) as ksb,
                        nc.sbuf_tensor(f"vsb{k}", [128, nkv * T], BF16) as vsb,
                        nc.sbuf_tensor(f"hqA{k}", [128, NCC * QT], BF16) as hqA,
                        nc.sbuf_tensor(f"hqB{k}", [128, NCC * QT], BF16) as hqB,
                        nc.sbuf_tensor(f"tabcA{k}", [128, QT], F32) as tabcA,
                        nc.sbuf_tensor(f"tabcB{k}", [128, QT], F32) as tabcB,
                        nc.sbuf_tensor(f"tabsA{k}", [128, QT], F32) as tabsA,
                        nc.sbuf_tensor(f"tabsB{k}", [128, QT], F32) as tabsB,
                        nc.sbuf_tensor(f"mk2_{k}", [128, 8], F32) as mk2,
                    ):
                        B.update(ksb=ksb, vsb=vsb, hq=[hqA, hqB],
                                 tabc=[tabcA, tabcB], tabs=[tabsA, tabsB], mk2=mk2)
                        if sl == "attn_a":
                            w = {"WqT": aWqT, "WkT": aWkT, "WvT": aWvT,
                                 "WoT": aWoT, "cosA": cosA_d, "sinA": sinA_d,
                                 "cosB": cosB_d, "sinB": sinB_d}
                        else:
                            w = {"WqT": bWqT, "WkT": bWkT, "WvT": bWvT,
                                 "WoT": bWoT, "cosA": cosA_d, "sinA": sinA_d,
                                 "cosB": cosB_d, "sinB": sinB_d}
                        env["sfx"] = f"_{k}"
                        emit_attn_phase(block, env, h_sc, y_sc, w,
                                        "a" if sl == "attn_a" else "b")
                elif sl == "ln":
                    pass
                else:
                    raise NotImplementedError(sl)

                x_cur = x_next
                if sl != "ln":
                    y_prev = y_sc

            emit_ln_phase(block, env, x_cur, out, y_prev, False, None)

            @block.sync
            def _(s):
                s.wait_ge(S.h["xst0"], S.val("xst0"))
                s.wait_ge(S.h["xst1"], S.val("xst1"))

    return nc


def prep_ffn_inputs(gate_w, up_w, down_w, gate_s, up_s, down_s):
    d = {}
    d["gWT"] = np.ascontiguousarray(
        np.asarray(gate_w).reshape(NI, 128, NCC, 128).transpose(0, 3, 2, 1)
        .reshape(NI, 128, NCC * 128))
    d["uWT"] = np.ascontiguousarray(
        np.asarray(up_w).reshape(NI, 128, NCC, 128).transpose(0, 3, 2, 1)
        .reshape(NI, 128, NCC * 128))
    d["dWT3"] = np.ascontiguousarray(
        np.asarray(down_w).reshape(NCC, 128, NI, 128).transpose(0, 3, 2, 1)
        .reshape(NCC, 128, NI * 128))
    d["gS"] = np.ascontiguousarray(np.asarray(gate_s).reshape(NI, 128).T)
    d["uS"] = np.ascontiguousarray(np.asarray(up_s).reshape(NI, 128).T)
    d["dS"] = np.ascontiguousarray(np.asarray(down_s).reshape(NCC, 128).T)
    return d




def prep_attn_inputs(inputs):
    d = {}

    def tile_in(Wname, nheads):
        Wm = np.asarray(inputs[Wname])
        return np.ascontiguousarray(
            Wm.reshape(NCC, 128, nheads, 128).transpose(2, 1, 0, 3)
            .reshape(nheads, 128, NCC * 128))

    def tile_out(Wname):
        Wm = np.asarray(inputs[Wname])
        return np.ascontiguousarray(
            Wm.reshape(NH, 128, NCC, 128).transpose(2, 1, 0, 3)
            .reshape(NCC, 128, NH * 128))

    d["aWqT"] = tile_in("L0_attn_a_Wq", NH)
    d["aWkT"] = tile_in("L0_attn_a_Wk", NKV_A)
    d["aWvT"] = tile_in("L0_attn_a_Wv", NKV_A)
    d["aWoT"] = tile_out("L0_attn_a_Wo")
    d["bWqT"] = tile_in("L0_attn_b_Wq", NH)
    d["bWkT"] = tile_in("L0_attn_b_Wk", 1)
    d["bWvT"] = tile_in("L0_attn_b_Wv", 1)
    d["bWoT"] = tile_out("L0_attn_b_Wo")

    d["bqA"] = np.ascontiguousarray(np.asarray(inputs["L0_attn_a_bq"]).reshape(NH, 128).T)
    d["bkA"] = np.ascontiguousarray(np.asarray(inputs["L0_attn_a_bk"]).reshape(NKV_A, 128).T)
    d["bvA"] = np.ascontiguousarray(np.asarray(inputs["L0_attn_a_bv"]).reshape(NKV_A, 128).T)
    d["boA"] = np.ascontiguousarray(np.asarray(inputs["L0_attn_a_bo"]).reshape(NCC, 128).T)
    d["boB"] = np.ascontiguousarray(np.asarray(inputs["L0_attn_b_bo"]).reshape(NCC, 128).T)

    t = np.arange(T, dtype=np.float64)
    th_a = 10000.0 ** (-np.arange(0, HD, 2, dtype=np.float64) / HD)
    ang_a = np.outer(th_a, t)
    cosA = np.empty((128, T), np.float32)
    sinA = np.empty((128, T), np.float32)
    cosA[0::2] = np.cos(ang_a); cosA[1::2] = np.cos(ang_a)
    sinA[0::2] = -np.sin(ang_a); sinA[1::2] = np.sin(ang_a)
    d["cosA"], d["sinA"] = np.ascontiguousarray(cosA), np.ascontiguousarray(sinA)
    th_b = np.exp(-np.log(10000.0) * (np.arange(0, 64, 2, dtype=np.float64) / 64.0))
    ang_b = np.outer(th_b, t)
    cosB = np.ones((128, T), np.float32)
    sinB = np.zeros((128, T), np.float32)
    cosB[0:32] = np.cos(ang_b); cosB[32:64] = np.cos(ang_b)
    sinB[0:32] = -np.sin(ang_b); sinB[32:64] = np.sin(ang_b)
    d["cosB"], d["sinB"] = np.ascontiguousarray(cosB), np.ascontiguousarray(sinB)

    Ra = np.zeros((128, 128), np.float32)
    for i in range(64):
        Ra[2 * i, 2 * i + 1] = 1.0
        Ra[2 * i + 1, 2 * i] = 1.0
    Rb = np.zeros((128, 128), np.float32)
    for dd in range(32):
        Rb[dd, dd + 32] = 1.0
        Rb[dd + 32, dd] = 1.0
    d["RaI"], d["RbI"] = Ra, Rb
    d["identI"] = np.eye(128, dtype=np.float32)
    d["onesI"] = np.ones((128, 128), np.float32)

    kk = np.arange(128)[:, None]
    tt = np.arange(QT)[None, :]
    masks = np.zeros((128, 8 * QT), np.float32)
    for m in range(4):
        dd = m * 128
        masks[:, m * QT:(m + 1) * QT] = (kk + dd <= tt).astype(np.float32)
    for m in range(4):
        dd = -WINDOW + m * 128
        masks[:, (4 + m) * QT:(5 + m) * QT] = ((tt - kk) <= dd + WINDOW - 1).astype(np.float32)
    d["masksI"] = masks
    return d


SUBLAYERS = ("attn_a", "ffn", "attn_b", "ffn")


def make_in_maps(inputs):
    x = np.asarray(inputs["L0_attn_a_x"])
    wd = prep_ffn_inputs(inputs["L0_ffn_gate_w"], inputs["L0_ffn_up_w"],
                         inputs["L0_ffn_down_w"], inputs["L0_ffn_gate_s"],
                         inputs["L0_ffn_up_s"], inputs["L0_ffn_down_s"])
    wd.update(prep_attn_inputs(inputs))
    return [dict(wd, x=np.ascontiguousarray(x[b])) for b in range(8)]


def kernel(**inputs):
    nc = build_program(SUBLAYERS)
    in_maps = make_in_maps(inputs)
    res = run_bass_kernel_spmd(nc, in_maps, core_ids=list(range(8)))
    return np.stack([r["out"] for r in res.results], axis=0)


# version 03
# speedup vs baseline: 1.1922x; 1.0025x over previous
import numpy as np
import ml_dtypes
import concourse.bass as bass
import concourse.mybir as mybir
from concourse.bass_utils import run_bass_kernel_spmd

F32 = mybir.dt.float32
BF16 = mybir.dt.bfloat16
AF = mybir.ActivationFunctionType
ALU = mybir.AluOpType
AX = mybir.AxisListType
bfdt = ml_dtypes.bfloat16

T = 4096
C = 2048
HD = 128
NH = 16
NKV_A = 4
INTER = 5632
NI = INTER // 128
NCC = C // 128
QT = 512
NQT = T // QT
GROUP = 1024
NG = T // GROUP
TPG = GROUP // QT
NTT = 32
EPS = 1e-5
WINDOW = 512


class Sems:
    def __init__(self, nc, stack, names):
        self.h = {}
        self.n = {}
        for nm in names:
            self.h[nm] = stack.enter_context(nc.semaphore(nm))
            self.n[nm] = 0

    def inc(self, nm, k):
        self.n[nm] += k
        return self.h[nm], k

    def val(self, nm):
        return self.n[nm]


SEM_NAMES = [
    "xld0", "xld1", "yld0", "yld1", "hld", "hst0", "hst1", "xst",
    "yst0", "yst1", "xst0", "xst1",
    "vln", "sln", "happ",
    "gld0", "gld1", "uld0", "uld1", "dwld0", "dwld1",
    "mm", "act", "zst", "ymm", "yev",
    "bar", "aux", "aux2",
    "kw0", "kw1", "vw0", "vw1", "tabc0", "tabc1", "tabs0", "tabs1",
    "hq0", "hq1", "qw0", "qw1", "ow0", "ow1",
    "kev1", "kev2", "kev3", "kmax", "vev1", "vev2",
    "qev1", "qev2", "qev3", "cr", "cev", "varg", "pexp", "pmask", "lr", "oev",
]


def emit_ln_phase(block, env, x_src, x_dst, y_src, do_ln, h_dst):
    S, B = env["sems"], env["bufs"]
    xt, yt, ht = B["xt"], B["yt"], B["ht"]
    st8, bn6, sqs = B["st8"], B["bn6"], B["sqs"]
    have_y = y_src is not None
    b = {nm: S.val(nm) for nm in ("xld0", "xld1", "yld0", "yld1", "vln", "sln",
                                  "happ", "xst0", "xst1", "hst0", "hst1")}

    @block.sync
    def _(s):
        for i in range(NTT + 2):
            if 2 <= i:
                j = i - 2
                s.wait_ge(S.h["vln"], b["vln"] + j + 1)
                s.dma_start(out=x_dst[j * 128:(j + 1) * 128, :], in_=xt[j % 2][:, :])\
                    .then_inc(*S.inc(f"xst{j % 2}", 16))
                if do_ln:
                    s.dma_start(out=h_dst[j * 128:(j + 1) * 128, :], in_=ht[j % 2][:, :])\
                        .then_inc(*S.inc(f"hst{j % 2}", 16))
            if i < NTT:
                if i >= 2:
                    s.wait_ge(S.h[f"xst{i % 2}"], b[f"xst{i % 2}"] + 16 * (i // 2))
                s.dma_start(out=xt[i % 2][:, :], in_=x_src[i * 128:(i + 1) * 128, :])\
                    .then_inc(*S.inc(f"xld{i % 2}", 16))
                if have_y:
                    s.dma_start_transpose(yt[i % 2][:, :], y_src[:, i * 128:(i + 1) * 128])\
                        .then_inc(*S.inc(f"yld{i % 2}", 16))

    @block.vector
    def _(v):
        for i in range(NTT):
            p = i % 2
            v.wait_ge(S.h[f"xld{p}"], b[f"xld{p}"] + 16 * (i // 2 + 1))
            if have_y:
                v.wait_ge(S.h[f"yld{p}"], b[f"yld{p}"] + 16 * (i // 2 + 1))
            if do_ln:
                if have_y:
                    v.tensor_tensor(xt[p][:, :], xt[p][:, :], yt[p][:, :], ALU.add)
                    v.drain()
                if i >= 2:
                    v.wait_ge(S.h["happ"], b["happ"] + i - 1)
                for g in range(4):
                    v.bn_stats(bn6[:, p * 24 + 6 * g: p * 24 + 6 * g + 6],
                               xt[p][:, 512 * g:512 * (g + 1)])
                v.drain()
                b6 = bn6[:, p * 24: p * 24 + 24].rearrange("a (g s) -> a g s", s=3)
                col = st8[:, p * 8: p * 8 + 8]
                v.tensor_reduce(col[:, 0:1], b6[:, :, 1], AX.X, ALU.add)
                v.tensor_reduce(col[:, 1:2], b6[:, :, 2], AX.X, ALU.add)
                v.scalar_tensor_tensor(sqs[:, p * 8: p * 8 + 8],
                                       b6[:, :, 1], 1.0, b6[:, :, 1],
                                       ALU.bypass, ALU.mult,
                                       accum_out=col[:, 2:3])
                v.drain()
                v.tensor_scalar(col[:, 4:5], col[:, 0:1], 1.0 / 8, None, ALU.mult)
                v.tensor_scalar(col[:, 6:7], col[:, 2:3], 1.0 / 8, None, ALU.mult)
                v.drain()
                v.tensor_scalar(col[:, 5:6], col[:, 4:5], -1.0, None, ALU.mult)
                v.scalar_tensor_tensor(col[:, 6:7], col[:, 1:2], 1.0 / 2048,
                                       col[:, 6:7], ALU.mult, ALU.add)
                v.drain()
                v.scalar_tensor_tensor(col[:, 6:7], col[:, 4:5], col[:, 5:6],
                                       col[:, 6:7], ALU.mult, ALU.add)
                v.drain()
                v.tensor_single_scalar(col[:, 6:7], col[:, 6:7], EPS, ALU.add)\
                    .then_inc(*S.inc("sln", 1))
                v.wait_ge(S.h["happ"], b["happ"] + i + 1)
                if i >= 2:
                    v.wait_ge(S.h[f"hst{p}"], b[f"hst{p}"] + 16 * (i // 2))
                v.tensor_scalar(ht[p][:, :], xt[p][:, :], col[:, 4:5], col[:, 7:8],
                                ALU.subtract, ALU.mult).then_inc(*S.inc("vln", 1))
            elif have_y:
                v.tensor_tensor(xt[p][:, :], xt[p][:, :], yt[p][:, :], ALU.add)\
                    .then_inc(*S.inc("vln", 1))
            else:
                v.tensor_single_scalar(xt[p][:, :], xt[p][:, :], 0.0, ALU.add)\
                    .then_inc(*S.inc("vln", 1))

    if do_ln:
        @block.scalar
        def _(sc):
            for i in range(NTT):
                p = i % 2
                col = st8[:, p * 8: p * 8 + 8]
                sc.wait_ge(S.h["sln"], b["sln"] + i + 1)
                sc.activation(col[:, 7:8], col[:, 6:7], AF.Ln)
                sc.drain()
                sc.activation(col[:, 7:8], col[:, 7:8], AF.Exp, scale=-0.5)\
                    .then_inc(*S.inc("happ", 1))


def emit_ffn_phase(block, env, h_src, y_dst, w):
    S, B = env["sems"], env["bufs"]
    hsb, zsb = B["hsb"], B["zsb"]
    gw, uw, dw = B["gw"], B["uw"], B["dw"]
    silu, ysb = B["silu"], B["ysb"]
    pg, pu, py = B["pg"], B["pu"], B["py"]
    gS, uS, dS = w["gS"], w["uS"], w["dS"]

    b = {nm: S.val(nm) for nm in ("hld", "gld0", "gld1", "uld0", "uld1",
                                  "dwld0", "dwld1", "mm", "act", "zst",
                                  "ymm", "yev", "yst0", "yst1")}

    gWTr = w["gWT"]
    uWTr = w["uWT"]
    dWT3r = w["dWT3"]

    mm_g, mm_u, mm_y = {}, {}, {}
    mmc = 0
    for grp in range(NG):
        for i in range(NI):
            for tt in range(TPG):
                mmc += 1
                mm_g[(grp, i, tt)] = mmc
            for tt in range(TPG):
                mmc += 1
                mm_u[(grp, i, tt)] = mmc
        for co in range(NCC):
            for tt in range(TPG):
                mmc += 1
                mm_y[(grp, co, tt)] = mmc

    def chunk_gi(n):
        return divmod(n, NI)

    @block.gpsimd
    def _(g):
        for grp in range(NG):
            for i in range(NI):
                n = grp * NI + i
                par = n % 2
                if n >= 2:
                    pg_, pi_ = chunk_gi(n - 2)
                    g.wait_ge(S.h["mm"], b["mm"] + mm_g[(pg_, pi_, TPG - 1)])
                g.dma_start(out=gw[par][:, :], in_=gWTr[i, :, :])\
                    .then_inc(*S.inc(f"gld{par}", 16))
                if n >= 2:
                    pg_, pi_ = chunk_gi(n - 2)
                    g.wait_ge(S.h["mm"], b["mm"] + mm_u[(pg_, pi_, TPG - 1)])
                g.dma_start(out=uw[par][:, :], in_=uWTr[i, :, :])\
                    .then_inc(*S.inc(f"uld{par}", 16))
            for co in range(NCC):
                n = grp * NCC + co
                if n >= 2:
                    pg_, pco_ = divmod(n - 2, NCC)
                    g.wait_ge(S.h["mm"], b["mm"] + mm_y[(pg_, pco_, TPG - 1)])
                g.dma_start(out=dw[n % 2][:, :], in_=dWT3r[co, :, :])\
                    .then_inc(*S.inc(f"dwld{n % 2}", 16))

    @block.sync
    def _(s):
        for grp in range(NG):
            if grp >= 1:
                s.wait_ge(S.h["mm"], b["mm"] + mm_u[(grp - 1, NI - 1, TPG - 1)])
            for cc in range(NCC):
                for tt in range(TPG):
                    t0 = grp * GROUP + tt * QT
                    s.dma_start_transpose(
                        hsb[:, cc * GROUP + tt * QT: cc * GROUP + (tt + 1) * QT],
                        h_src[t0:t0 + QT, cc * 128:(cc + 1) * 128],
                    ).then_inc(*S.inc("hld", 16))

    @block.tensor
    def _(t):
        for grp in range(NG):
            t.wait_ge(S.h["hld"], b["hld"] + 16 * NCC * TPG * (grp + 1))
            for i in range(NI):
                n = grp * NI + i
                par = n % 2
                t.wait_ge(S.h[f"gld{par}"], b[f"gld{par}"] + 16 * (n // 2 + 1))
                for tt in range(TPG):
                    gi = n * TPG + tt
                    if n >= 1:
                        t.wait_ge(S.h["act"], b["act"] + gi - TPG + 1)
                    for cc in range(NCC):
                        ins = t.matmul(pg[tt][:, :], gw[par][:, cc * 128:(cc + 1) * 128],
                                       hsb[:, cc * GROUP + tt * QT: cc * GROUP + (tt + 1) * QT],
                                       start=(cc == 0), stop=(cc == NCC - 1))
                    ins.then_inc(*S.inc("mm", 1))
                t.wait_ge(S.h[f"uld{par}"], b[f"uld{par}"] + 16 * (n // 2 + 1))
                for tt in range(TPG):
                    gi = n * TPG + tt
                    if n >= 1:
                        t.wait_ge(S.h["zst"], b["zst"] + gi - TPG + 1)
                    for cc in range(NCC):
                        ins = t.matmul(pu[tt][:, :], uw[par][:, cc * 128:(cc + 1) * 128],
                                       hsb[:, cc * GROUP + tt * QT: cc * GROUP + (tt + 1) * QT],
                                       start=(cc == 0), stop=(cc == NCC - 1))
                    ins.then_inc(*S.inc("mm", 1))
            t.wait_ge(S.h["zst"], b["zst"] + (grp + 1) * NI * TPG)
            for co in range(NCC):
                n = grp * NCC + co
                t.wait_ge(S.h[f"dwld{n % 2}"], b[f"dwld{n % 2}"] + 16 * (n // 2 + 1))
                for tt in range(TPG):
                    yi = n * TPG + tt
                    if yi >= 2:
                        t.wait_ge(S.h["yev"], b["yev"] + yi - 1)
                    for ich in range(NI):
                        ins = t.matmul(py[yi % 2][:, :], dw[n % 2][:, ich * 128:(ich + 1) * 128],
                                       zsb[:, ich * GROUP + tt * QT: ich * GROUP + (tt + 1) * QT],
                                       start=(ich == 0), stop=(ich == NI - 1))
                    ins.then_inc(*S.inc("mm", 1))
                    t.sem_inc(*S.inc("ymm", 1))

    @block.scalar
    def _(sc):
        for grp in range(NG):
            for i in range(NI):
                for tt in range(TPG):
                    gi = (grp * NI + i) * TPG + tt
                    sc.wait_ge(S.h["mm"], b["mm"] + mm_g[(grp, i, tt)])
                    if gi >= 2:
                        sc.wait_ge(S.h["zst"], b["zst"] + gi - 1)
                    sc.activation(silu[gi % 2][:, :], pg[tt][:, :], AF.Silu,
                                  scale=gS[:, i:i + 1]).then_inc(*S.inc("act", 1))
            for co in range(NCC):
                for tt in range(TPG):
                    yi = (grp * NCC + co) * TPG + tt
                    t0 = grp * GROUP + tt * QT
                    sc.wait_ge(S.h["yev"], b["yev"] + yi + 1)
                    sc.dma_start(out=y_dst[co * 128:(co + 1) * 128, t0:t0 + QT],
                                 in_=ysb[yi % 2][:, :]).then_inc(*S.inc(f"yst{yi % 2}", 16))

    @block.vector
    def _(v):
        for grp in range(NG):
            for i in range(NI):
                for tt in range(TPG):
                    gi = (grp * NI + i) * TPG + tt
                    v.wait_ge(S.h["mm"], b["mm"] + mm_u[(grp, i, tt)])
                    v.wait_ge(S.h["act"], b["act"] + gi + 1)
                    if grp >= 1 and i == 0 and tt == 0:
                        v.wait_ge(S.h["mm"], b["mm"] + mm_y[(grp - 1, NCC - 1, TPG - 1)])
                    v.scalar_tensor_tensor(
                        zsb[:, i * GROUP + tt * QT: i * GROUP + (tt + 1) * QT],
                        pu[tt][:, :], uS[:, i:i + 1], silu[gi % 2][:, :],
                        ALU.mult, ALU.mult).then_inc(*S.inc("zst", 1))
            for co in range(NCC):
                for tt in range(TPG):
                    yi = (grp * NCC + co) * TPG + tt
                    v.wait_ge(S.h["mm"], b["mm"] + mm_y[(grp, co, tt)])
                    if yi >= 2:
                        v.wait_ge(S.h[f"yst{yi % 2}"], b[f"yst{yi % 2}"] + 16 * (yi // 2))
                    v.tensor_scalar(ysb[yi % 2][:, :], py[yi % 2][:, :],
                                    dS[:, co:co + 1], None, ALU.mult)\
                        .then_inc(*S.inc("yev", 1))




def attn_chunks(kind, qi):
    Q0 = qi * QT
    lo = 0 if kind == "a" else max(0, Q0 - WINDOW)
    out = []
    for K0 in range(lo, Q0 + QT - 128 + 1, 128):
        d = K0 - Q0
        if d >= 0:
            slot = d // 128
        elif kind == "b":
            slot = 4 + (d + WINDOW) // 128
        else:
            slot = None
        out.append((K0, slot))
    return out


def emit_attn_phase(block, env, h_src, y_dst, w, kind):
    sfx = env.get("sfx", "")
    S, B = env["sems"], env["bufs"]
    nkv = NKV_A if kind == "a" else 1
    r_imm = 1.0 if kind == "a" else float(np.sqrt(2.0))
    qk_scale = 0.5 if kind == "a" else 0.25

    masks = B["masks"]
    ident, ones = B["ident"], B["ones"]
    shiftc = B["shiftc"]
    R_sb = B["Ra"] if kind == "a" else B["Rb"]
    bq = B.get("bqA") if kind == "a" else None
    bk = B.get("bkA") if kind == "a" else None
    bv = B.get("bvA") if kind == "a" else None
    bo = B["boA"] if kind == "a" else B["boB"]
    cosT, sinT = (w["cosA"], w["sinA"]) if kind == "a" else (w["cosB"], w["sinB"])
    WqT, WkT, WvT, WoT = w["WqT"], w["WkT"], w["WvT"], w["WoT"]

    hq = B["hq"]
    ksb, vsb = B["ksb"], B["vsb"]
    tabc, tabs = B["tabc"], B["tabs"]

    base = {nm: S.val(nm) for nm in (
        "mm", "kw0", "vw0", "tabc0", "tabc1", "tabs0", "tabs1",
        "hq0", "hq1", "qw0", "qw1", "ow0", "ow1",
        "kev1", "kev2", "vev1", "vev2",
        "qev1", "qev2", "pexp", "pmask", "lr",
        "oev", "yev", "yst0", "yst1")}

    chunks = {qi: attn_chunks(kind, qi) for qi in range(NQT)}
    pair_of = {}
    pairs_of_g = {}
    pmask_idx = {}
    pgc = 0
    pmc = 0
    for qi in range(NQT):
        for h in range(NH):
            g = qi * NH + h
            lst = []
            for n, (K0, slot) in enumerate(chunks[qi]):
                pgc += 1
                pair_of[(g, n)] = pgc
                lst.append(pgc)
                if slot is not None:
                    pmc += 1
                    pmask_idx[pgc] = pmc
            pairs_of_g[g] = lst

    mmpos = {}

    from contextlib import ExitStack

    with ExitStack() as _st:
        _sb = lambda n, sh, dt: _st.enter_context(env["nc"].sbuf_tensor(n + sfx, sh, dt))
        _ps = lambda n, sh, dt: _st.enter_context(env["nc"].psum_tensor(n + sfx, sh, dt))
        kwb = [_sb(f"kwb{j}", [128, NCC * 128], BF16) for j in range(nkv)]
        vwb = [_sb(f"vwb{j}", [128, NCC * 128], BF16) for j in range(nkv)]
        kscb = _sb("kscb", [128, QT], BF16)
        vscb = _sb("vscb", [128, QT], BF16)
        scr1 = _sb("scr1", [128, QT], BF16)
        scr2 = _sb("scr2", [128, QT], BF16)
        kraw = _ps("kraw", [128, QT], F32)
        rotb = _ps("rotb", [128, QT], F32)
        vraw = _ps("vraw", [128, QT], F32)
        vtb = _ps("vtb", [128, QT], BF16)

        def kv_idx(tt8, j):
            return tt8 * nkv + j + 1

        @block.tensor
        def _(t):
            for tt8 in range(8):
                for j in range(nkv):
                    idx = kv_idx(tt8, j)
                    t.wait_ge(S.h["kw0"], base["kw0"] + 16 * nkv)
                    t.wait_ge(S.h[f"hq{tt8 % 2}"],
                              base[f"hq{tt8 % 2}"] + 16 * NCC * (tt8 // 2 + 1))
                    if idx >= 2:
                        t.wait_ge(S.h["kev1"], base["kev1"] + idx - 1)
                    for cc in range(NCC):
                        ins = t.matmul(kraw[:, :], kwb[j][:, cc * 128:(cc + 1) * 128],
                                       hq[tt8 % 2][:, cc * QT:(cc + 1) * QT],
                                       start=(cc == 0), stop=(cc == NCC - 1))
                    ins.then_inc(*S.inc("mm", 1))
                    mmpos[("kraw", idx)] = S.val("mm")
                    t.wait_ge(S.h["kev1"], base["kev1"] + idx)
                    if idx >= 2:
                        t.wait_ge(S.h["kev2"], base["kev2"] + idx - 1)
                    ins = t.matmul(rotb[:, :], R_sb[:, :], kscb[:, :],
                                   start=True, stop=True)
                    ins.then_inc(*S.inc("mm", 1))
                    mmpos[("rot", idx)] = S.val("mm")
                    t.wait_ge(S.h["vw0"], base["vw0"] + 16 * nkv)
                    if idx >= 2:
                        t.wait_ge(S.h["vev1"], base["vev1"] + idx - 1)
                    for cc in range(NCC):
                        ins = t.matmul(vraw[:, :], vwb[j][:, cc * 128:(cc + 1) * 128],
                                       hq[tt8 % 2][:, cc * QT:(cc + 1) * QT],
                                       start=(cc == 0), stop=(cc == NCC - 1))
                    ins.then_inc(*S.inc("mm", 1))
                    mmpos[("vraw", idx)] = S.val("mm")
                    t.wait_ge(S.h["vev1"], base["vev1"] + idx)
                    if idx >= 2:
                        t.wait_ge(S.h["vev2"], base["vev2"] + idx - 1)
                    for s_ in range(4):
                        ins = t.transpose(vtb[:, s_ * 128:(s_ + 1) * 128],
                                          vscb[:, s_ * 128:(s_ + 1) * 128],
                                          ident[:, :])
                    ins.then_inc(*S.inc("mm", 1))
                    mmpos[("vt", idx)] = S.val("mm")

        @block.gpsimd
        def _(g):
            for j in range(nkv):
                g.dma_start(out=kwb[j][:, :], in_=WkT[j, :, :])\
                    .then_inc(*S.inc("kw0", 16))
                g.dma_start(out=vwb[j][:, :], in_=WvT[j, :, :])\
                    .then_inc(*S.inc("vw0", 16))
            for tt8 in range(8):
                par = tt8 % 2
                if tt8 >= 2:
                    g.wait_ge(S.h["kev2"], base["kev2"] + kv_idx(tt8 - 2, nkv - 1))
                g.dma_start(out=tabc[par][:, :], in_=cosT[:, tt8 * QT:(tt8 + 1) * QT])\
                    .then_inc(*S.inc(f"tabc{par}", 16))
                g.dma_start(out=tabs[par][:, :], in_=sinT[:, tt8 * QT:(tt8 + 1) * QT])\
                    .then_inc(*S.inc(f"tabs{par}", 16))

        @block.sync
        def _(s):
            for tt8 in range(8):
                par = tt8 % 2
                if tt8 >= 2:
                    s.wait_ge(S.h["mm"], mmpos[("vraw", kv_idx(tt8 - 2, nkv - 1))])
                for cc in range(NCC):
                    s.dma_start_transpose(
                        hq[par][:, cc * QT:(cc + 1) * QT],
                        h_src[tt8 * QT:(tt8 + 1) * QT, cc * 128:(cc + 1) * 128],
                    ).then_inc(*S.inc(f"hq{par}", 16))


        @block.vector
        def _(v):
            for tt8 in range(8):
                par = tt8 % 2
                for j in range(nkv):
                    idx = kv_idx(tt8, j)
                    v.wait_ge(S.h["mm"], mmpos[("kraw", idx)])
                    if idx >= 2:
                        v.wait_ge(S.h["mm"], mmpos[("rot", idx - 1)])
                    if kind == "a":
                        v.tensor_scalar(kscb[:, :], kraw[:, :], bk[:, j:j + 1],
                                        None, ALU.add)\
                            .then_inc(*S.inc("kev1", 1))
                    else:
                        v.tensor_copy(kscb[:, :], kraw[:, :])\
                            .then_inc(*S.inc("kev1", 1))
                    v.drain()
                    v.wait_ge(S.h[f"tabc{par}"], base[f"tabc{par}"] + 16 * (tt8 // 2 + 1))
                    v.wait_ge(S.h[f"tabs{par}"], base[f"tabs{par}"] + 16 * (tt8 // 2 + 1))
                    v.tensor_tensor(scr1[:, :], kscb[:, :], tabc[par][:, :], ALU.mult)
                    v.wait_ge(S.h["mm"], mmpos[("rot", idx)])
                    v.tensor_tensor(scr2[:, :], rotb[:, :], tabs[par][:, :], ALU.mult)
                    v.drain()
                    v.tensor_tensor(ksb[:, j * T + tt8 * QT: j * T + (tt8 + 1) * QT],
                                    scr1[:, :], scr2[:, :], ALU.add)\
                        .then_inc(*S.inc("kev2", 1))
                    v.drain()
                    v.wait_ge(S.h["mm"], mmpos[("vraw", idx)])
                    if idx >= 2:
                        v.wait_ge(S.h["mm"], mmpos[("vt", idx - 1)])
                    if kind == "a":
                        v.tensor_scalar(vscb[:, :], vraw[:, :], bv[:, j:j + 1],
                                        None, ALU.add).then_inc(*S.inc("vev1", 1))
                    else:
                        v.tensor_copy(vscb[:, :], vraw[:, :])\
                            .then_inc(*S.inc("vev1", 1))
                    v.wait_ge(S.h["mm"], mmpos[("vt", idx)])
                    v.tensor_copy(
                        vsb[:, (j * 32 + tt8 * 4) * 128:(j * 32 + tt8 * 4 + 4) * 128],
                        vtb[:, :]).then_inc(*S.inc("vev2", 1))

        emit_barrier(block, env, ["kev1", "kev2", "vev1", "vev2",
                                  "kw0", "vw0", "tabc0", "tabc1",
                                  "tabs0", "tabs1", "hq0", "hq1", "mm"])

    NPB = 4
    NG_ALL = NQT * NH
    with ExitStack() as _st:
        _sb = lambda n, sh, dt: _st.enter_context(env["nc"].sbuf_tensor(n + sfx, sh, dt))
        _ps = lambda n, sh, dt: _st.enter_context(env["nc"].psum_tensor(n + sfx, sh, dt))
        qwb = [_sb(f"qwb{i}", [128, NCC * 128], BF16) for i in range(2)]
        owb = [_sb(f"owb{i}", [128, NH * 128], BF16) for i in range(2)]
        qscb = _sb("qscb", [128, QT], BF16)
        qsb = [_sb(f"qsb{i}", [128, QT], BF16) for i in range(2)]
        scr1 = _sb("scr1q", [128, QT], BF16)
        scr2 = _sb("scr2q", [128, QT], BF16)
        psb = [_sb(f"psb{i}", [128, QT], BF16) for i in range(NPB)]
        Rsb2 = [_sb(f"Rsb{i}", [128, QT], F32) for i in range(2)]
        osb = _sb("osb", [128, NH * QT], BF16)
        ysb2 = [_sb(f"ysb2{i}", [128, QT], BF16) for i in range(2)]
        qrot = [_ps(f"qrot{i}", [128, QT], F32) for i in range(2)]
        sbk = [_ps(f"sb{i}", [128, QT], F32) for i in range(NPB)]
        obk = _ps("ob", [128, QT], F32)
        lbk = _ps("lb", [128, QT], F32)

        def emit_prep(t, g):
            qi, h = divmod(g, NH)
            par = g % 2
            t.wait_ge(S.h[f"qw{par}"], base[f"qw{par}"] + 16 * (g // 2 + 1))
            t.wait_ge(S.h[f"hq{qi % 2}"],
                      base[f"hq{qi % 2}"] + 16 * NCC * (4 + qi // 2 + 1))
            if g >= 2:
                t.wait_ge(S.h["qev2"], base["qev2"] + g - 1)
            for cc in range(NCC):
                ins = t.matmul(qrot[par][:, :], qwb[par][:, cc * 128:(cc + 1) * 128],
                               hq[qi % 2][:, cc * QT:(cc + 1) * QT],
                               start=(cc == 0), stop=(cc == NCC - 1))
            ins.then_inc(*S.inc("mm", 1))
            mmpos[("qraw", g)] = S.val("mm")
            t.wait_ge(S.h["qev1"], base["qev1"] + g + 1)
            ins = t.matmul(qrot[par][:, :], R_sb[:, :], qscb[:, :],
                           start=True, stop=True)
            ins.then_inc(*S.inc("mm", 1))
            mmpos[("rotq", g)] = S.val("mm")

        @block.tensor
        def _(t):
            emit_prep(t, 0)
            emit_prep(t, 1)
            for qi in range(NQT):
                lst = chunks[qi]
                for h in range(NH):
                    g = qi * NH + h
                    j = h % nkv
                    par = g % 2
                    for n, (K0, slot) in enumerate(lst):
                        pg = pair_of[(g, n)]
                        if n == 0:
                            t.wait_ge(S.h["qev2"], base["qev2"] + g + 1)
                        if pg > NPB:
                            t.wait_ge(S.h["pexp"], base["pexp"] + pg - NPB)
                        if qi >= 1 and h == 0 and n < NPB:
                            t.wait_ge(S.h["yev"], base["yev"] + qi * NCC)
                        ins = t.matmul(sbk[pg % NPB][:, :],
                                       ksb[:, j * T + K0: j * T + K0 + 128],
                                       qsb[par][:, :], start=True, stop=True)
                        ins.then_inc(*S.inc("mm", 1))
                        mmpos[("s", pg)] = S.val("mm")
                        t.wait_ge(S.h["pexp"], base["pexp"] + pg)
                        if slot is not None:
                            t.wait_ge(S.h["pmask"], base["pmask"] + pmask_idx[pg])
                        if n == 0 and g >= 1:
                            t.wait_ge(S.h["lr"], base["lr"] + g)
                            t.wait_ge(S.h["oev"], base["oev"] + g)
                        t.matmul(lbk[:, :], ones[:, :], psb[pg % NPB][:, :],
                                 start=(n == 0), stop=(n == len(lst) - 1))
                        ins = t.matmul(obk[:, :],
                                       vsb[:, (j * 32 + (K0 // 128)) * 128:
                                           (j * 32 + (K0 // 128) + 1) * 128],
                                       psb[pg % NPB][:, :],
                                       start=(n == 0), stop=(n == len(lst) - 1))
                        ins.then_inc(*S.inc("mm", 1))
                        mmpos[("o", pg)] = S.val("mm")
                    if g + 2 < NG_ALL:
                        emit_prep(t, g + 2)
                for co in range(NCC):
                    oc = qi * NCC + co
                    t.wait_ge(S.h[f"ow{oc % 2}"], base[f"ow{oc % 2}"] + 16 * (oc // 2 + 1))
                    t.wait_ge(S.h["oev"], base["oev"] + (qi + 1) * NH)
                    t.wait_ge(S.h["pexp"],
                              base["pexp"] + pair_of[(qi * NH + NH - 1, len(lst) - 1)])
                    if oc >= NPB:
                        t.wait_ge(S.h["yev"], base["yev"] + oc - NPB + 1)
                    for hh in range(NH):
                        ins = t.matmul(sbk[co % NPB][:, :],
                                       owb[oc % 2][:, hh * 128:(hh + 1) * 128],
                                       osb[:, hh * QT:(hh + 1) * QT],
                                       start=(hh == 0), stop=(hh == NH - 1))
                    ins.then_inc(*S.inc("mm", 1))
                    mmpos[("yq", qi, co)] = S.val("mm")

        @block.gpsimd
        def _(g_):
            def qwload(g):
                if g >= 2:
                    g_.wait_ge(S.h["mm"], mmpos[("qraw", g - 2)])
                g_.dma_start(out=qwb[g % 2][:, :], in_=WqT[g % NH, :, :])\
                    .then_inc(*S.inc(f"qw{g % 2}", 16))

            def tabload(qi2):
                par8 = qi2 % 2
                if qi2 >= 2:
                    g_.wait_ge(S.h["qev2"], base["qev2"] + (qi2 - 1) * NH)
                g_.dma_start(out=tabc[par8][:, :], in_=cosT[:, qi2 * QT:(qi2 + 1) * QT])\
                    .then_inc(*S.inc(f"tabc{par8}", 16))
                g_.dma_start(out=tabs[par8][:, :], in_=sinT[:, qi2 * QT:(qi2 + 1) * QT])\
                    .then_inc(*S.inc(f"tabs{par8}", 16))

            qwload(0)
            qwload(1)
            tabload(0)
            for qi in range(NQT):
                for h in range(NH):
                    g = qi * NH + h
                    if h == NH - 2 and qi + 1 < NQT:
                        tabload(qi + 1)
                    if g + 2 < NG_ALL:
                        qwload(g + 2)
                for co in range(NCC):
                    oc = qi * NCC + co
                    if oc >= 2:
                        g_.wait_ge(S.h["mm"], mmpos[("yq", (oc - 2) // NCC,
                                                     (oc - 2) % NCC)])
                    g_.dma_start(out=owb[oc % 2][:, :], in_=WoT[co, :, :])\
                        .then_inc(*S.inc(f"ow{oc % 2}", 16))

        @block.sync
        def _(s):
            for qi in range(NQT):
                par = qi % 2
                if qi >= 2:
                    s.wait_ge(S.h["mm"], mmpos[("qraw", (qi - 2) * NH + NH - 1)])
                for cc in range(NCC):
                    s.dma_start_transpose(
                        hq[par][:, cc * QT:(cc + 1) * QT],
                        h_src[qi * QT:(qi + 1) * QT, cc * 128:(cc + 1) * 128],
                    ).then_inc(*S.inc(f"hq{par}", 16))

        @block.scalar
        def _(sc):
            for qi in range(NQT):
                for h in range(NH):
                    g = qi * NH + h
                    par = g % 2
                    for n, (K0, slot) in enumerate(chunks[qi]):
                        pg = pair_of[(g, n)]
                        sc.wait_ge(S.h["mm"], mmpos[("s", pg)])
                        if pg > NPB:
                            sc.wait_ge(S.h["mm"], mmpos[("o", pg - NPB)])
                        sc.activation(psb[pg % NPB][:, :], sbk[pg % NPB][:, :],
                                      AF.Exp, bias=shiftc[:, 0:1], scale=r_imm)\
                            .then_inc(*S.inc("pexp", 1))
                    sc.wait_ge(S.h["mm"], mmpos[("o", pairs_of_g[g][-1])])
                    if g >= 2:
                        sc.wait_ge(S.h["oev"], base["oev"] + g - 1)
                    sc.activation(Rsb2[par][:, :], lbk[:, :], AF.Ln)
                    sc.drain()
                    sc.activation(Rsb2[par][:, :], Rsb2[par][:, :], AF.Exp, scale=-1.0)\
                        .then_inc(*S.inc("lr", 1))
                for co in range(NCC):
                    oc = qi * NCC + co
                    sc.wait_ge(S.h["yev"], base["yev"] + oc + 1)
                    sc.dma_start(out=y_dst[co * 128:(co + 1) * 128,
                                           qi * QT:(qi + 1) * QT],
                                 in_=ysb2[oc % 2][:, :])\
                        .then_inc(*S.inc(f"yst{oc % 2}", 16))

        @block.vector
        def _(v):
            def rope(g):
                qi2, h2 = divmod(g, NH)
                par2 = g % 2
                v.wait_ge(S.h["mm"], mmpos[("qraw", g)])
                if g >= 1:
                    v.wait_ge(S.h["mm"], mmpos[("rotq", g - 1)])
                if kind == "a":
                    v.tensor_scalar(qscb[:, :], qrot[par2][:, :], bq[:, h2:h2 + 1],
                                    None, ALU.add).then_inc(*S.inc("qev1", 1))
                else:
                    v.tensor_copy(qscb[:, :], qrot[par2][:, :])\
                        .then_inc(*S.inc("qev1", 1))
                v.drain()
                v.wait_ge(S.h[f"tabc{qi2 % 2}"],
                          base[f"tabc{qi2 % 2}"] + 16 * (4 + qi2 // 2 + 1))
                v.wait_ge(S.h[f"tabs{qi2 % 2}"],
                          base[f"tabs{qi2 % 2}"] + 16 * (4 + qi2 // 2 + 1))
                if g >= 2:
                    v.wait_ge(S.h["mm"], mmpos[("s", pairs_of_g[g - 2][-1])])
                v.tensor_tensor(scr1[:, :], qscb[:, :], tabc[qi2 % 2][:, :], ALU.mult)
                v.wait_ge(S.h["mm"], mmpos[("rotq", g)])
                v.tensor_tensor(scr2[:, :], qrot[par2][:, :], tabs[qi2 % 2][:, :],
                                ALU.mult)
                v.drain()
                v.tensor_tensor(qsb[par2][:, :], scr1[:, :], scr2[:, :], ALU.add)\
                    .then_inc(*S.inc("qev2", 1))
                v.drain()

            rope(0)
            if NG_ALL > 1:
                rope(1)
            for qi in range(NQT):
                for h in range(NH):
                    g = qi * NH + h
                    for n, (K0, slot) in enumerate(chunks[qi]):
                        pg = pair_of[(g, n)]
                        if slot is not None:
                            v.wait_ge(S.h["pexp"], base["pexp"] + pg)
                            v.tensor_tensor(psb[pg % NPB][:, :], psb[pg % NPB][:, :],
                                            masks[:, slot * QT:(slot + 1) * QT],
                                            ALU.mult).then_inc(*S.inc("pmask", 1))
                    v.wait_ge(S.h["lr"], base["lr"] + g + 1)
                    v.wait_ge(S.h["mm"], mmpos[("o", pairs_of_g[g][-1])])
                    v.tensor_tensor(osb[:, h * QT:(h + 1) * QT], obk[:, :],
                                    Rsb2[g % 2][:, :], ALU.mult)\
                        .then_inc(*S.inc("oev", 1))
                    if g + 2 < NG_ALL:
                        rope(g + 2)
                for co in range(NCC):
                    oc = qi * NCC + co
                    v.wait_ge(S.h["mm"], mmpos[("yq", qi, co)])
                    if oc >= 2:
                        v.wait_ge(S.h[f"yst{oc % 2}"],
                                  base[f"yst{oc % 2}"] + 16 * (oc // 2))
                    v.tensor_scalar(ysb2[oc % 2][:, :], sbk[co % NPB][:, :],
                                    bo[:, co:co + 1], None, ALU.add)\
                        .then_inc(*S.inc("yev", 1))

        emit_barrier(block, env, ["qev1", "qev2", "pexp", "pmask", "lr", "oev",
                                  "yev", "yst0", "yst1", "qw0", "qw1", "ow0",
                                  "ow1", "hq0", "hq1", "tabc0", "tabc1",
                                  "tabs0", "tabs1", "mm"])


def emit_barrier(block, env, waits):
    S = env["sems"]
    vals = [(S.h[nm], S.val(nm)) for nm in waits]

    @block.sync
    def _(s):
        for h, v in vals:
            s.wait_ge(h, v)
        s.sem_inc(S.h["bar"], 1)

    S.n["bar"] += 1
    tgt = S.val("bar")
    for eng in ("tensor", "vector", "scalar", "gpsimd"):
        getattr(block, eng)(lambda e: e.wait_ge(S.h["bar"], tgt))


def build_program(sublayers=("attn_a", "ffn", "attn_b", "ffn"), h_out=False):
    nc = bass.Bass("TRN2", target_bir_lowering=False)
    ap = lambda t: t.ap()

    x_in = ap(nc.dram_tensor("x", [T, C], F32, kind="ExternalInput"))
    out = ap(nc.dram_tensor("out", [T, C], F32, kind="ExternalOutput"))

    gWT = ap(nc.dram_tensor("gWT", [NI, 128, NCC * 128], F32, kind="ExternalInput"))
    uWT = ap(nc.dram_tensor("uWT", [NI, 128, NCC * 128], F32, kind="ExternalInput"))
    dWT3 = ap(nc.dram_tensor("dWT3", [NCC, 128, NI * 128], F32, kind="ExternalInput"))
    gS_d = ap(nc.dram_tensor("gS", [128, NI], F32, kind="ExternalInput"))
    uS_d = ap(nc.dram_tensor("uS", [128, NI], F32, kind="ExternalInput"))
    dS_d = ap(nc.dram_tensor("dS", [128, NCC], F32, kind="ExternalInput"))

    aWqT = ap(nc.dram_tensor("aWqT", [NH, 128, NCC * 128], F32, kind="ExternalInput"))
    aWkT = ap(nc.dram_tensor("aWkT", [NKV_A, 128, NCC * 128], F32, kind="ExternalInput"))
    aWvT = ap(nc.dram_tensor("aWvT", [NKV_A, 128, NCC * 128], F32, kind="ExternalInput"))
    aWoT = ap(nc.dram_tensor("aWoT", [NCC, 128, NH * 128], F32, kind="ExternalInput"))
    bWqT = ap(nc.dram_tensor("bWqT", [NH, 128, NCC * 128], F32, kind="ExternalInput"))
    bWkT = ap(nc.dram_tensor("bWkT", [1, 128, NCC * 128], F32, kind="ExternalInput"))
    bWvT = ap(nc.dram_tensor("bWvT", [1, 128, NCC * 128], F32, kind="ExternalInput"))
    bWoT = ap(nc.dram_tensor("bWoT", [NCC, 128, NH * 128], F32, kind="ExternalInput"))
    bqA_d = ap(nc.dram_tensor("bqA", [128, NH], F32, kind="ExternalInput"))
    bkA_d = ap(nc.dram_tensor("bkA", [128, NKV_A], F32, kind="ExternalInput"))
    bvA_d = ap(nc.dram_tensor("bvA", [128, NKV_A], F32, kind="ExternalInput"))
    boA_d = ap(nc.dram_tensor("boA", [128, NCC], F32, kind="ExternalInput"))
    boB_d = ap(nc.dram_tensor("boB", [128, NCC], F32, kind="ExternalInput"))
    cosA_d = ap(nc.dram_tensor("cosA", [128, T], F32, kind="ExternalInput"))
    sinA_d = ap(nc.dram_tensor("sinA", [128, T], F32, kind="ExternalInput"))
    cosB_d = ap(nc.dram_tensor("cosB", [128, T], F32, kind="ExternalInput"))
    sinB_d = ap(nc.dram_tensor("sinB", [128, T], F32, kind="ExternalInput"))
    RaI = ap(nc.dram_tensor("RaI", [128, 128], F32, kind="ExternalInput"))
    RbI = ap(nc.dram_tensor("RbI", [128, 128], F32, kind="ExternalInput"))
    masksI = ap(nc.dram_tensor("masksI", [128, 8 * QT], F32, kind="ExternalInput"))
    identI = ap(nc.dram_tensor("identI", [128, 128], F32, kind="ExternalInput"))
    onesI = ap(nc.dram_tensor("onesI", [128, 128], F32, kind="ExternalInput"))

    xs_a = ap(nc.dram_tensor("xs_a", [T, C], F32))
    xs_b = ap(nc.dram_tensor("xs_b", [T, C], F32))
    h_sc = ap(nc.dram_tensor("h_sc", [T, C], BF16,
                             kind="ExternalOutput" if h_out else "Internal"))
    y_sc = ap(nc.dram_tensor("y_sc", [C, T], BF16))

    from contextlib import ExitStack
    with ExitStack() as stack:
        S = Sems(nc, stack, SEM_NAMES)
        sb = lambda name, shape, dt: stack.enter_context(nc.sbuf_tensor(name, shape, dt))
        B = {}
        B["xt"] = [sb(f"xt{i}", [128, C], F32) for i in range(2)]
        B["yt"] = [sb(f"yt{i}", [128, C], BF16) for i in range(2)]
        B["ht"] = [sb(f"ht{i}", [128, C], BF16) for i in range(2)]
        B["st8"] = sb("st8", [128, 16], F32)
        B["bn6"] = sb("bn6", [128, 48], F32)
        B["sqs"] = sb("sqs", [128, 16], F32)
        B["gSs"] = sb("gSs", [128, NI], F32)
        B["uSs"] = sb("uSs", [128, NI], F32)
        B["dSs"] = sb("dSs", [128, NCC], F32)
        B["masks"] = sb("masks", [128, 8 * QT], BF16)
        B["ident"] = sb("ident", [128, 128], BF16)
        B["ones"] = sb("ones", [128, 128], BF16)
        B["Ra"] = sb("Ra", [128, 128], BF16)
        B["Rb"] = sb("Rb", [128, 128], BF16)
        B["bqA"] = sb("bqA_s", [128, NH], F32)
        B["bkA"] = sb("bkA_s", [128, NKV_A], F32)
        B["bvA"] = sb("bvA_s", [128, NKV_A], F32)
        B["boA"] = sb("boA_s", [128, NCC], F32)
        B["boB"] = sb("boB_s", [128, NCC], F32)
        B["shiftc"] = sb("shiftc", [128, 1], F32)

        env = {"nc": nc, "sems": S, "bufs": B}

        with nc.Block() as block:
            @block.gpsimd
            def _(g):
                for dst, srcd in [(B["gSs"], gS_d), (B["uSs"], uS_d), (B["dSs"], dS_d),
                                  (B["masks"], masksI), (B["ident"], identI),
                                  (B["ones"], onesI), (B["Ra"], RaI),
                                  (B["Rb"], RbI), (B["bqA"], bqA_d), (B["bkA"], bkA_d),
                                  (B["bvA"], bvA_d), (B["boA"], boA_d),
                                  (B["boB"], boB_d)]:
                    g.dma_start(out=dst[:, :], in_=srcd[:, :]).then_inc(*S.inc("aux", 16))
                g.memset(B["shiftc"][:, :], -20.0).then_inc(*S.inc("aux2", 1))

            auxtot = S.val("aux")
            aux2tot = S.val("aux2")
            for eng in ("tensor", "vector", "scalar"):
                def _w(e):
                    e.wait_ge(S.h["aux"], auxtot)
                    e.wait_ge(S.h["aux2"], aux2tot)
                getattr(block, eng)(_w)

            x_cur = x_in
            x_scr = [xs_a, xs_b]
            y_prev = None
            for k, sl in enumerate(sublayers):
                x_next = x_scr[k % 2]
                emit_ln_phase(block, env, x_cur, x_next, y_prev, True, h_sc)
                emit_barrier(block, env, ["xld0", "xld1", "yld0", "yld1", "xst0",
                                          "xst1", "hst0", "hst1", "vln"])

                if sl == "ffn":
                    with (
                        nc.sbuf_tensor(f"hsb_{k}", [128, NCC * GROUP], BF16) as hsb,
                        nc.sbuf_tensor(f"zsb_{k}", [128, NI * GROUP], BF16) as zsb,
                        nc.sbuf_tensor(f"gw0_{k}", [128, C], BF16) as gw0,
                        nc.sbuf_tensor(f"gw1_{k}", [128, C], BF16) as gw1,
                        nc.sbuf_tensor(f"uw0_{k}", [128, C], BF16) as uw0,
                        nc.sbuf_tensor(f"uw1_{k}", [128, C], BF16) as uw1,
                        nc.sbuf_tensor(f"dw0_{k}", [128, INTER], BF16) as dw0,
                        nc.sbuf_tensor(f"dw1_{k}", [128, INTER], BF16) as dw1,
                        nc.sbuf_tensor(f"silu0_{k}", [128, QT], BF16) as silu0,
                        nc.sbuf_tensor(f"silu1_{k}", [128, QT], BF16) as silu1,
                        nc.sbuf_tensor(f"ysb0_{k}", [128, QT], BF16) as ysb0,
                        nc.sbuf_tensor(f"ysb1_{k}", [128, QT], BF16) as ysb1,
                        nc.psum_tensor(f"pg0_{k}", [128, QT], F32) as pg0,
                        nc.psum_tensor(f"pg1_{k}", [128, QT], F32) as pg1,
                        nc.psum_tensor(f"pu0_{k}", [128, QT], F32) as pu0,
                        nc.psum_tensor(f"pu1_{k}", [128, QT], F32) as pu1,
                        nc.psum_tensor(f"py0_{k}", [128, QT], F32) as py0,
                        nc.psum_tensor(f"py1_{k}", [128, QT], F32) as py1,
                    ):
                        B.update(hsb=hsb, zsb=zsb, gw=[gw0, gw1], uw=[uw0, uw1],
                                 dw=[dw0, dw1], silu=[silu0, silu1],
                                 ysb=[ysb0, ysb1], pg=[pg0, pg1], pu=[pu0, pu1],
                                 py=[py0, py1])
                        w = {"gWT": gWT, "uWT": uWT, "dWT3": dWT3,
                             "gS": B["gSs"], "uS": B["uSs"], "dS": B["dSs"]}
                        emit_ffn_phase(block, env, h_sc, y_sc, w)
                        emit_barrier(block, env,
                                     ["hld", "gld0", "gld1", "uld0", "uld1",
                                      "dwld0", "dwld1", "yst0", "yst1", "yev",
                                      "zst", "act", "mm"])
                elif sl in ("attn_a", "attn_b"):
                    nkv = NKV_A if sl == "attn_a" else 1
                    with (
                        nc.sbuf_tensor(f"ksb{k}", [128, nkv * T], BF16) as ksb,
                        nc.sbuf_tensor(f"vsb{k}", [128, nkv * T], BF16) as vsb,
                        nc.sbuf_tensor(f"hqA{k}", [128, NCC * QT], BF16) as hqA,
                        nc.sbuf_tensor(f"hqB{k}", [128, NCC * QT], BF16) as hqB,
                        nc.sbuf_tensor(f"tabcA{k}", [128, QT], BF16) as tabcA,
                        nc.sbuf_tensor(f"tabcB{k}", [128, QT], BF16) as tabcB,
                        nc.sbuf_tensor(f"tabsA{k}", [128, QT], BF16) as tabsA,
                        nc.sbuf_tensor(f"tabsB{k}", [128, QT], BF16) as tabsB,
                        nc.sbuf_tensor(f"mk2_{k}", [128, 8], F32) as mk2,
                    ):
                        B.update(ksb=ksb, vsb=vsb, hq=[hqA, hqB],
                                 tabc=[tabcA, tabcB], tabs=[tabsA, tabsB], mk2=mk2)
                        if sl == "attn_a":
                            w = {"WqT": aWqT, "WkT": aWkT, "WvT": aWvT,
                                 "WoT": aWoT, "cosA": cosA_d, "sinA": sinA_d,
                                 "cosB": cosB_d, "sinB": sinB_d}
                        else:
                            w = {"WqT": bWqT, "WkT": bWkT, "WvT": bWvT,
                                 "WoT": bWoT, "cosA": cosA_d, "sinA": sinA_d,
                                 "cosB": cosB_d, "sinB": sinB_d}
                        env["sfx"] = f"_{k}"
                        emit_attn_phase(block, env, h_sc, y_sc, w,
                                        "a" if sl == "attn_a" else "b")
                elif sl == "ln":
                    pass
                else:
                    raise NotImplementedError(sl)

                x_cur = x_next
                if sl != "ln":
                    y_prev = y_sc

            emit_ln_phase(block, env, x_cur, out, y_prev, False, None)

            @block.sync
            def _(s):
                s.wait_ge(S.h["xst0"], S.val("xst0"))
                s.wait_ge(S.h["xst1"], S.val("xst1"))

    return nc


def prep_ffn_inputs(gate_w, up_w, down_w, gate_s, up_s, down_s):
    d = {}
    d["gWT"] = np.ascontiguousarray(
        np.asarray(gate_w).reshape(NI, 128, NCC, 128).transpose(0, 3, 2, 1)
        .reshape(NI, 128, NCC * 128))
    d["uWT"] = np.ascontiguousarray(
        np.asarray(up_w).reshape(NI, 128, NCC, 128).transpose(0, 3, 2, 1)
        .reshape(NI, 128, NCC * 128))
    d["dWT3"] = np.ascontiguousarray(
        np.asarray(down_w).reshape(NCC, 128, NI, 128).transpose(0, 3, 2, 1)
        .reshape(NCC, 128, NI * 128))
    d["gS"] = np.ascontiguousarray(np.asarray(gate_s).reshape(NI, 128).T)
    d["uS"] = np.ascontiguousarray(np.asarray(up_s).reshape(NI, 128).T)
    d["dS"] = np.ascontiguousarray(np.asarray(down_s).reshape(NCC, 128).T)
    return d




def prep_attn_inputs(inputs):
    d = {}

    def tile_in(Wname, nheads):
        Wm = np.asarray(inputs[Wname])
        return np.ascontiguousarray(
            Wm.reshape(NCC, 128, nheads, 128).transpose(2, 1, 0, 3)
            .reshape(nheads, 128, NCC * 128))

    def tile_out(Wname):
        Wm = np.asarray(inputs[Wname])
        return np.ascontiguousarray(
            Wm.reshape(NH, 128, NCC, 128).transpose(2, 1, 0, 3)
            .reshape(NCC, 128, NH * 128))

    d["aWqT"] = tile_in("L0_attn_a_Wq", NH)
    d["aWkT"] = tile_in("L0_attn_a_Wk", NKV_A)
    d["aWvT"] = tile_in("L0_attn_a_Wv", NKV_A)
    d["aWoT"] = tile_out("L0_attn_a_Wo")
    d["bWqT"] = tile_in("L0_attn_b_Wq", NH)
    d["bWkT"] = tile_in("L0_attn_b_Wk", 1)
    d["bWvT"] = tile_in("L0_attn_b_Wv", 1)
    d["bWoT"] = tile_out("L0_attn_b_Wo")

    d["bqA"] = np.ascontiguousarray(np.asarray(inputs["L0_attn_a_bq"]).reshape(NH, 128).T)
    d["bkA"] = np.ascontiguousarray(np.asarray(inputs["L0_attn_a_bk"]).reshape(NKV_A, 128).T)
    d["bvA"] = np.ascontiguousarray(np.asarray(inputs["L0_attn_a_bv"]).reshape(NKV_A, 128).T)
    d["boA"] = np.ascontiguousarray(np.asarray(inputs["L0_attn_a_bo"]).reshape(NCC, 128).T)
    d["boB"] = np.ascontiguousarray(np.asarray(inputs["L0_attn_b_bo"]).reshape(NCC, 128).T)

    t = np.arange(T, dtype=np.float64)
    th_a = 10000.0 ** (-np.arange(0, HD, 2, dtype=np.float64) / HD)
    ang_a = np.outer(th_a, t)
    cosA = np.empty((128, T), np.float32)
    sinA = np.empty((128, T), np.float32)
    cosA[0::2] = np.cos(ang_a); cosA[1::2] = np.cos(ang_a)
    sinA[0::2] = -np.sin(ang_a); sinA[1::2] = np.sin(ang_a)
    cosA *= 0.5
    sinA *= 0.5
    d["cosA"], d["sinA"] = np.ascontiguousarray(cosA), np.ascontiguousarray(sinA)
    th_b = np.exp(-np.log(10000.0) * (np.arange(0, 64, 2, dtype=np.float64) / 64.0))
    ang_b = np.outer(th_b, t)
    cosB = np.ones((128, T), np.float32)
    sinB = np.zeros((128, T), np.float32)
    cosB[0:32] = np.cos(ang_b); cosB[32:64] = np.cos(ang_b)
    sinB[0:32] = -np.sin(ang_b); sinB[32:64] = np.sin(ang_b)
    cosB *= 0.25
    sinB *= 0.25
    d["cosB"], d["sinB"] = np.ascontiguousarray(cosB), np.ascontiguousarray(sinB)

    Ra = np.zeros((128, 128), np.float32)
    for i in range(64):
        Ra[2 * i, 2 * i + 1] = 1.0
        Ra[2 * i + 1, 2 * i] = 1.0
    Rb = np.zeros((128, 128), np.float32)
    for dd in range(32):
        Rb[dd, dd + 32] = 1.0
        Rb[dd + 32, dd] = 1.0
    d["RaI"], d["RbI"] = Ra, Rb
    d["identI"] = np.eye(128, dtype=np.float32)
    d["onesI"] = np.ones((128, 128), np.float32)

    kk = np.arange(128)[:, None]
    tt = np.arange(QT)[None, :]
    masks = np.zeros((128, 8 * QT), np.float32)
    for m in range(4):
        dd = m * 128
        masks[:, m * QT:(m + 1) * QT] = (kk + dd <= tt).astype(np.float32)
    for m in range(4):
        dd = -WINDOW + m * 128
        masks[:, (4 + m) * QT:(5 + m) * QT] = ((tt - kk) <= dd + WINDOW - 1).astype(np.float32)
    d["masksI"] = masks
    return d


SUBLAYERS = ("attn_a", "ffn", "attn_b", "ffn")


def make_in_maps(inputs):
    x = np.asarray(inputs["L0_attn_a_x"])
    wd = prep_ffn_inputs(inputs["L0_ffn_gate_w"], inputs["L0_ffn_up_w"],
                         inputs["L0_ffn_down_w"], inputs["L0_ffn_gate_s"],
                         inputs["L0_ffn_up_s"], inputs["L0_ffn_down_s"])
    wd.update(prep_attn_inputs(inputs))
    return [dict(wd, x=np.ascontiguousarray(x[b])) for b in range(8)]


def kernel(**inputs):
    nc = build_program(SUBLAYERS)
    in_maps = make_in_maps(inputs)
    res = run_bass_kernel_spmd(nc, in_maps, core_ids=list(range(8)))
    return np.stack([r["out"] for r in res.results], axis=0)


# version 04
# speedup vs baseline: 1.6622x; 1.3943x over previous
import numpy as np
import ml_dtypes
import concourse.bass as bass
import concourse.mybir as mybir
from concourse.bass_utils import run_bass_kernel_spmd

F32 = mybir.dt.float32
BF16 = mybir.dt.bfloat16
AF = mybir.ActivationFunctionType
ALU = mybir.AluOpType
AX = mybir.AxisListType
bfdt = ml_dtypes.bfloat16

T = 4096
C = 2048
HD = 128
NH = 16
NKV_A = 4
INTER = 5632
NI = INTER // 128
NCC = C // 128
QT = 512
NQT = T // QT
GROUP = 1024
NG = T // GROUP
TPG = GROUP // QT
NTT = 32
EPS = 1e-5
WINDOW = 512


class Sems:
    def __init__(self, nc, stack, names):
        self.h = {}
        self.n = {}
        for nm in names:
            self.h[nm] = stack.enter_context(nc.semaphore(nm))
            self.n[nm] = 0

    def inc(self, nm, k):
        self.n[nm] += k
        return self.h[nm], k

    def val(self, nm):
        return self.n[nm]


SEM_NAMES = [
    "xld0", "xld1", "yld0", "yld1", "hld", "hst0", "hst1", "xst",
    "yst0", "yst1", "xst0", "xst1",
    "vln", "sln", "happ",
    "gld0", "gld1", "uld0", "uld1", "dwld0", "dwld1",
    "mm", "act", "zst", "ymm", "yev",
    "bar", "aux", "aux2",
    "kw0", "kw1", "vw0", "vw1", "tabc0", "tabc1", "tabs0", "tabs1",
    "hq0", "hq1", "qw0", "qw1", "ow0", "ow1",
    "kev1", "kev2", "kev3", "kmax", "vev1", "vev2",
    "qev1", "qev2", "qev3", "cr", "cev", "varg", "pexp", "pmask", "lr", "oev",
]


def emit_ln_phase(block, env, x_src, x_dst, y_src, do_ln, h_dst):
    S, B = env["sems"], env["bufs"]
    xt, yt, ht = B["xt"], B["yt"], B["ht"]
    st8, bn6, sqs = B["st8"], B["bn6"], B["sqs"]
    have_y = y_src is not None
    b = {nm: S.val(nm) for nm in ("xld0", "xld1", "yld0", "yld1", "vln", "sln",
                                  "happ", "xst0", "xst1", "hst0", "hst1")}

    @block.sync
    def _(s):
        for i in range(NTT + 2):
            if 2 <= i:
                j = i - 2
                s.wait_ge(S.h["vln"], b["vln"] + j + 1)
                s.dma_start(out=x_dst[j * 128:(j + 1) * 128, :], in_=xt[j % 2][:, :])\
                    .then_inc(*S.inc(f"xst{j % 2}", 16))
                if do_ln:
                    s.dma_start(out=h_dst[j * 128:(j + 1) * 128, :], in_=ht[j % 2][:, :])\
                        .then_inc(*S.inc(f"hst{j % 2}", 16))
            if i < NTT:
                if i >= 2:
                    s.wait_ge(S.h[f"xst{i % 2}"], b[f"xst{i % 2}"] + 16 * (i // 2))
                s.dma_start(out=xt[i % 2][:, :], in_=x_src[i * 128:(i + 1) * 128, :])\
                    .then_inc(*S.inc(f"xld{i % 2}", 16))
                if have_y:
                    s.dma_start_transpose(yt[i % 2][:, :], y_src[:, i * 128:(i + 1) * 128])\
                        .then_inc(*S.inc(f"yld{i % 2}", 16))

    @block.vector
    def _(v):
        for i in range(NTT):
            p = i % 2
            v.wait_ge(S.h[f"xld{p}"], b[f"xld{p}"] + 16 * (i // 2 + 1))
            if have_y:
                v.wait_ge(S.h[f"yld{p}"], b[f"yld{p}"] + 16 * (i // 2 + 1))
            if do_ln:
                if have_y:
                    v.tensor_tensor(xt[p][:, :], xt[p][:, :], yt[p][:, :], ALU.add)
                    v.drain()
                if i >= 2:
                    v.wait_ge(S.h["happ"], b["happ"] + i - 1)
                for g in range(4):
                    v.bn_stats(bn6[:, p * 24 + 6 * g: p * 24 + 6 * g + 6],
                               xt[p][:, 512 * g:512 * (g + 1)])
                v.drain()
                b6 = bn6[:, p * 24: p * 24 + 24].rearrange("a (g s) -> a g s", s=3)
                col = st8[:, p * 8: p * 8 + 8]
                v.tensor_reduce(col[:, 0:1], b6[:, :, 1], AX.X, ALU.add)
                v.tensor_reduce(col[:, 1:2], b6[:, :, 2], AX.X, ALU.add)
                v.scalar_tensor_tensor(sqs[:, p * 8: p * 8 + 8],
                                       b6[:, :, 1], 1.0, b6[:, :, 1],
                                       ALU.bypass, ALU.mult,
                                       accum_out=col[:, 2:3])
                v.drain()
                v.tensor_scalar(col[:, 4:5], col[:, 0:1], 1.0 / 8, None, ALU.mult)
                v.tensor_scalar(col[:, 6:7], col[:, 2:3], 1.0 / 8, None, ALU.mult)
                v.drain()
                v.tensor_scalar(col[:, 5:6], col[:, 4:5], -1.0, None, ALU.mult)
                v.scalar_tensor_tensor(col[:, 6:7], col[:, 1:2], 1.0 / 2048,
                                       col[:, 6:7], ALU.mult, ALU.add)
                v.drain()
                v.scalar_tensor_tensor(col[:, 6:7], col[:, 4:5], col[:, 5:6],
                                       col[:, 6:7], ALU.mult, ALU.add)
                v.drain()
                v.tensor_single_scalar(col[:, 6:7], col[:, 6:7], EPS, ALU.add)\
                    .then_inc(*S.inc("sln", 1))
                v.wait_ge(S.h["happ"], b["happ"] + i + 1)
                if i >= 2:
                    v.wait_ge(S.h[f"hst{p}"], b[f"hst{p}"] + 16 * (i // 2))
                v.tensor_scalar(ht[p][:, :], xt[p][:, :], col[:, 4:5], col[:, 7:8],
                                ALU.subtract, ALU.mult).then_inc(*S.inc("vln", 1))
            elif have_y:
                v.tensor_tensor(xt[p][:, :], xt[p][:, :], yt[p][:, :], ALU.add)\
                    .then_inc(*S.inc("vln", 1))
            else:
                v.tensor_single_scalar(xt[p][:, :], xt[p][:, :], 0.0, ALU.add)\
                    .then_inc(*S.inc("vln", 1))

    if do_ln:
        @block.scalar
        def _(sc):
            for i in range(NTT):
                p = i % 2
                col = st8[:, p * 8: p * 8 + 8]
                sc.wait_ge(S.h["sln"], b["sln"] + i + 1)
                sc.activation(col[:, 7:8], col[:, 6:7], AF.Ln)
                sc.drain()
                sc.activation(col[:, 7:8], col[:, 7:8], AF.Exp, scale=-0.5)\
                    .then_inc(*S.inc("happ", 1))


def emit_ffn_phase(block, env, h_src, y_dst, w):
    S, B = env["sems"], env["bufs"]
    hsb, zsb = B["hsb"], B["zsb"]
    gw, uw, dw = B["gw"], B["uw"], B["dw"]
    silu, ysb = B["silu"], B["ysb"]
    pg, pu, py = B["pg"], B["pu"], B["py"]
    gS, uS, dS = w["gS"], w["uS"], w["dS"]

    b = {nm: S.val(nm) for nm in ("hld", "gld0", "gld1", "uld0", "uld1",
                                  "dwld0", "dwld1", "mm", "act", "zst",
                                  "ymm", "yev", "yst0", "yst1")}

    gWTr = w["gWT"]
    uWTr = w["uWT"]
    dWT3r = w["dWT3"]

    mm_g, mm_u, mm_y = {}, {}, {}
    mmc = 0
    for grp in range(NG):
        for i in range(NI):
            for tt in range(TPG):
                mmc += 1
                mm_g[(grp, i, tt)] = mmc
            for tt in range(TPG):
                mmc += 1
                mm_u[(grp, i, tt)] = mmc
        for co in range(NCC):
            for tt in range(TPG):
                mmc += 1
                mm_y[(grp, co, tt)] = mmc

    def chunk_gi(n):
        return divmod(n, NI)

    @block.gpsimd
    def _(g):
        for grp in range(NG):
            for i in range(NI):
                n = grp * NI + i
                par = n % 2
                if n >= 2:
                    pg_, pi_ = chunk_gi(n - 2)
                    g.wait_ge(S.h["mm"], b["mm"] + mm_g[(pg_, pi_, TPG - 1)])
                g.dma_start(out=gw[par][:, :], in_=gWTr[i, :, :])\
                    .then_inc(*S.inc(f"gld{par}", 16))
                if n >= 2:
                    pg_, pi_ = chunk_gi(n - 2)
                    g.wait_ge(S.h["mm"], b["mm"] + mm_u[(pg_, pi_, TPG - 1)])
                g.dma_start(out=uw[par][:, :], in_=uWTr[i, :, :])\
                    .then_inc(*S.inc(f"uld{par}", 16))
            for co in range(NCC):
                n = grp * NCC + co
                if n >= 2:
                    pg_, pco_ = divmod(n - 2, NCC)
                    g.wait_ge(S.h["mm"], b["mm"] + mm_y[(pg_, pco_, TPG - 1)])
                g.dma_start(out=dw[n % 2][:, :], in_=dWT3r[co, :, :])\
                    .then_inc(*S.inc(f"dwld{n % 2}", 16))

    @block.sync
    def _(s):
        for grp in range(NG):
            if grp >= 1:
                s.wait_ge(S.h["mm"], b["mm"] + mm_u[(grp - 1, NI - 1, TPG - 1)])
            for cc in range(NCC):
                for tt in range(TPG):
                    t0 = grp * GROUP + tt * QT
                    s.dma_start_transpose(
                        hsb[:, cc * GROUP + tt * QT: cc * GROUP + (tt + 1) * QT],
                        h_src[t0:t0 + QT, cc * 128:(cc + 1) * 128],
                    ).then_inc(*S.inc("hld", 16))

    @block.tensor
    def _(t):
        for grp in range(NG):
            t.wait_ge(S.h["hld"], b["hld"] + 16 * NCC * TPG * (grp + 1))
            for i in range(NI):
                n = grp * NI + i
                par = n % 2
                t.wait_ge(S.h[f"gld{par}"], b[f"gld{par}"] + 16 * (n // 2 + 1))
                for tt in range(TPG):
                    gi = n * TPG + tt
                    if n >= 1:
                        t.wait_ge(S.h["act"], b["act"] + gi - TPG + 1)
                    for cc in range(NCC):
                        ins = t.matmul(pg[tt][:, :], gw[par][:, cc * 128:(cc + 1) * 128],
                                       hsb[:, cc * GROUP + tt * QT: cc * GROUP + (tt + 1) * QT],
                                       start=(cc == 0), stop=(cc == NCC - 1))
                    ins.then_inc(*S.inc("mm", 1))
                t.wait_ge(S.h[f"uld{par}"], b[f"uld{par}"] + 16 * (n // 2 + 1))
                for tt in range(TPG):
                    gi = n * TPG + tt
                    if n >= 1:
                        t.wait_ge(S.h["zst"], b["zst"] + gi - TPG + 1)
                    for cc in range(NCC):
                        ins = t.matmul(pu[tt][:, :], uw[par][:, cc * 128:(cc + 1) * 128],
                                       hsb[:, cc * GROUP + tt * QT: cc * GROUP + (tt + 1) * QT],
                                       start=(cc == 0), stop=(cc == NCC - 1))
                    ins.then_inc(*S.inc("mm", 1))
            t.wait_ge(S.h["zst"], b["zst"] + (grp + 1) * NI * TPG)
            for co in range(NCC):
                n = grp * NCC + co
                t.wait_ge(S.h[f"dwld{n % 2}"], b[f"dwld{n % 2}"] + 16 * (n // 2 + 1))
                for tt in range(TPG):
                    yi = n * TPG + tt
                    if yi >= 2:
                        t.wait_ge(S.h["yev"], b["yev"] + yi - 1)
                    for ich in range(NI):
                        ins = t.matmul(py[yi % 2][:, :], dw[n % 2][:, ich * 128:(ich + 1) * 128],
                                       zsb[:, ich * GROUP + tt * QT: ich * GROUP + (tt + 1) * QT],
                                       start=(ich == 0), stop=(ich == NI - 1))
                    ins.then_inc(*S.inc("mm", 1))
                    t.sem_inc(*S.inc("ymm", 1))

    @block.scalar
    def _(sc):
        for grp in range(NG):
            for i in range(NI):
                for tt in range(TPG):
                    gi = (grp * NI + i) * TPG + tt
                    sc.wait_ge(S.h["mm"], b["mm"] + mm_g[(grp, i, tt)])
                    if gi >= 2:
                        sc.wait_ge(S.h["zst"], b["zst"] + gi - 1)
                    sc.activation(silu[gi % 2][:, :], pg[tt][:, :], AF.Silu,
                                  scale=gS[:, i:i + 1]).then_inc(*S.inc("act", 1))
            for co in range(NCC):
                for tt in range(TPG):
                    yi = (grp * NCC + co) * TPG + tt
                    t0 = grp * GROUP + tt * QT
                    sc.wait_ge(S.h["yev"], b["yev"] + yi + 1)
                    sc.dma_start(out=y_dst[co * 128:(co + 1) * 128, t0:t0 + QT],
                                 in_=ysb[yi % 2][:, :]).then_inc(*S.inc(f"yst{yi % 2}", 16))

    @block.vector
    def _(v):
        for grp in range(NG):
            for i in range(NI):
                for tt in range(TPG):
                    gi = (grp * NI + i) * TPG + tt
                    v.wait_ge(S.h["mm"], b["mm"] + mm_u[(grp, i, tt)])
                    v.wait_ge(S.h["act"], b["act"] + gi + 1)
                    if grp >= 1 and i == 0 and tt == 0:
                        v.wait_ge(S.h["mm"], b["mm"] + mm_y[(grp - 1, NCC - 1, TPG - 1)])
                    v.scalar_tensor_tensor(
                        zsb[:, i * GROUP + tt * QT: i * GROUP + (tt + 1) * QT],
                        pu[tt][:, :], uS[:, i:i + 1], silu[gi % 2][:, :],
                        ALU.mult, ALU.mult).then_inc(*S.inc("zst", 1))
            for co in range(NCC):
                for tt in range(TPG):
                    yi = (grp * NCC + co) * TPG + tt
                    v.wait_ge(S.h["mm"], b["mm"] + mm_y[(grp, co, tt)])
                    if yi >= 2:
                        v.wait_ge(S.h[f"yst{yi % 2}"], b[f"yst{yi % 2}"] + 16 * (yi // 2))
                    v.tensor_scalar(ysb[yi % 2][:, :], py[yi % 2][:, :],
                                    dS[:, co:co + 1], None, ALU.mult)\
                        .then_inc(*S.inc("yev", 1))




def attn_chunks(kind, qi):
    Q0 = qi * QT
    lo = 0 if kind == "a" else max(0, Q0 - WINDOW)
    out = []
    for K0 in range(lo, Q0 + QT - 128 + 1, 128):
        d = K0 - Q0
        if d >= 0:
            slot = d // 128
        elif kind == "b":
            slot = 4 + (d + WINDOW) // 128
        else:
            slot = None
        out.append((K0, slot))
    return out


def emit_attn_phase(block, env, h_src, y_dst, w, kind):
    sfx = env.get("sfx", "")
    S, B = env["sems"], env["bufs"]
    nkv = NKV_A if kind == "a" else 1
    r_imm = 1.0 if kind == "a" else float(np.sqrt(2.0))
    qk_scale = 0.5 if kind == "a" else 0.25

    masks = B["masks"]
    ident, ones = B["ident"], B["ones"]
    shiftc = B["shiftc"]
    R_sb = B["Ra"] if kind == "a" else B["Rb"]
    bq = B.get("bqA") if kind == "a" else None
    bk = B.get("bkA") if kind == "a" else None
    bv = B.get("bvA") if kind == "a" else None
    bo = B["boA"] if kind == "a" else B["boB"]
    cosT, sinT = (w["cosA"], w["sinA"]) if kind == "a" else (w["cosB"], w["sinB"])
    WqT, WkT, WvT, WoT = w["WqT"], w["WkT"], w["WvT"], w["WoT"]

    hq = B["hq"]
    ksb, vsb = B["ksb"], B["vsb"]
    tabc, tabs = B["tabc"], B["tabs"]

    base = {nm: S.val(nm) for nm in (
        "mm", "kw0", "vw0", "tabc0", "tabc1", "tabs0", "tabs1",
        "hq0", "hq1", "qw0", "qw1", "ow0", "ow1",
        "kev1", "kev2", "vev1", "vev2",
        "qev1", "qev2", "pexp", "pmask", "lr",
        "oev", "yev", "yst0", "yst1")}

    chunks = {qi: attn_chunks(kind, qi) for qi in range(NQT)}
    pair_of = {}
    pairs_of_g = {}
    pmask_idx = {}
    pgc = 0
    pmc = 0
    for qi in range(NQT):
        for h in range(NH):
            g = qi * NH + h
            lst = []
            for n, (K0, slot) in enumerate(chunks[qi]):
                pgc += 1
                pair_of[(g, n)] = pgc
                lst.append(pgc)
                if slot is not None:
                    pmc += 1
                    pmask_idx[pgc] = pmc
            pairs_of_g[g] = lst

    mmpos = {}

    from contextlib import ExitStack

    with ExitStack() as _st:
        _sb = lambda n, sh, dt: _st.enter_context(env["nc"].sbuf_tensor(n + sfx, sh, dt))
        _ps = lambda n, sh, dt: _st.enter_context(env["nc"].psum_tensor(n + sfx, sh, dt))
        kwb = [_sb(f"kwb{j}", [128, NCC * 128], BF16) for j in range(nkv)]
        vwb = [_sb(f"vwb{j}", [128, NCC * 128], BF16) for j in range(nkv)]
        kscb = _sb("kscb", [128, QT], BF16)
        vscb = _sb("vscb", [128, QT], BF16)
        scr1 = _sb("scr1", [128, QT], BF16)
        scr2 = _sb("scr2", [128, QT], BF16)
        kraw = _ps("kraw", [128, QT], F32)
        rotb = _ps("rotb", [128, QT], F32)
        vraw = _ps("vraw", [128, QT], F32)
        vtb = _ps("vtb", [128, QT], BF16)

        def kv_idx(tt8, j):
            return tt8 * nkv + j + 1

        @block.tensor
        def _(t):
            for tt8 in range(8):
                for j in range(nkv):
                    idx = kv_idx(tt8, j)
                    t.wait_ge(S.h["kw0"], base["kw0"] + 16 * nkv)
                    t.wait_ge(S.h[f"hq{tt8 % 2}"],
                              base[f"hq{tt8 % 2}"] + 16 * NCC * (tt8 // 2 + 1))
                    if idx >= 2:
                        t.wait_ge(S.h["kev1"], base["kev1"] + idx - 1)
                    for cc in range(NCC):
                        ins = t.matmul(kraw[:, :], kwb[j][:, cc * 128:(cc + 1) * 128],
                                       hq[tt8 % 2][:, cc * QT:(cc + 1) * QT],
                                       start=(cc == 0), stop=(cc == NCC - 1))
                    ins.then_inc(*S.inc("mm", 1))
                    mmpos[("kraw", idx)] = S.val("mm")
                    t.wait_ge(S.h["kev1"], base["kev1"] + idx)
                    if idx >= 2:
                        t.wait_ge(S.h["kev2"], base["kev2"] + idx - 1)
                    ins = t.matmul(rotb[:, :], R_sb[:, :], kscb[:, :],
                                   start=True, stop=True)
                    ins.then_inc(*S.inc("mm", 1))
                    mmpos[("rot", idx)] = S.val("mm")
                    t.wait_ge(S.h["vw0"], base["vw0"] + 16 * nkv)
                    if idx >= 2:
                        t.wait_ge(S.h["vev1"], base["vev1"] + idx - 1)
                    for cc in range(NCC):
                        ins = t.matmul(vraw[:, :], vwb[j][:, cc * 128:(cc + 1) * 128],
                                       hq[tt8 % 2][:, cc * QT:(cc + 1) * QT],
                                       start=(cc == 0), stop=(cc == NCC - 1))
                    ins.then_inc(*S.inc("mm", 1))
                    mmpos[("vraw", idx)] = S.val("mm")
                    t.wait_ge(S.h["vev1"], base["vev1"] + idx)
                    if idx >= 2:
                        t.wait_ge(S.h["vev2"], base["vev2"] + idx - 1)
                    for s_ in range(4):
                        ins = t.transpose(vtb[:, s_ * 128:(s_ + 1) * 128],
                                          vscb[:, s_ * 128:(s_ + 1) * 128],
                                          ident[:, :])
                    ins.then_inc(*S.inc("mm", 1))
                    mmpos[("vt", idx)] = S.val("mm")

        @block.gpsimd
        def _(g):
            for j in range(nkv):
                g.dma_start(out=kwb[j][:, :], in_=WkT[j, :, :])\
                    .then_inc(*S.inc("kw0", 16))
                g.dma_start(out=vwb[j][:, :], in_=WvT[j, :, :])\
                    .then_inc(*S.inc("vw0", 16))
            for tt8 in range(8):
                par = tt8 % 2
                if tt8 >= 2:
                    g.wait_ge(S.h["kev2"], base["kev2"] + kv_idx(tt8 - 2, nkv - 1))
                g.dma_start(out=tabc[par][:, :], in_=cosT[:, tt8 * QT:(tt8 + 1) * QT])\
                    .then_inc(*S.inc(f"tabc{par}", 16))
                g.dma_start(out=tabs[par][:, :], in_=sinT[:, tt8 * QT:(tt8 + 1) * QT])\
                    .then_inc(*S.inc(f"tabs{par}", 16))

        @block.sync
        def _(s):
            for tt8 in range(8):
                par = tt8 % 2
                if tt8 >= 2:
                    s.wait_ge(S.h["mm"], mmpos[("vraw", kv_idx(tt8 - 2, nkv - 1))])
                for cc in range(NCC):
                    s.dma_start_transpose(
                        hq[par][:, cc * QT:(cc + 1) * QT],
                        h_src[tt8 * QT:(tt8 + 1) * QT, cc * 128:(cc + 1) * 128],
                    ).then_inc(*S.inc(f"hq{par}", 16))


        @block.vector
        def _(v):
            for tt8 in range(8):
                par = tt8 % 2
                for j in range(nkv):
                    idx = kv_idx(tt8, j)
                    v.wait_ge(S.h["mm"], mmpos[("kraw", idx)])
                    if idx >= 2:
                        v.wait_ge(S.h["mm"], mmpos[("rot", idx - 1)])
                    if kind == "a":
                        v.tensor_scalar(kscb[:, :], kraw[:, :], bk[:, j:j + 1],
                                        None, ALU.add)\
                            .then_inc(*S.inc("kev1", 1))
                    else:
                        v.tensor_copy(kscb[:, :], kraw[:, :])\
                            .then_inc(*S.inc("kev1", 1))
                    v.drain()
                    v.wait_ge(S.h[f"tabc{par}"], base[f"tabc{par}"] + 16 * (tt8 // 2 + 1))
                    v.wait_ge(S.h[f"tabs{par}"], base[f"tabs{par}"] + 16 * (tt8 // 2 + 1))
                    v.tensor_tensor(scr1[:, :], kscb[:, :], tabc[par][:, :], ALU.mult)
                    v.wait_ge(S.h["mm"], mmpos[("rot", idx)])
                    v.tensor_tensor(scr2[:, :], rotb[:, :], tabs[par][:, :], ALU.mult)
                    v.drain()
                    v.tensor_tensor(ksb[:, j * T + tt8 * QT: j * T + (tt8 + 1) * QT],
                                    scr1[:, :], scr2[:, :], ALU.add)\
                        .then_inc(*S.inc("kev2", 1))
                    v.drain()
                    v.wait_ge(S.h["mm"], mmpos[("vraw", idx)])
                    if idx >= 2:
                        v.wait_ge(S.h["mm"], mmpos[("vt", idx - 1)])
                    if kind == "a":
                        v.tensor_scalar(vscb[:, :], vraw[:, :], bv[:, j:j + 1],
                                        None, ALU.add).then_inc(*S.inc("vev1", 1))
                    else:
                        v.tensor_copy(vscb[:, :], vraw[:, :])\
                            .then_inc(*S.inc("vev1", 1))
                    v.wait_ge(S.h["mm"], mmpos[("vt", idx)])
                    v.tensor_copy(
                        vsb[:, (j * 32 + tt8 * 4) * 128:(j * 32 + tt8 * 4 + 4) * 128],
                        vtb[:, :]).then_inc(*S.inc("vev2", 1))

        emit_barrier(block, env, ["kev1", "kev2", "vev1", "vev2",
                                  "kw0", "vw0", "tabc0", "tabc1",
                                  "tabs0", "tabs1", "hq0", "hq1", "mm"])

    NPB = 4
    NG_ALL = NQT * NH
    with ExitStack() as _st:
        _sb = lambda n, sh, dt: _st.enter_context(env["nc"].sbuf_tensor(n + sfx, sh, dt))
        _ps = lambda n, sh, dt: _st.enter_context(env["nc"].psum_tensor(n + sfx, sh, dt))
        qwb = [_sb(f"qwb{i}", [128, NCC * 128], BF16) for i in range(2)]
        owb = [_sb(f"owb{i}", [128, NH * 128], BF16) for i in range(2)]
        qscb = _sb("qscb", [128, QT], BF16)
        qsb = [_sb(f"qsb{i}", [128, QT], BF16) for i in range(2)]
        scr1 = _sb("scr1q", [128, QT], BF16)
        scr2 = _sb("scr2q", [128, QT], BF16)
        psb = [_sb(f"psb{i}", [128, QT], BF16) for i in range(NPB)]
        Rsb2 = [_sb(f"Rsb{i}", [128, QT], F32) for i in range(2)]
        osb = _sb("osb", [128, NH * QT], BF16)
        ysb2 = [_sb(f"ysb2{i}", [128, QT], BF16) for i in range(2)]
        qrot = [_ps(f"qrot{i}", [128, QT], F32) for i in range(2)]
        sbk = [_ps(f"sb{i}", [128, QT], F32) for i in range(NPB)]
        obk = _ps("ob", [128, QT], F32)
        lbk = _ps("lb", [128, QT], F32)

        def emit_prep(t, g):
            qi, h = divmod(g, NH)
            par = g % 2
            t.wait_ge(S.h[f"qw{par}"], base[f"qw{par}"] + 16 * (g // 2 + 1))
            t.wait_ge(S.h[f"hq{qi % 2}"],
                      base[f"hq{qi % 2}"] + 16 * NCC * (4 + qi // 2 + 1))
            if g >= 2:
                t.wait_ge(S.h["qev2"], base["qev2"] + g - 1)
            for cc in range(NCC):
                ins = t.matmul(qrot[par][:, :], qwb[par][:, cc * 128:(cc + 1) * 128],
                               hq[qi % 2][:, cc * QT:(cc + 1) * QT],
                               start=(cc == 0), stop=(cc == NCC - 1))
            ins.then_inc(*S.inc("mm", 1))
            mmpos[("qraw", g)] = S.val("mm")
            t.wait_ge(S.h["qev1"], base["qev1"] + g + 1)
            ins = t.matmul(qrot[par][:, :], R_sb[:, :], qscb[:, :],
                           start=True, stop=True)
            ins.then_inc(*S.inc("mm", 1))
            mmpos[("rotq", g)] = S.val("mm")

        @block.tensor
        def _(t):
            emit_prep(t, 0)
            emit_prep(t, 1)
            for qi in range(NQT):
                lst = chunks[qi]
                for h in range(NH):
                    g = qi * NH + h
                    j = h % nkv
                    par = g % 2
                    P = len(lst)
                    LEAD = NPB - 1
                    for n in range(P + LEAD):
                        if n < P:
                            K0, slot = lst[n]
                            pg = pair_of[(g, n)]
                            if n == 0:
                                t.wait_ge(S.h["qev2"], base["qev2"] + g + 1)
                            if pg > NPB:
                                t.wait_ge(S.h["pexp"], base["pexp"] + pg - NPB)
                            if qi >= 1 and h == 0 and n < NPB:
                                t.wait_ge(S.h["yev"], base["yev"] + qi * NCC)
                            ins = t.matmul(sbk[pg % NPB][:, :],
                                           ksb[:, j * T + K0: j * T + K0 + 128],
                                           qsb[par][:, :], start=True, stop=True)
                            ins.then_inc(*S.inc("mm", 1))
                            mmpos[("s", pg)] = S.val("mm")
                        if n >= LEAD:
                            m = n - LEAD
                            K0m, slotm = lst[m]
                            pgm = pair_of[(g, m)]
                            t.wait_ge(S.h["pexp"], base["pexp"] + pgm)
                            if slotm is not None:
                                t.wait_ge(S.h["pmask"], base["pmask"] + pmask_idx[pgm])
                            if m == 0 and g >= 1:
                                t.wait_ge(S.h["lr"], base["lr"] + g)
                                t.wait_ge(S.h["oev"], base["oev"] + g)
                            t.matmul(lbk[:, :], ones[:, :], psb[pgm % NPB][:, :],
                                     start=(m == 0), stop=(m == P - 1))
                            ins = t.matmul(obk[:, :],
                                           vsb[:, (j * 32 + (K0m // 128)) * 128:
                                               (j * 32 + (K0m // 128) + 1) * 128],
                                           psb[pgm % NPB][:, :],
                                           start=(m == 0), stop=(m == P - 1))
                            ins.then_inc(*S.inc("mm", 1))
                            mmpos[("o", pgm)] = S.val("mm")
                    if g + 2 < NG_ALL:
                        emit_prep(t, g + 2)
                for co in range(NCC):
                    oc = qi * NCC + co
                    t.wait_ge(S.h[f"ow{oc % 2}"], base[f"ow{oc % 2}"] + 16 * (oc // 2 + 1))
                    t.wait_ge(S.h["oev"], base["oev"] + (qi + 1) * NH)
                    t.wait_ge(S.h["pexp"],
                              base["pexp"] + pair_of[(qi * NH + NH - 1, len(lst) - 1)])
                    if oc >= NPB:
                        t.wait_ge(S.h["yev"], base["yev"] + oc - NPB + 1)
                    for hh in range(NH):
                        ins = t.matmul(sbk[co % NPB][:, :],
                                       owb[oc % 2][:, hh * 128:(hh + 1) * 128],
                                       osb[:, hh * QT:(hh + 1) * QT],
                                       start=(hh == 0), stop=(hh == NH - 1))
                    ins.then_inc(*S.inc("mm", 1))
                    mmpos[("yq", qi, co)] = S.val("mm")

        @block.gpsimd
        def _(g_):
            def qwload(g):
                if g >= 2:
                    g_.wait_ge(S.h["mm"], mmpos[("qraw", g - 2)])
                g_.dma_start(out=qwb[g % 2][:, :], in_=WqT[g % NH, :, :])\
                    .then_inc(*S.inc(f"qw{g % 2}", 16))

            def tabload(qi2):
                par8 = qi2 % 2
                if qi2 >= 2:
                    g_.wait_ge(S.h["qev2"], base["qev2"] + (qi2 - 1) * NH)
                g_.dma_start(out=tabc[par8][:, :], in_=cosT[:, qi2 * QT:(qi2 + 1) * QT])\
                    .then_inc(*S.inc(f"tabc{par8}", 16))
                g_.dma_start(out=tabs[par8][:, :], in_=sinT[:, qi2 * QT:(qi2 + 1) * QT])\
                    .then_inc(*S.inc(f"tabs{par8}", 16))

            qwload(0)
            qwload(1)
            tabload(0)
            for qi in range(NQT):
                for h in range(NH):
                    g = qi * NH + h
                    if h == NH - 2 and qi + 1 < NQT:
                        tabload(qi + 1)
                    if g + 2 < NG_ALL:
                        qwload(g + 2)
                for co in range(NCC):
                    oc = qi * NCC + co
                    if oc >= 2:
                        g_.wait_ge(S.h["mm"], mmpos[("yq", (oc - 2) // NCC,
                                                     (oc - 2) % NCC)])
                    g_.dma_start(out=owb[oc % 2][:, :], in_=WoT[co, :, :])\
                        .then_inc(*S.inc(f"ow{oc % 2}", 16))

        @block.sync
        def _(s):
            for qi in range(NQT):
                par = qi % 2
                if qi >= 2:
                    s.wait_ge(S.h["mm"], mmpos[("qraw", (qi - 2) * NH + NH - 1)])
                for cc in range(NCC):
                    s.dma_start_transpose(
                        hq[par][:, cc * QT:(cc + 1) * QT],
                        h_src[qi * QT:(qi + 1) * QT, cc * 128:(cc + 1) * 128],
                    ).then_inc(*S.inc(f"hq{par}", 16))

        @block.scalar
        def _(sc):
            for qi in range(NQT):
                for h in range(NH):
                    g = qi * NH + h
                    par = g % 2
                    for n, (K0, slot) in enumerate(chunks[qi]):
                        pg = pair_of[(g, n)]
                        sc.wait_ge(S.h["mm"], mmpos[("s", pg)])
                        if pg > NPB:
                            sc.wait_ge(S.h["mm"], mmpos[("o", pg - NPB)])
                        sc.activation(psb[pg % NPB][:, :], sbk[pg % NPB][:, :],
                                      AF.Exp, bias=shiftc[:, 0:1], scale=r_imm)\
                            .then_inc(*S.inc("pexp", 1))
                    sc.wait_ge(S.h["mm"], mmpos[("o", pairs_of_g[g][-1])])
                    if g >= 2:
                        sc.wait_ge(S.h["oev"], base["oev"] + g - 1)
                    sc.activation(Rsb2[par][:, :], lbk[:, :], AF.Ln)
                    sc.drain()
                    sc.activation(Rsb2[par][:, :], Rsb2[par][:, :], AF.Exp, scale=-1.0)\
                        .then_inc(*S.inc("lr", 1))
                for co in range(NCC):
                    oc = qi * NCC + co
                    sc.wait_ge(S.h["yev"], base["yev"] + oc + 1)
                    sc.dma_start(out=y_dst[co * 128:(co + 1) * 128,
                                           qi * QT:(qi + 1) * QT],
                                 in_=ysb2[oc % 2][:, :])\
                        .then_inc(*S.inc(f"yst{oc % 2}", 16))

        @block.vector
        def _(v):
            def rope(g):
                qi2, h2 = divmod(g, NH)
                par2 = g % 2
                v.wait_ge(S.h["mm"], mmpos[("qraw", g)])
                if g >= 1:
                    v.wait_ge(S.h["mm"], mmpos[("rotq", g - 1)])
                if kind == "a":
                    v.tensor_scalar(qscb[:, :], qrot[par2][:, :], bq[:, h2:h2 + 1],
                                    None, ALU.add).then_inc(*S.inc("qev1", 1))
                else:
                    v.tensor_copy(qscb[:, :], qrot[par2][:, :])\
                        .then_inc(*S.inc("qev1", 1))
                v.drain()
                v.wait_ge(S.h[f"tabc{qi2 % 2}"],
                          base[f"tabc{qi2 % 2}"] + 16 * (4 + qi2 // 2 + 1))
                v.wait_ge(S.h[f"tabs{qi2 % 2}"],
                          base[f"tabs{qi2 % 2}"] + 16 * (4 + qi2 // 2 + 1))
                if g >= 2:
                    v.wait_ge(S.h["mm"], mmpos[("s", pairs_of_g[g - 2][-1])])
                v.tensor_tensor(scr1[:, :], qscb[:, :], tabc[qi2 % 2][:, :], ALU.mult)
                v.wait_ge(S.h["mm"], mmpos[("rotq", g)])
                v.tensor_tensor(scr2[:, :], qrot[par2][:, :], tabs[qi2 % 2][:, :],
                                ALU.mult)
                v.drain()
                v.tensor_tensor(qsb[par2][:, :], scr1[:, :], scr2[:, :], ALU.add)\
                    .then_inc(*S.inc("qev2", 1))
                v.drain()

            rope(0)
            if NG_ALL > 1:
                rope(1)
            for qi in range(NQT):
                for h in range(NH):
                    g = qi * NH + h
                    for n, (K0, slot) in enumerate(chunks[qi]):
                        pg = pair_of[(g, n)]
                        if slot is not None:
                            v.wait_ge(S.h["pexp"], base["pexp"] + pg)
                            v.tensor_tensor(psb[pg % NPB][:, :], psb[pg % NPB][:, :],
                                            masks[:, slot * QT:(slot + 1) * QT],
                                            ALU.mult).then_inc(*S.inc("pmask", 1))
                    v.wait_ge(S.h["lr"], base["lr"] + g + 1)
                    v.wait_ge(S.h["mm"], mmpos[("o", pairs_of_g[g][-1])])
                    v.tensor_tensor(osb[:, h * QT:(h + 1) * QT], obk[:, :],
                                    Rsb2[g % 2][:, :], ALU.mult)\
                        .then_inc(*S.inc("oev", 1))
                    if g + 2 < NG_ALL:
                        rope(g + 2)
                for co in range(NCC):
                    oc = qi * NCC + co
                    v.wait_ge(S.h["mm"], mmpos[("yq", qi, co)])
                    if oc >= 2:
                        v.wait_ge(S.h[f"yst{oc % 2}"],
                                  base[f"yst{oc % 2}"] + 16 * (oc // 2))
                    v.tensor_scalar(ysb2[oc % 2][:, :], sbk[co % NPB][:, :],
                                    bo[:, co:co + 1], None, ALU.add)\
                        .then_inc(*S.inc("yev", 1))

        emit_barrier(block, env, ["qev1", "qev2", "pexp", "pmask", "lr", "oev",
                                  "yev", "yst0", "yst1", "qw0", "qw1", "ow0",
                                  "ow1", "hq0", "hq1", "tabc0", "tabc1",
                                  "tabs0", "tabs1", "mm"])


def emit_barrier(block, env, waits):
    S = env["sems"]
    vals = [(S.h[nm], S.val(nm)) for nm in waits]

    @block.sync
    def _(s):
        for h, v in vals:
            s.wait_ge(h, v)
        s.sem_inc(S.h["bar"], 1)

    S.n["bar"] += 1
    tgt = S.val("bar")
    for eng in ("tensor", "vector", "scalar", "gpsimd"):
        getattr(block, eng)(lambda e: e.wait_ge(S.h["bar"], tgt))


def build_program(sublayers=("attn_a", "ffn", "attn_b", "ffn"), h_out=False):
    nc = bass.Bass("TRN2", target_bir_lowering=False)
    ap = lambda t: t.ap()

    x_in = ap(nc.dram_tensor("x", [T, C], F32, kind="ExternalInput"))
    out = ap(nc.dram_tensor("out", [T, C], F32, kind="ExternalOutput"))

    gWT = ap(nc.dram_tensor("gWT", [NI, 128, NCC * 128], F32, kind="ExternalInput"))
    uWT = ap(nc.dram_tensor("uWT", [NI, 128, NCC * 128], F32, kind="ExternalInput"))
    dWT3 = ap(nc.dram_tensor("dWT3", [NCC, 128, NI * 128], F32, kind="ExternalInput"))
    gS_d = ap(nc.dram_tensor("gS", [128, NI], F32, kind="ExternalInput"))
    uS_d = ap(nc.dram_tensor("uS", [128, NI], F32, kind="ExternalInput"))
    dS_d = ap(nc.dram_tensor("dS", [128, NCC], F32, kind="ExternalInput"))

    aWqT = ap(nc.dram_tensor("aWqT", [NH, 128, NCC * 128], F32, kind="ExternalInput"))
    aWkT = ap(nc.dram_tensor("aWkT", [NKV_A, 128, NCC * 128], F32, kind="ExternalInput"))
    aWvT = ap(nc.dram_tensor("aWvT", [NKV_A, 128, NCC * 128], F32, kind="ExternalInput"))
    aWoT = ap(nc.dram_tensor("aWoT", [NCC, 128, NH * 128], F32, kind="ExternalInput"))
    bWqT = ap(nc.dram_tensor("bWqT", [NH, 128, NCC * 128], F32, kind="ExternalInput"))
    bWkT = ap(nc.dram_tensor("bWkT", [1, 128, NCC * 128], F32, kind="ExternalInput"))
    bWvT = ap(nc.dram_tensor("bWvT", [1, 128, NCC * 128], F32, kind="ExternalInput"))
    bWoT = ap(nc.dram_tensor("bWoT", [NCC, 128, NH * 128], F32, kind="ExternalInput"))
    bqA_d = ap(nc.dram_tensor("bqA", [128, NH], F32, kind="ExternalInput"))
    bkA_d = ap(nc.dram_tensor("bkA", [128, NKV_A], F32, kind="ExternalInput"))
    bvA_d = ap(nc.dram_tensor("bvA", [128, NKV_A], F32, kind="ExternalInput"))
    boA_d = ap(nc.dram_tensor("boA", [128, NCC], F32, kind="ExternalInput"))
    boB_d = ap(nc.dram_tensor("boB", [128, NCC], F32, kind="ExternalInput"))
    cosA_d = ap(nc.dram_tensor("cosA", [128, T], F32, kind="ExternalInput"))
    sinA_d = ap(nc.dram_tensor("sinA", [128, T], F32, kind="ExternalInput"))
    cosB_d = ap(nc.dram_tensor("cosB", [128, T], F32, kind="ExternalInput"))
    sinB_d = ap(nc.dram_tensor("sinB", [128, T], F32, kind="ExternalInput"))
    RaI = ap(nc.dram_tensor("RaI", [128, 128], F32, kind="ExternalInput"))
    RbI = ap(nc.dram_tensor("RbI", [128, 128], F32, kind="ExternalInput"))
    masksI = ap(nc.dram_tensor("masksI", [128, 8 * QT], F32, kind="ExternalInput"))
    identI = ap(nc.dram_tensor("identI", [128, 128], F32, kind="ExternalInput"))
    onesI = ap(nc.dram_tensor("onesI", [128, 128], F32, kind="ExternalInput"))

    xs_a = ap(nc.dram_tensor("xs_a", [T, C], F32))
    xs_b = ap(nc.dram_tensor("xs_b", [T, C], F32))
    h_sc = ap(nc.dram_tensor("h_sc", [T, C], BF16,
                             kind="ExternalOutput" if h_out else "Internal"))
    y_sc = ap(nc.dram_tensor("y_sc", [C, T], BF16))

    from contextlib import ExitStack
    with ExitStack() as stack:
        S = Sems(nc, stack, SEM_NAMES)
        sb = lambda name, shape, dt: stack.enter_context(nc.sbuf_tensor(name, shape, dt))
        B = {}
        B["xt"] = [sb(f"xt{i}", [128, C], F32) for i in range(2)]
        B["yt"] = [sb(f"yt{i}", [128, C], BF16) for i in range(2)]
        B["ht"] = [sb(f"ht{i}", [128, C], BF16) for i in range(2)]
        B["st8"] = sb("st8", [128, 16], F32)
        B["bn6"] = sb("bn6", [128, 48], F32)
        B["sqs"] = sb("sqs", [128, 16], F32)
        B["gSs"] = sb("gSs", [128, NI], F32)
        B["uSs"] = sb("uSs", [128, NI], F32)
        B["dSs"] = sb("dSs", [128, NCC], F32)
        B["masks"] = sb("masks", [128, 8 * QT], BF16)
        B["ident"] = sb("ident", [128, 128], BF16)
        B["ones"] = sb("ones", [128, 128], BF16)
        B["Ra"] = sb("Ra", [128, 128], BF16)
        B["Rb"] = sb("Rb", [128, 128], BF16)
        B["bqA"] = sb("bqA_s", [128, NH], F32)
        B["bkA"] = sb("bkA_s", [128, NKV_A], F32)
        B["bvA"] = sb("bvA_s", [128, NKV_A], F32)
        B["boA"] = sb("boA_s", [128, NCC], F32)
        B["boB"] = sb("boB_s", [128, NCC], F32)
        B["shiftc"] = sb("shiftc", [128, 1], F32)

        env = {"nc": nc, "sems": S, "bufs": B}

        with nc.Block() as block:
            @block.gpsimd
            def _(g):
                for dst, srcd in [(B["gSs"], gS_d), (B["uSs"], uS_d), (B["dSs"], dS_d),
                                  (B["masks"], masksI), (B["ident"], identI),
                                  (B["ones"], onesI), (B["Ra"], RaI),
                                  (B["Rb"], RbI), (B["bqA"], bqA_d), (B["bkA"], bkA_d),
                                  (B["bvA"], bvA_d), (B["boA"], boA_d),
                                  (B["boB"], boB_d)]:
                    g.dma_start(out=dst[:, :], in_=srcd[:, :]).then_inc(*S.inc("aux", 16))
                g.memset(B["shiftc"][:, :], -20.0).then_inc(*S.inc("aux2", 1))

            auxtot = S.val("aux")
            aux2tot = S.val("aux2")
            for eng in ("tensor", "vector", "scalar"):
                def _w(e):
                    e.wait_ge(S.h["aux"], auxtot)
                    e.wait_ge(S.h["aux2"], aux2tot)
                getattr(block, eng)(_w)

            x_cur = x_in
            x_scr = [xs_a, xs_b]
            y_prev = None
            for k, sl in enumerate(sublayers):
                x_next = x_scr[k % 2]
                emit_ln_phase(block, env, x_cur, x_next, y_prev, True, h_sc)
                emit_barrier(block, env, ["xld0", "xld1", "yld0", "yld1", "xst0",
                                          "xst1", "hst0", "hst1", "vln"])

                if sl == "ffn":
                    with (
                        nc.sbuf_tensor(f"hsb_{k}", [128, NCC * GROUP], BF16) as hsb,
                        nc.sbuf_tensor(f"zsb_{k}", [128, NI * GROUP], BF16) as zsb,
                        nc.sbuf_tensor(f"gw0_{k}", [128, C], BF16) as gw0,
                        nc.sbuf_tensor(f"gw1_{k}", [128, C], BF16) as gw1,
                        nc.sbuf_tensor(f"uw0_{k}", [128, C], BF16) as uw0,
                        nc.sbuf_tensor(f"uw1_{k}", [128, C], BF16) as uw1,
                        nc.sbuf_tensor(f"dw0_{k}", [128, INTER], BF16) as dw0,
                        nc.sbuf_tensor(f"dw1_{k}", [128, INTER], BF16) as dw1,
                        nc.sbuf_tensor(f"silu0_{k}", [128, QT], BF16) as silu0,
                        nc.sbuf_tensor(f"silu1_{k}", [128, QT], BF16) as silu1,
                        nc.sbuf_tensor(f"ysb0_{k}", [128, QT], BF16) as ysb0,
                        nc.sbuf_tensor(f"ysb1_{k}", [128, QT], BF16) as ysb1,
                        nc.psum_tensor(f"pg0_{k}", [128, QT], F32) as pg0,
                        nc.psum_tensor(f"pg1_{k}", [128, QT], F32) as pg1,
                        nc.psum_tensor(f"pu0_{k}", [128, QT], F32) as pu0,
                        nc.psum_tensor(f"pu1_{k}", [128, QT], F32) as pu1,
                        nc.psum_tensor(f"py0_{k}", [128, QT], F32) as py0,
                        nc.psum_tensor(f"py1_{k}", [128, QT], F32) as py1,
                    ):
                        B.update(hsb=hsb, zsb=zsb, gw=[gw0, gw1], uw=[uw0, uw1],
                                 dw=[dw0, dw1], silu=[silu0, silu1],
                                 ysb=[ysb0, ysb1], pg=[pg0, pg1], pu=[pu0, pu1],
                                 py=[py0, py1])
                        w = {"gWT": gWT, "uWT": uWT, "dWT3": dWT3,
                             "gS": B["gSs"], "uS": B["uSs"], "dS": B["dSs"]}
                        emit_ffn_phase(block, env, h_sc, y_sc, w)
                        emit_barrier(block, env,
                                     ["hld", "gld0", "gld1", "uld0", "uld1",
                                      "dwld0", "dwld1", "yst0", "yst1", "yev",
                                      "zst", "act", "mm"])
                elif sl in ("attn_a", "attn_b"):
                    nkv = NKV_A if sl == "attn_a" else 1
                    with (
                        nc.sbuf_tensor(f"ksb{k}", [128, nkv * T], BF16) as ksb,
                        nc.sbuf_tensor(f"vsb{k}", [128, nkv * T], BF16) as vsb,
                        nc.sbuf_tensor(f"hqA{k}", [128, NCC * QT], BF16) as hqA,
                        nc.sbuf_tensor(f"hqB{k}", [128, NCC * QT], BF16) as hqB,
                        nc.sbuf_tensor(f"tabcA{k}", [128, QT], BF16) as tabcA,
                        nc.sbuf_tensor(f"tabcB{k}", [128, QT], BF16) as tabcB,
                        nc.sbuf_tensor(f"tabsA{k}", [128, QT], BF16) as tabsA,
                        nc.sbuf_tensor(f"tabsB{k}", [128, QT], BF16) as tabsB,
                        nc.sbuf_tensor(f"mk2_{k}", [128, 8], F32) as mk2,
                    ):
                        B.update(ksb=ksb, vsb=vsb, hq=[hqA, hqB],
                                 tabc=[tabcA, tabcB], tabs=[tabsA, tabsB], mk2=mk2)
                        if sl == "attn_a":
                            w = {"WqT": aWqT, "WkT": aWkT, "WvT": aWvT,
                                 "WoT": aWoT, "cosA": cosA_d, "sinA": sinA_d,
                                 "cosB": cosB_d, "sinB": sinB_d}
                        else:
                            w = {"WqT": bWqT, "WkT": bWkT, "WvT": bWvT,
                                 "WoT": bWoT, "cosA": cosA_d, "sinA": sinA_d,
                                 "cosB": cosB_d, "sinB": sinB_d}
                        env["sfx"] = f"_{k}"
                        emit_attn_phase(block, env, h_sc, y_sc, w,
                                        "a" if sl == "attn_a" else "b")
                elif sl == "ln":
                    pass
                else:
                    raise NotImplementedError(sl)

                x_cur = x_next
                if sl != "ln":
                    y_prev = y_sc

            emit_ln_phase(block, env, x_cur, out, y_prev, False, None)

            @block.sync
            def _(s):
                s.wait_ge(S.h["xst0"], S.val("xst0"))
                s.wait_ge(S.h["xst1"], S.val("xst1"))

    return nc


def prep_ffn_inputs(gate_w, up_w, down_w, gate_s, up_s, down_s):
    d = {}
    d["gWT"] = np.ascontiguousarray(
        np.asarray(gate_w).reshape(NI, 128, NCC, 128).transpose(0, 3, 2, 1)
        .reshape(NI, 128, NCC * 128))
    d["uWT"] = np.ascontiguousarray(
        np.asarray(up_w).reshape(NI, 128, NCC, 128).transpose(0, 3, 2, 1)
        .reshape(NI, 128, NCC * 128))
    d["dWT3"] = np.ascontiguousarray(
        np.asarray(down_w).reshape(NCC, 128, NI, 128).transpose(0, 3, 2, 1)
        .reshape(NCC, 128, NI * 128))
    d["gS"] = np.ascontiguousarray(np.asarray(gate_s).reshape(NI, 128).T)
    d["uS"] = np.ascontiguousarray(np.asarray(up_s).reshape(NI, 128).T)
    d["dS"] = np.ascontiguousarray(np.asarray(down_s).reshape(NCC, 128).T)
    return d




def prep_attn_inputs(inputs):
    d = {}

    def tile_in(Wname, nheads):
        Wm = np.asarray(inputs[Wname])
        return np.ascontiguousarray(
            Wm.reshape(NCC, 128, nheads, 128).transpose(2, 1, 0, 3)
            .reshape(nheads, 128, NCC * 128))

    def tile_out(Wname):
        Wm = np.asarray(inputs[Wname])
        return np.ascontiguousarray(
            Wm.reshape(NH, 128, NCC, 128).transpose(2, 1, 0, 3)
            .reshape(NCC, 128, NH * 128))

    d["aWqT"] = tile_in("L0_attn_a_Wq", NH)
    d["aWkT"] = tile_in("L0_attn_a_Wk", NKV_A)
    d["aWvT"] = tile_in("L0_attn_a_Wv", NKV_A)
    d["aWoT"] = tile_out("L0_attn_a_Wo")
    d["bWqT"] = tile_in("L0_attn_b_Wq", NH)
    d["bWkT"] = tile_in("L0_attn_b_Wk", 1)
    d["bWvT"] = tile_in("L0_attn_b_Wv", 1)
    d["bWoT"] = tile_out("L0_attn_b_Wo")

    d["bqA"] = np.ascontiguousarray(np.asarray(inputs["L0_attn_a_bq"]).reshape(NH, 128).T)
    d["bkA"] = np.ascontiguousarray(np.asarray(inputs["L0_attn_a_bk"]).reshape(NKV_A, 128).T)
    d["bvA"] = np.ascontiguousarray(np.asarray(inputs["L0_attn_a_bv"]).reshape(NKV_A, 128).T)
    d["boA"] = np.ascontiguousarray(np.asarray(inputs["L0_attn_a_bo"]).reshape(NCC, 128).T)
    d["boB"] = np.ascontiguousarray(np.asarray(inputs["L0_attn_b_bo"]).reshape(NCC, 128).T)

    t = np.arange(T, dtype=np.float64)
    th_a = 10000.0 ** (-np.arange(0, HD, 2, dtype=np.float64) / HD)
    ang_a = np.outer(th_a, t)
    cosA = np.empty((128, T), np.float32)
    sinA = np.empty((128, T), np.float32)
    cosA[0::2] = np.cos(ang_a); cosA[1::2] = np.cos(ang_a)
    sinA[0::2] = -np.sin(ang_a); sinA[1::2] = np.sin(ang_a)
    cosA *= 0.5
    sinA *= 0.5
    d["cosA"], d["sinA"] = np.ascontiguousarray(cosA), np.ascontiguousarray(sinA)
    th_b = np.exp(-np.log(10000.0) * (np.arange(0, 64, 2, dtype=np.float64) / 64.0))
    ang_b = np.outer(th_b, t)
    cosB = np.ones((128, T), np.float32)
    sinB = np.zeros((128, T), np.float32)
    cosB[0:32] = np.cos(ang_b); cosB[32:64] = np.cos(ang_b)
    sinB[0:32] = -np.sin(ang_b); sinB[32:64] = np.sin(ang_b)
    cosB *= 0.25
    sinB *= 0.25
    d["cosB"], d["sinB"] = np.ascontiguousarray(cosB), np.ascontiguousarray(sinB)

    Ra = np.zeros((128, 128), np.float32)
    for i in range(64):
        Ra[2 * i, 2 * i + 1] = 1.0
        Ra[2 * i + 1, 2 * i] = 1.0
    Rb = np.zeros((128, 128), np.float32)
    for dd in range(32):
        Rb[dd, dd + 32] = 1.0
        Rb[dd + 32, dd] = 1.0
    d["RaI"], d["RbI"] = Ra, Rb
    d["identI"] = np.eye(128, dtype=np.float32)
    d["onesI"] = np.ones((128, 128), np.float32)

    kk = np.arange(128)[:, None]
    tt = np.arange(QT)[None, :]
    masks = np.zeros((128, 8 * QT), np.float32)
    for m in range(4):
        dd = m * 128
        masks[:, m * QT:(m + 1) * QT] = (kk + dd <= tt).astype(np.float32)
    for m in range(4):
        dd = -WINDOW + m * 128
        masks[:, (4 + m) * QT:(5 + m) * QT] = ((tt - kk) <= dd + WINDOW - 1).astype(np.float32)
    d["masksI"] = masks
    return d


SUBLAYERS = ("attn_a", "ffn", "attn_b", "ffn")


def make_in_maps(inputs):
    x = np.asarray(inputs["L0_attn_a_x"])
    wd = prep_ffn_inputs(inputs["L0_ffn_gate_w"], inputs["L0_ffn_up_w"],
                         inputs["L0_ffn_down_w"], inputs["L0_ffn_gate_s"],
                         inputs["L0_ffn_up_s"], inputs["L0_ffn_down_s"])
    wd.update(prep_attn_inputs(inputs))
    return [dict(wd, x=np.ascontiguousarray(x[b])) for b in range(8)]


def kernel(**inputs):
    nc = build_program(SUBLAYERS)
    in_maps = make_in_maps(inputs)
    res = run_bass_kernel_spmd(nc, in_maps, core_ids=list(range(8)))
    return np.stack([r["out"] for r in res.results], axis=0)


# version 05 (final)
# speedup vs baseline: 1.7093x; 1.0283x over previous
import numpy as np
import ml_dtypes
import concourse.bass as bass
import concourse.mybir as mybir
from concourse.bass_utils import run_bass_kernel_spmd

F32 = mybir.dt.float32
BF16 = mybir.dt.bfloat16
AF = mybir.ActivationFunctionType
ALU = mybir.AluOpType
AX = mybir.AxisListType
bfdt = ml_dtypes.bfloat16

T = 4096
C = 2048
HD = 128
NH = 16
NKV_A = 4
INTER = 5632
NI = INTER // 128
NCC = C // 128
QT = 512
NQT = T // QT
GROUP = 1024
NG = T // GROUP
TPG = GROUP // QT
NTT = 32
EPS = 1e-5
WINDOW = 512


class Sems:
    def __init__(self, nc, stack, names):
        self.h = {}
        self.n = {}
        for nm in names:
            self.h[nm] = stack.enter_context(nc.semaphore(nm))
            self.n[nm] = 0

    def inc(self, nm, k):
        self.n[nm] += k
        return self.h[nm], k

    def val(self, nm):
        return self.n[nm]


SEM_NAMES = [
    "xld0", "xld1", "yld0", "yld1", "hld", "hst0", "hst1", "xst",
    "yst0", "yst1", "xst0", "xst1",
    "vln", "sln", "happ",
    "gld0", "gld1", "uld0", "uld1", "dwld0", "dwld1",
    "mm", "act", "zst", "ymm", "yev",
    "bar", "aux", "aux2",
    "kw0", "kw1", "vw0", "vw1", "tabc0", "tabc1", "tabs0", "tabs1",
    "hq0", "hq1", "qw0", "qw1", "ow0", "ow1",
    "kev1", "kev2", "kev3", "kmax", "vev1", "vev2",
    "qev1", "qev2", "qev3", "cr", "cev", "varg", "pexp", "pmask", "lr", "oev",
]


def emit_ln_phase(block, env, x_src, x_dst, y_src, do_ln, h_dst, y_gate=None):
    S, B = env["sems"], env["bufs"]
    xt, yt, ht = B["xt"], B["yt"], B["ht"]
    st8, bn6, sqs = B["st8"], B["bn6"], B["sqs"]
    have_y = y_src is not None
    b = {nm: S.val(nm) for nm in ("xld0", "xld1", "yld0", "yld1", "vln", "sln",
                                  "happ", "xst0", "xst1", "hst0", "hst1")}

    @block.sync
    def _(s):
        for i in range(NTT + 2):
            if 2 <= i:
                j = i - 2
                s.wait_ge(S.h["vln"], b["vln"] + j + 1)
                s.dma_start(out=x_dst[j * 128:(j + 1) * 128, :], in_=xt[j % 2][:, :])\
                    .then_inc(*S.inc(f"xst{j % 2}", 16))
                if do_ln:
                    s.dma_start(out=h_dst[j * 128:(j + 1) * 128, :], in_=ht[j % 2][:, :])\
                        .then_inc(*S.inc(f"hst{j % 2}", 16))
            if i < NTT:
                s.wait_ge(S.h[f"xst{i % 2}"], b[f"xst{i % 2}"] + 16 * (i // 2))
                s.dma_start(out=xt[i % 2][:, :], in_=x_src[i * 128:(i + 1) * 128, :])\
                    .then_inc(*S.inc(f"xld{i % 2}", 16))
                if have_y:
                    s.wait_ge(S.h["vln"], b["vln"] + max(0, i - 1))
                    if y_gate is not None:
                        for gh, gv in y_gate(i):
                            s.wait_ge(gh, gv)
                    s.dma_start_transpose(yt[i % 2][:, :], y_src[:, i * 128:(i + 1) * 128])\
                        .then_inc(*S.inc(f"yld{i % 2}", 16))

    @block.vector
    def _(v):
        for i in range(NTT):
            p = i % 2
            v.wait_ge(S.h[f"xld{p}"], b[f"xld{p}"] + 16 * (i // 2 + 1))
            if have_y:
                v.wait_ge(S.h[f"yld{p}"], b[f"yld{p}"] + 16 * (i // 2 + 1))
            if do_ln:
                if have_y:
                    v.tensor_tensor(xt[p][:, :], xt[p][:, :], yt[p][:, :], ALU.add)
                    v.drain()
                if i >= 2:
                    v.wait_ge(S.h["happ"], b["happ"] + i - 1)
                for g in range(4):
                    v.bn_stats(bn6[:, p * 24 + 6 * g: p * 24 + 6 * g + 6],
                               xt[p][:, 512 * g:512 * (g + 1)])
                v.drain()
                b6 = bn6[:, p * 24: p * 24 + 24].rearrange("a (g s) -> a g s", s=3)
                col = st8[:, p * 8: p * 8 + 8]
                v.tensor_reduce(col[:, 0:1], b6[:, :, 1], AX.X, ALU.add)
                v.tensor_reduce(col[:, 1:2], b6[:, :, 2], AX.X, ALU.add)
                v.scalar_tensor_tensor(sqs[:, p * 8: p * 8 + 8],
                                       b6[:, :, 1], 1.0, b6[:, :, 1],
                                       ALU.bypass, ALU.mult,
                                       accum_out=col[:, 2:3])
                v.drain()
                v.tensor_scalar(col[:, 4:5], col[:, 0:1], 1.0 / 8, None, ALU.mult)
                v.tensor_scalar(col[:, 6:7], col[:, 2:3], 1.0 / 8, None, ALU.mult)
                v.drain()
                v.tensor_scalar(col[:, 5:6], col[:, 4:5], -1.0, None, ALU.mult)
                v.scalar_tensor_tensor(col[:, 6:7], col[:, 1:2], 1.0 / 2048,
                                       col[:, 6:7], ALU.mult, ALU.add)
                v.drain()
                v.scalar_tensor_tensor(col[:, 6:7], col[:, 4:5], col[:, 5:6],
                                       col[:, 6:7], ALU.mult, ALU.add)
                v.drain()
                v.tensor_single_scalar(col[:, 6:7], col[:, 6:7], EPS, ALU.add)\
                    .then_inc(*S.inc("sln", 1))
                v.wait_ge(S.h["happ"], b["happ"] + i + 1)
                v.wait_ge(S.h[f"hst{p}"], b[f"hst{p}"] + 16 * (i // 2))
                v.tensor_scalar(ht[p][:, :], xt[p][:, :], col[:, 4:5], col[:, 7:8],
                                ALU.subtract, ALU.mult).then_inc(*S.inc("vln", 1))
            elif have_y:
                v.tensor_tensor(xt[p][:, :], xt[p][:, :], yt[p][:, :], ALU.add)\
                    .then_inc(*S.inc("vln", 1))
            else:
                v.tensor_single_scalar(xt[p][:, :], xt[p][:, :], 0.0, ALU.add)\
                    .then_inc(*S.inc("vln", 1))

    if do_ln:
        @block.scalar
        def _(sc):
            for i in range(NTT):
                p = i % 2
                col = st8[:, p * 8: p * 8 + 8]
                sc.wait_ge(S.h["sln"], b["sln"] + i + 1)
                sc.activation(col[:, 7:8], col[:, 6:7], AF.Ln)
                sc.drain()
                sc.activation(col[:, 7:8], col[:, 7:8], AF.Exp, scale=-0.5)\
                    .then_inc(*S.inc("happ", 1))


def emit_ffn_phase(block, env, h_src, y_dst, w, h_gate=None):
    S, B = env["sems"], env["bufs"]
    hsb, zsb = B["hsb"], B["zsb"]
    gw, uw, dw = B["gw"], B["uw"], B["dw"]
    silu, ysb = B["silu"], B["ysb"]
    pg, pu, py = B["pg"], B["pu"], B["py"]
    gS, uS, dS = w["gS"], w["uS"], w["dS"]

    b = {nm: S.val(nm) for nm in ("hld", "gld0", "gld1", "uld0", "uld1",
                                  "dwld0", "dwld1", "mm", "act", "zst",
                                  "ymm", "yev", "yst0", "yst1")}

    gWTr = w["gWT"]
    uWTr = w["uWT"]
    dWT3r = w["dWT3"]

    mm_g, mm_u, mm_y = {}, {}, {}
    mmc = 0
    for grp in range(NG):
        for i in range(NI):
            for tt in range(TPG):
                mmc += 1
                mm_g[(grp, i, tt)] = mmc
            for tt in range(TPG):
                mmc += 1
                mm_u[(grp, i, tt)] = mmc
        for co in range(NCC):
            for tt in range(TPG):
                mmc += 1
                mm_y[(grp, co, tt)] = mmc

    def chunk_gi(n):
        return divmod(n, NI)

    @block.gpsimd
    def _(g):
        for grp in range(NG):
            for i in range(NI):
                n = grp * NI + i
                par = n % 2
                if n >= 2:
                    pg_, pi_ = chunk_gi(n - 2)
                    g.wait_ge(S.h["mm"], b["mm"] + mm_g[(pg_, pi_, TPG - 1)])
                g.dma_start(out=gw[par][:, :], in_=gWTr[i, :, :])\
                    .then_inc(*S.inc(f"gld{par}", 16))
                if n >= 2:
                    pg_, pi_ = chunk_gi(n - 2)
                    g.wait_ge(S.h["mm"], b["mm"] + mm_u[(pg_, pi_, TPG - 1)])
                g.dma_start(out=uw[par][:, :], in_=uWTr[i, :, :])\
                    .then_inc(*S.inc(f"uld{par}", 16))
            for co in range(NCC):
                n = grp * NCC + co
                if n >= 2:
                    pg_, pco_ = divmod(n - 2, NCC)
                    g.wait_ge(S.h["mm"], b["mm"] + mm_y[(pg_, pco_, TPG - 1)])
                g.dma_start(out=dw[n % 2][:, :], in_=dWT3r[co, :, :])\
                    .then_inc(*S.inc(f"dwld{n % 2}", 16))

    @block.sync
    def _(s):
        for grp in range(NG):
            if grp >= 1:
                s.wait_ge(S.h["mm"], b["mm"] + mm_u[(grp - 1, NI - 1, TPG - 1)])
            for tt in range(TPG):
                if h_gate is not None:
                    for gh, gv in h_gate(grp * TPG + tt):
                        s.wait_ge(gh, gv)
                for cc in range(NCC):
                    t0 = grp * GROUP + tt * QT
                    s.dma_start_transpose(
                        hsb[:, cc * GROUP + tt * QT: cc * GROUP + (tt + 1) * QT],
                        h_src[t0:t0 + QT, cc * 128:(cc + 1) * 128],
                    ).then_inc(*S.inc("hld", 16))

    @block.tensor
    def _(t):
        for grp in range(NG):
            t.wait_ge(S.h["hld"], b["hld"] + 16 * NCC * TPG * (grp + 1))
            for i in range(NI):
                n = grp * NI + i
                par = n % 2
                t.wait_ge(S.h[f"gld{par}"], b[f"gld{par}"] + 16 * (n // 2 + 1))
                for tt in range(TPG):
                    gi = n * TPG + tt
                    if n >= 1:
                        t.wait_ge(S.h["act"], b["act"] + gi - TPG + 1)
                    for cc in range(NCC):
                        ins = t.matmul(pg[tt][:, :], gw[par][:, cc * 128:(cc + 1) * 128],
                                       hsb[:, cc * GROUP + tt * QT: cc * GROUP + (tt + 1) * QT],
                                       start=(cc == 0), stop=(cc == NCC - 1))
                    ins.then_inc(*S.inc("mm", 1))
                t.wait_ge(S.h[f"uld{par}"], b[f"uld{par}"] + 16 * (n // 2 + 1))
                for tt in range(TPG):
                    gi = n * TPG + tt
                    if n >= 1:
                        t.wait_ge(S.h["zst"], b["zst"] + gi - TPG + 1)
                    for cc in range(NCC):
                        ins = t.matmul(pu[tt][:, :], uw[par][:, cc * 128:(cc + 1) * 128],
                                       hsb[:, cc * GROUP + tt * QT: cc * GROUP + (tt + 1) * QT],
                                       start=(cc == 0), stop=(cc == NCC - 1))
                    ins.then_inc(*S.inc("mm", 1))
            t.wait_ge(S.h["zst"], b["zst"] + (grp + 1) * NI * TPG)
            for co in range(NCC):
                n = grp * NCC + co
                t.wait_ge(S.h[f"dwld{n % 2}"], b[f"dwld{n % 2}"] + 16 * (n // 2 + 1))
                for tt in range(TPG):
                    yi = n * TPG + tt
                    if yi >= 2:
                        t.wait_ge(S.h["yev"], b["yev"] + yi - 1)
                    for ich in range(NI):
                        ins = t.matmul(py[yi % 2][:, :], dw[n % 2][:, ich * 128:(ich + 1) * 128],
                                       zsb[:, ich * GROUP + tt * QT: ich * GROUP + (tt + 1) * QT],
                                       start=(ich == 0), stop=(ich == NI - 1))
                    ins.then_inc(*S.inc("mm", 1))
                    t.sem_inc(*S.inc("ymm", 1))

    @block.scalar
    def _(sc):
        for grp in range(NG):
            for i in range(NI):
                for tt in range(TPG):
                    gi = (grp * NI + i) * TPG + tt
                    sc.wait_ge(S.h["mm"], b["mm"] + mm_g[(grp, i, tt)])
                    if gi >= 2:
                        sc.wait_ge(S.h["zst"], b["zst"] + gi - 1)
                    sc.activation(silu[gi % 2][:, :], pg[tt][:, :], AF.Silu,
                                  scale=gS[:, i:i + 1]).then_inc(*S.inc("act", 1))
            for co in range(NCC):
                for tt in range(TPG):
                    yi = (grp * NCC + co) * TPG + tt
                    t0 = grp * GROUP + tt * QT
                    sc.wait_ge(S.h["yev"], b["yev"] + yi + 1)
                    sc.dma_start(out=y_dst[co * 128:(co + 1) * 128, t0:t0 + QT],
                                 in_=ysb[yi % 2][:, :]).then_inc(*S.inc(f"yst{yi % 2}", 16))

    @block.vector
    def _(v):
        for grp in range(NG):
            for i in range(NI):
                for tt in range(TPG):
                    gi = (grp * NI + i) * TPG + tt
                    v.wait_ge(S.h["mm"], b["mm"] + mm_u[(grp, i, tt)])
                    v.wait_ge(S.h["act"], b["act"] + gi + 1)
                    if grp >= 1 and i == 0 and tt == 0:
                        v.wait_ge(S.h["mm"], b["mm"] + mm_y[(grp - 1, NCC - 1, TPG - 1)])
                    v.scalar_tensor_tensor(
                        zsb[:, i * GROUP + tt * QT: i * GROUP + (tt + 1) * QT],
                        pu[tt][:, :], uS[:, i:i + 1], silu[gi % 2][:, :],
                        ALU.mult, ALU.mult).then_inc(*S.inc("zst", 1))
            for co in range(NCC):
                for tt in range(TPG):
                    yi = (grp * NCC + co) * TPG + tt
                    v.wait_ge(S.h["mm"], b["mm"] + mm_y[(grp, co, tt)])
                    if yi >= 2:
                        v.wait_ge(S.h[f"yst{yi % 2}"], b[f"yst{yi % 2}"] + 16 * (yi // 2))
                    v.tensor_scalar(ysb[yi % 2][:, :], py[yi % 2][:, :],
                                    dS[:, co:co + 1], None, ALU.mult)\
                        .then_inc(*S.inc("yev", 1))




def attn_chunks(kind, qi):
    Q0 = qi * QT
    lo = 0 if kind == "a" else max(0, Q0 - WINDOW)
    out = []
    for K0 in range(lo, Q0 + QT - 128 + 1, 128):
        d = K0 - Q0
        if d >= 0:
            slot = d // 128
        elif kind == "b":
            slot = 4 + (d + WINDOW) // 128
        else:
            slot = None
        out.append((K0, slot))
    return out


def emit_attn_phase(block, env, h_src, y_dst, w, kind, h_gate=None):
    sfx = env.get("sfx", "")
    S, B = env["sems"], env["bufs"]
    nkv = NKV_A if kind == "a" else 1
    r_imm = 1.0 if kind == "a" else float(np.sqrt(2.0))
    qk_scale = 0.5 if kind == "a" else 0.25

    masks = B["masks"]
    ident, ones = B["ident"], B["ones"]
    shiftc = B["shiftc"]
    R_sb = B["Ra"] if kind == "a" else B["Rb"]
    bq = B.get("bqA") if kind == "a" else None
    bk = B.get("bkA") if kind == "a" else None
    bv = B.get("bvA") if kind == "a" else None
    bo = B["boA"] if kind == "a" else B["boB"]
    cosT, sinT = (w["cosA"], w["sinA"]) if kind == "a" else (w["cosB"], w["sinB"])
    WqT, WkT, WvT, WoT = w["WqT"], w["WkT"], w["WvT"], w["WoT"]

    hq = B["hq"]
    ksb, vsb = B["ksb"], B["vsb"]
    tabc, tabs = B["tabc"], B["tabs"]

    base = {nm: S.val(nm) for nm in (
        "mm", "kw0", "vw0", "tabc0", "tabc1", "tabs0", "tabs1",
        "hq0", "hq1", "qw0", "qw1", "ow0", "ow1",
        "kev1", "kev2", "vev1", "vev2",
        "qev1", "qev2", "pexp", "pmask", "lr",
        "oev", "yev", "yst0", "yst1")}

    chunks = {qi: attn_chunks(kind, qi) for qi in range(NQT)}
    pair_of = {}
    pairs_of_g = {}
    pmask_idx = {}
    pgc = 0
    pmc = 0
    for qi in range(NQT):
        for h in range(NH):
            g = qi * NH + h
            lst = []
            for n, (K0, slot) in enumerate(chunks[qi]):
                pgc += 1
                pair_of[(g, n)] = pgc
                lst.append(pgc)
                if slot is not None:
                    pmc += 1
                    pmask_idx[pgc] = pmc
            pairs_of_g[g] = lst

    mmpos = {}

    from contextlib import ExitStack

    with ExitStack() as _st:
        _sb = lambda n, sh, dt: _st.enter_context(env["nc"].sbuf_tensor(n + sfx, sh, dt))
        _ps = lambda n, sh, dt: _st.enter_context(env["nc"].psum_tensor(n + sfx, sh, dt))
        kwb = [_sb(f"kwb{j}", [128, NCC * 128], BF16) for j in range(nkv)]
        vwb = [_sb(f"vwb{j}", [128, NCC * 128], BF16) for j in range(nkv)]
        kscb = _sb("kscb", [128, QT], BF16)
        vscb = _sb("vscb", [128, QT], BF16)
        scr1 = _sb("scr1", [128, QT], BF16)
        scr2 = _sb("scr2", [128, QT], BF16)
        kraw = _ps("kraw", [128, QT], F32)
        rotb = _ps("rotb", [128, QT], F32)
        vraw = _ps("vraw", [128, QT], F32)
        vtb = _ps("vtb", [128, QT], BF16)

        def kv_idx(tt8, j):
            return tt8 * nkv + j + 1

        @block.tensor
        def _(t):
            for tt8 in range(8):
                for j in range(nkv):
                    idx = kv_idx(tt8, j)
                    t.wait_ge(S.h["kw0"], base["kw0"] + 16 * nkv)
                    t.wait_ge(S.h[f"hq{tt8 % 2}"],
                              base[f"hq{tt8 % 2}"] + 16 * NCC * (tt8 // 2 + 1))
                    if idx >= 2:
                        t.wait_ge(S.h["kev1"], base["kev1"] + idx - 1)
                    for cc in range(NCC):
                        ins = t.matmul(kraw[:, :], kwb[j][:, cc * 128:(cc + 1) * 128],
                                       hq[tt8 % 2][:, cc * QT:(cc + 1) * QT],
                                       start=(cc == 0), stop=(cc == NCC - 1))
                    ins.then_inc(*S.inc("mm", 1))
                    mmpos[("kraw", idx)] = S.val("mm")
                    t.wait_ge(S.h["kev1"], base["kev1"] + idx)
                    if idx >= 2:
                        t.wait_ge(S.h["kev2"], base["kev2"] + idx - 1)
                    ins = t.matmul(rotb[:, :], R_sb[:, :], kscb[:, :],
                                   start=True, stop=True)
                    ins.then_inc(*S.inc("mm", 1))
                    mmpos[("rot", idx)] = S.val("mm")
                    t.wait_ge(S.h["vw0"], base["vw0"] + 16 * nkv)
                    if idx >= 2:
                        t.wait_ge(S.h["vev1"], base["vev1"] + idx - 1)
                    for cc in range(NCC):
                        ins = t.matmul(vraw[:, :], vwb[j][:, cc * 128:(cc + 1) * 128],
                                       hq[tt8 % 2][:, cc * QT:(cc + 1) * QT],
                                       start=(cc == 0), stop=(cc == NCC - 1))
                    ins.then_inc(*S.inc("mm", 1))
                    mmpos[("vraw", idx)] = S.val("mm")
                    t.wait_ge(S.h["vev1"], base["vev1"] + idx)
                    if idx >= 2:
                        t.wait_ge(S.h["vev2"], base["vev2"] + idx - 1)
                    for s_ in range(4):
                        ins = t.transpose(vtb[:, s_ * 128:(s_ + 1) * 128],
                                          vscb[:, s_ * 128:(s_ + 1) * 128],
                                          ident[:, :])
                    ins.then_inc(*S.inc("mm", 1))
                    mmpos[("vt", idx)] = S.val("mm")

        @block.gpsimd
        def _(g):
            for j in range(nkv):
                g.dma_start(out=kwb[j][:, :], in_=WkT[j, :, :])\
                    .then_inc(*S.inc("kw0", 16))
                g.dma_start(out=vwb[j][:, :], in_=WvT[j, :, :])\
                    .then_inc(*S.inc("vw0", 16))
            for tt8 in range(8):
                par = tt8 % 2
                if tt8 >= 2:
                    g.wait_ge(S.h["kev2"], base["kev2"] + kv_idx(tt8 - 2, nkv - 1))
                g.dma_start(out=tabc[par][:, :], in_=cosT[:, tt8 * QT:(tt8 + 1) * QT])\
                    .then_inc(*S.inc(f"tabc{par}", 16))
                g.dma_start(out=tabs[par][:, :], in_=sinT[:, tt8 * QT:(tt8 + 1) * QT])\
                    .then_inc(*S.inc(f"tabs{par}", 16))

        @block.sync
        def _(s):
            for tt8 in range(8):
                par = tt8 % 2
                if tt8 >= 2:
                    s.wait_ge(S.h["mm"], mmpos[("vraw", kv_idx(tt8 - 2, nkv - 1))])
                if h_gate is not None:
                    for gh, gv in h_gate(tt8):
                        s.wait_ge(gh, gv)
                for cc in range(NCC):
                    s.dma_start_transpose(
                        hq[par][:, cc * QT:(cc + 1) * QT],
                        h_src[tt8 * QT:(tt8 + 1) * QT, cc * 128:(cc + 1) * 128],
                    ).then_inc(*S.inc(f"hq{par}", 16))


        @block.vector
        def _(v):
            for tt8 in range(8):
                par = tt8 % 2
                for j in range(nkv):
                    idx = kv_idx(tt8, j)
                    v.wait_ge(S.h["mm"], mmpos[("kraw", idx)])
                    if idx >= 2:
                        v.wait_ge(S.h["mm"], mmpos[("rot", idx - 1)])
                    if kind == "a":
                        v.tensor_scalar(kscb[:, :], kraw[:, :], bk[:, j:j + 1],
                                        None, ALU.add)\
                            .then_inc(*S.inc("kev1", 1))
                    else:
                        v.tensor_copy(kscb[:, :], kraw[:, :])\
                            .then_inc(*S.inc("kev1", 1))
                    v.drain()
                    v.wait_ge(S.h[f"tabc{par}"], base[f"tabc{par}"] + 16 * (tt8 // 2 + 1))
                    v.wait_ge(S.h[f"tabs{par}"], base[f"tabs{par}"] + 16 * (tt8 // 2 + 1))
                    v.tensor_tensor(scr1[:, :], kscb[:, :], tabc[par][:, :], ALU.mult)
                    v.wait_ge(S.h["mm"], mmpos[("rot", idx)])
                    v.tensor_tensor(scr2[:, :], rotb[:, :], tabs[par][:, :], ALU.mult)
                    v.drain()
                    v.tensor_tensor(ksb[:, j * T + tt8 * QT: j * T + (tt8 + 1) * QT],
                                    scr1[:, :], scr2[:, :], ALU.add)\
                        .then_inc(*S.inc("kev2", 1))
                    v.drain()
                    v.wait_ge(S.h["mm"], mmpos[("vraw", idx)])
                    if idx >= 2:
                        v.wait_ge(S.h["mm"], mmpos[("vt", idx - 1)])
                    if kind == "a":
                        v.tensor_scalar(vscb[:, :], vraw[:, :], bv[:, j:j + 1],
                                        None, ALU.add).then_inc(*S.inc("vev1", 1))
                    else:
                        v.tensor_copy(vscb[:, :], vraw[:, :])\
                            .then_inc(*S.inc("vev1", 1))
                    v.wait_ge(S.h["mm"], mmpos[("vt", idx)])
                    v.tensor_copy(
                        vsb[:, (j * 32 + tt8 * 4) * 128:(j * 32 + tt8 * 4 + 4) * 128],
                        vtb[:, :]).then_inc(*S.inc("vev2", 1))

        emit_barrier(block, env, ["kev1", "kev2", "vev1", "vev2",
                                  "kw0", "vw0", "tabc0", "tabc1",
                                  "tabs0", "tabs1", "hq0", "hq1", "mm"])

    NPB = 4
    NG_ALL = NQT * NH
    with ExitStack() as _st:
        _sb = lambda n, sh, dt: _st.enter_context(env["nc"].sbuf_tensor(n + sfx, sh, dt))
        _ps = lambda n, sh, dt: _st.enter_context(env["nc"].psum_tensor(n + sfx, sh, dt))
        qwb = [_sb(f"qwb{i}", [128, NCC * 128], BF16) for i in range(2)]
        owb = [_sb(f"owb{i}", [128, NH * 128], BF16) for i in range(2)]
        qscb = _sb("qscb", [128, QT], BF16)
        qsb = [_sb(f"qsb{i}", [128, QT], BF16) for i in range(2)]
        scr1 = _sb("scr1q", [128, QT], BF16)
        scr2 = _sb("scr2q", [128, QT], BF16)
        psb = [_sb(f"psb{i}", [128, QT], BF16) for i in range(NPB)]
        Rsb2 = [_sb(f"Rsb{i}", [128, QT], F32) for i in range(2)]
        osb = _sb("osb", [128, NH * QT], BF16)
        ysb2 = [_sb(f"ysb2{i}", [128, QT], BF16) for i in range(2)]
        qrot = [_ps(f"qrot{i}", [128, QT], F32) for i in range(2)]
        sbk = [_ps(f"sb{i}", [128, QT], F32) for i in range(NPB)]
        obk = _ps("ob", [128, QT], F32)
        lbk = _ps("lb", [128, QT], F32)

        def emit_prep(t, g):
            qi, h = divmod(g, NH)
            par = g % 2
            t.wait_ge(S.h[f"qw{par}"], base[f"qw{par}"] + 16 * (g // 2 + 1))
            t.wait_ge(S.h[f"hq{qi % 2}"],
                      base[f"hq{qi % 2}"] + 16 * NCC * (4 + qi // 2 + 1))
            if g >= 2:
                t.wait_ge(S.h["qev2"], base["qev2"] + g - 1)
            for cc in range(NCC):
                ins = t.matmul(qrot[par][:, :], qwb[par][:, cc * 128:(cc + 1) * 128],
                               hq[qi % 2][:, cc * QT:(cc + 1) * QT],
                               start=(cc == 0), stop=(cc == NCC - 1))
            ins.then_inc(*S.inc("mm", 1))
            mmpos[("qraw", g)] = S.val("mm")
            t.wait_ge(S.h["qev1"], base["qev1"] + g + 1)
            ins = t.matmul(qrot[par][:, :], R_sb[:, :], qscb[:, :],
                           start=True, stop=True)
            ins.then_inc(*S.inc("mm", 1))
            mmpos[("rotq", g)] = S.val("mm")

        @block.tensor
        def _(t):
            emit_prep(t, 0)
            emit_prep(t, 1)
            for qi in range(NQT):
                lst = chunks[qi]
                for h in range(NH):
                    g = qi * NH + h
                    j = h % nkv
                    par = g % 2
                    P = len(lst)
                    LEAD = NPB - 1
                    for n in range(P + LEAD):
                        if n < P:
                            K0, slot = lst[n]
                            pg = pair_of[(g, n)]
                            if n == 0:
                                t.wait_ge(S.h["qev2"], base["qev2"] + g + 1)
                            if pg > NPB:
                                t.wait_ge(S.h["pexp"], base["pexp"] + pg - NPB)
                            if qi >= 1 and h == 0 and n < NPB:
                                t.wait_ge(S.h["yev"], base["yev"] + qi * NCC)
                            ins = t.matmul(sbk[pg % NPB][:, :],
                                           ksb[:, j * T + K0: j * T + K0 + 128],
                                           qsb[par][:, :], start=True, stop=True)
                            ins.then_inc(*S.inc("mm", 1))
                            mmpos[("s", pg)] = S.val("mm")
                        if n >= LEAD:
                            m = n - LEAD
                            K0m, slotm = lst[m]
                            pgm = pair_of[(g, m)]
                            t.wait_ge(S.h["pexp"], base["pexp"] + pgm)
                            if slotm is not None:
                                t.wait_ge(S.h["pmask"], base["pmask"] + pmask_idx[pgm])
                            if m == 0 and g >= 1:
                                t.wait_ge(S.h["lr"], base["lr"] + g)
                                t.wait_ge(S.h["oev"], base["oev"] + g)
                            t.matmul(lbk[:, :], ones[:, :], psb[pgm % NPB][:, :],
                                     start=(m == 0), stop=(m == P - 1))
                            ins = t.matmul(obk[:, :],
                                           vsb[:, (j * 32 + (K0m // 128)) * 128:
                                               (j * 32 + (K0m // 128) + 1) * 128],
                                           psb[pgm % NPB][:, :],
                                           start=(m == 0), stop=(m == P - 1))
                            ins.then_inc(*S.inc("mm", 1))
                            mmpos[("o", pgm)] = S.val("mm")
                    if g + 2 < NG_ALL:
                        emit_prep(t, g + 2)
                for co in range(NCC):
                    oc = qi * NCC + co
                    t.wait_ge(S.h[f"ow{oc % 2}"], base[f"ow{oc % 2}"] + 16 * (oc // 2 + 1))
                    t.wait_ge(S.h["oev"], base["oev"] + (qi + 1) * NH)
                    t.wait_ge(S.h["pexp"],
                              base["pexp"] + pair_of[(qi * NH + NH - 1, len(lst) - 1)])
                    if oc >= NPB:
                        t.wait_ge(S.h["yev"], base["yev"] + oc - NPB + 1)
                    for hh in range(NH):
                        ins = t.matmul(sbk[co % NPB][:, :],
                                       owb[oc % 2][:, hh * 128:(hh + 1) * 128],
                                       osb[:, hh * QT:(hh + 1) * QT],
                                       start=(hh == 0), stop=(hh == NH - 1))
                    ins.then_inc(*S.inc("mm", 1))
                    mmpos[("yq", qi, co)] = S.val("mm")

        @block.gpsimd
        def _(g_):
            def qwload(g):
                if g >= 2:
                    g_.wait_ge(S.h["mm"], mmpos[("qraw", g - 2)])
                g_.dma_start(out=qwb[g % 2][:, :], in_=WqT[g % NH, :, :])\
                    .then_inc(*S.inc(f"qw{g % 2}", 16))

            def tabload(qi2):
                par8 = qi2 % 2
                if qi2 >= 2:
                    g_.wait_ge(S.h["qev2"], base["qev2"] + (qi2 - 1) * NH)
                g_.dma_start(out=tabc[par8][:, :], in_=cosT[:, qi2 * QT:(qi2 + 1) * QT])\
                    .then_inc(*S.inc(f"tabc{par8}", 16))
                g_.dma_start(out=tabs[par8][:, :], in_=sinT[:, qi2 * QT:(qi2 + 1) * QT])\
                    .then_inc(*S.inc(f"tabs{par8}", 16))

            qwload(0)
            qwload(1)
            tabload(0)
            for qi in range(NQT):
                for h in range(NH):
                    g = qi * NH + h
                    if h == NH - 2 and qi + 1 < NQT:
                        tabload(qi + 1)
                    if g + 2 < NG_ALL:
                        qwload(g + 2)
                for co in range(NCC):
                    oc = qi * NCC + co
                    if oc >= 2:
                        g_.wait_ge(S.h["mm"], mmpos[("yq", (oc - 2) // NCC,
                                                     (oc - 2) % NCC)])
                    g_.dma_start(out=owb[oc % 2][:, :], in_=WoT[co, :, :])\
                        .then_inc(*S.inc(f"ow{oc % 2}", 16))

        @block.sync
        def _(s):
            for qi in range(NQT):
                par = qi % 2
                if qi >= 2:
                    s.wait_ge(S.h["mm"], mmpos[("qraw", (qi - 2) * NH + NH - 1)])
                for cc in range(NCC):
                    s.dma_start_transpose(
                        hq[par][:, cc * QT:(cc + 1) * QT],
                        h_src[qi * QT:(qi + 1) * QT, cc * 128:(cc + 1) * 128],
                    ).then_inc(*S.inc(f"hq{par}", 16))

        @block.scalar
        def _(sc):
            for qi in range(NQT):
                for h in range(NH):
                    g = qi * NH + h
                    par = g % 2
                    for n, (K0, slot) in enumerate(chunks[qi]):
                        pg = pair_of[(g, n)]
                        sc.wait_ge(S.h["mm"], mmpos[("s", pg)])
                        if pg > NPB:
                            sc.wait_ge(S.h["mm"], mmpos[("o", pg - NPB)])
                        sc.activation(psb[pg % NPB][:, :], sbk[pg % NPB][:, :],
                                      AF.Exp, bias=shiftc[:, 0:1], scale=r_imm)\
                            .then_inc(*S.inc("pexp", 1))
                    sc.wait_ge(S.h["mm"], mmpos[("o", pairs_of_g[g][-1])])
                    if g >= 2:
                        sc.wait_ge(S.h["oev"], base["oev"] + g - 1)
                    sc.activation(Rsb2[par][:, :], lbk[:, :], AF.Ln)
                    sc.drain()
                    sc.activation(Rsb2[par][:, :], Rsb2[par][:, :], AF.Exp, scale=-1.0)\
                        .then_inc(*S.inc("lr", 1))
                for co in range(NCC):
                    oc = qi * NCC + co
                    sc.wait_ge(S.h["yev"], base["yev"] + oc + 1)
                    sc.dma_start(out=y_dst[co * 128:(co + 1) * 128,
                                           qi * QT:(qi + 1) * QT],
                                 in_=ysb2[oc % 2][:, :])\
                        .then_inc(*S.inc(f"yst{oc % 2}", 16))

        @block.vector
        def _(v):
            def rope(g):
                qi2, h2 = divmod(g, NH)
                par2 = g % 2
                v.wait_ge(S.h["mm"], mmpos[("qraw", g)])
                if g >= 1:
                    v.wait_ge(S.h["mm"], mmpos[("rotq", g - 1)])
                if kind == "a":
                    v.tensor_scalar(qscb[:, :], qrot[par2][:, :], bq[:, h2:h2 + 1],
                                    None, ALU.add).then_inc(*S.inc("qev1", 1))
                else:
                    v.tensor_copy(qscb[:, :], qrot[par2][:, :])\
                        .then_inc(*S.inc("qev1", 1))
                v.drain()
                v.wait_ge(S.h[f"tabc{qi2 % 2}"],
                          base[f"tabc{qi2 % 2}"] + 16 * (4 + qi2 // 2 + 1))
                v.wait_ge(S.h[f"tabs{qi2 % 2}"],
                          base[f"tabs{qi2 % 2}"] + 16 * (4 + qi2 // 2 + 1))
                if g >= 2:
                    v.wait_ge(S.h["mm"], mmpos[("s", pairs_of_g[g - 2][-1])])
                v.tensor_tensor(scr1[:, :], qscb[:, :], tabc[qi2 % 2][:, :], ALU.mult)
                v.wait_ge(S.h["mm"], mmpos[("rotq", g)])
                v.tensor_tensor(scr2[:, :], qrot[par2][:, :], tabs[qi2 % 2][:, :],
                                ALU.mult)
                v.drain()
                v.tensor_tensor(qsb[par2][:, :], scr1[:, :], scr2[:, :], ALU.add)\
                    .then_inc(*S.inc("qev2", 1))
                v.drain()

            rope(0)
            if NG_ALL > 1:
                rope(1)
            for qi in range(NQT):
                for h in range(NH):
                    g = qi * NH + h
                    for n, (K0, slot) in enumerate(chunks[qi]):
                        pg = pair_of[(g, n)]
                        if slot is not None:
                            v.wait_ge(S.h["pexp"], base["pexp"] + pg)
                            v.tensor_tensor(psb[pg % NPB][:, :], psb[pg % NPB][:, :],
                                            masks[:, slot * QT:(slot + 1) * QT],
                                            ALU.mult).then_inc(*S.inc("pmask", 1))
                    v.wait_ge(S.h["lr"], base["lr"] + g + 1)
                    v.wait_ge(S.h["mm"], mmpos[("o", pairs_of_g[g][-1])])
                    v.tensor_tensor(osb[:, h * QT:(h + 1) * QT], obk[:, :],
                                    Rsb2[g % 2][:, :], ALU.mult)\
                        .then_inc(*S.inc("oev", 1))
                    if g + 2 < NG_ALL:
                        rope(g + 2)
                for co in range(NCC):
                    oc = qi * NCC + co
                    v.wait_ge(S.h["mm"], mmpos[("yq", qi, co)])
                    if oc >= 2:
                        v.wait_ge(S.h[f"yst{oc % 2}"],
                                  base[f"yst{oc % 2}"] + 16 * (oc // 2))
                    v.tensor_scalar(ysb2[oc % 2][:, :], sbk[co % NPB][:, :],
                                    bo[:, co:co + 1], None, ALU.add)\
                        .then_inc(*S.inc("yev", 1))



def emit_barrier(block, env, waits):
    S = env["sems"]
    vals = [(S.h[nm], S.val(nm)) for nm in waits]

    @block.sync
    def _(s):
        for h, v in vals:
            s.wait_ge(h, v)
        s.sem_inc(S.h["bar"], 1)

    S.n["bar"] += 1
    tgt = S.val("bar")
    for eng in ("tensor", "vector", "scalar", "gpsimd"):
        getattr(block, eng)(lambda e: e.wait_ge(S.h["bar"], tgt))


def build_program(sublayers=("attn_a", "ffn", "attn_b", "ffn"), h_out=False):
    nc = bass.Bass("TRN2", target_bir_lowering=False)
    ap = lambda t: t.ap()

    x_in = ap(nc.dram_tensor("x", [T, C], F32, kind="ExternalInput"))
    out = ap(nc.dram_tensor("out", [T, C], F32, kind="ExternalOutput"))

    gWT = ap(nc.dram_tensor("gWT", [NI, 128, NCC * 128], F32, kind="ExternalInput"))
    uWT = ap(nc.dram_tensor("uWT", [NI, 128, NCC * 128], F32, kind="ExternalInput"))
    dWT3 = ap(nc.dram_tensor("dWT3", [NCC, 128, NI * 128], F32, kind="ExternalInput"))
    gS_d = ap(nc.dram_tensor("gS", [128, NI], F32, kind="ExternalInput"))
    uS_d = ap(nc.dram_tensor("uS", [128, NI], F32, kind="ExternalInput"))
    dS_d = ap(nc.dram_tensor("dS", [128, NCC], F32, kind="ExternalInput"))

    aWqT = ap(nc.dram_tensor("aWqT", [NH, 128, NCC * 128], F32, kind="ExternalInput"))
    aWkT = ap(nc.dram_tensor("aWkT", [NKV_A, 128, NCC * 128], F32, kind="ExternalInput"))
    aWvT = ap(nc.dram_tensor("aWvT", [NKV_A, 128, NCC * 128], F32, kind="ExternalInput"))
    aWoT = ap(nc.dram_tensor("aWoT", [NCC, 128, NH * 128], F32, kind="ExternalInput"))
    bWqT = ap(nc.dram_tensor("bWqT", [NH, 128, NCC * 128], F32, kind="ExternalInput"))
    bWkT = ap(nc.dram_tensor("bWkT", [1, 128, NCC * 128], F32, kind="ExternalInput"))
    bWvT = ap(nc.dram_tensor("bWvT", [1, 128, NCC * 128], F32, kind="ExternalInput"))
    bWoT = ap(nc.dram_tensor("bWoT", [NCC, 128, NH * 128], F32, kind="ExternalInput"))
    bqA_d = ap(nc.dram_tensor("bqA", [128, NH], F32, kind="ExternalInput"))
    bkA_d = ap(nc.dram_tensor("bkA", [128, NKV_A], F32, kind="ExternalInput"))
    bvA_d = ap(nc.dram_tensor("bvA", [128, NKV_A], F32, kind="ExternalInput"))
    boA_d = ap(nc.dram_tensor("boA", [128, NCC], F32, kind="ExternalInput"))
    boB_d = ap(nc.dram_tensor("boB", [128, NCC], F32, kind="ExternalInput"))
    cosA_d = ap(nc.dram_tensor("cosA", [128, T], F32, kind="ExternalInput"))
    sinA_d = ap(nc.dram_tensor("sinA", [128, T], F32, kind="ExternalInput"))
    cosB_d = ap(nc.dram_tensor("cosB", [128, T], F32, kind="ExternalInput"))
    sinB_d = ap(nc.dram_tensor("sinB", [128, T], F32, kind="ExternalInput"))
    RaI = ap(nc.dram_tensor("RaI", [128, 128], F32, kind="ExternalInput"))
    RbI = ap(nc.dram_tensor("RbI", [128, 128], F32, kind="ExternalInput"))
    masksI = ap(nc.dram_tensor("masksI", [128, 8 * QT], F32, kind="ExternalInput"))
    identI = ap(nc.dram_tensor("identI", [128, 128], F32, kind="ExternalInput"))
    onesI = ap(nc.dram_tensor("onesI", [128, 128], F32, kind="ExternalInput"))

    xs_a = ap(nc.dram_tensor("xs_a", [T, C], F32))
    xs_b = ap(nc.dram_tensor("xs_b", [T, C], F32))
    h_sc_a = ap(nc.dram_tensor("h_sc", [T, C], BF16,
                               kind="ExternalOutput" if h_out else "Internal"))
    h_sc_b = ap(nc.dram_tensor("h_sc_b", [T, C], BF16))
    h_scr = [h_sc_a, h_sc_b]
    y_sc = ap(nc.dram_tensor("y_sc", [C, T], BF16))

    from contextlib import ExitStack
    with ExitStack() as stack:
        S = Sems(nc, stack, SEM_NAMES)
        sb = lambda name, shape, dt: stack.enter_context(nc.sbuf_tensor(name, shape, dt))
        B = {}
        B["xt"] = [sb(f"xt{i}", [128, C], F32) for i in range(2)]
        B["yt"] = [sb(f"yt{i}", [128, C], BF16) for i in range(2)]
        B["ht"] = [sb(f"ht{i}", [128, C], BF16) for i in range(2)]
        B["st8"] = sb("st8", [128, 16], F32)
        B["bn6"] = sb("bn6", [128, 48], F32)
        B["sqs"] = sb("sqs", [128, 16], F32)
        B["gSs"] = sb("gSs", [128, NI], F32)
        B["uSs"] = sb("uSs", [128, NI], F32)
        B["dSs"] = sb("dSs", [128, NCC], F32)
        B["masks"] = sb("masks", [128, 8 * QT], BF16)
        B["ident"] = sb("ident", [128, 128], BF16)
        B["ones"] = sb("ones", [128, 128], BF16)
        B["Ra"] = sb("Ra", [128, 128], BF16)
        B["Rb"] = sb("Rb", [128, 128], BF16)
        B["bqA"] = sb("bqA_s", [128, NH], F32)
        B["bkA"] = sb("bkA_s", [128, NKV_A], F32)
        B["bvA"] = sb("bvA_s", [128, NKV_A], F32)
        B["boA"] = sb("boA_s", [128, NCC], F32)
        B["boB"] = sb("boB_s", [128, NCC], F32)
        B["shiftc"] = sb("shiftc", [128, 1], F32)

        env = {"nc": nc, "sems": S, "bufs": B}

        with nc.Block() as block:
            @block.gpsimd
            def _(g):
                for dst, srcd in [(B["gSs"], gS_d), (B["uSs"], uS_d), (B["dSs"], dS_d),
                                  (B["masks"], masksI), (B["ident"], identI),
                                  (B["ones"], onesI), (B["Ra"], RaI),
                                  (B["Rb"], RbI), (B["bqA"], bqA_d), (B["bkA"], bkA_d),
                                  (B["bvA"], bvA_d), (B["boA"], boA_d),
                                  (B["boB"], boB_d)]:
                    g.dma_start(out=dst[:, :], in_=srcd[:, :]).then_inc(*S.inc("aux", 16))
                g.memset(B["shiftc"][:, :], -20.0).then_inc(*S.inc("aux2", 1))

            auxtot = S.val("aux")
            aux2tot = S.val("aux2")
            for eng in ("tensor", "vector", "scalar"):
                def _w(e):
                    e.wait_ge(S.h["aux"], auxtot)
                    e.wait_ge(S.h["aux2"], aux2tot)
                getattr(block, eng)(_w)

            x_cur = x_in
            x_scr = [xs_a, xs_b]
            y_prev = None
            y_gate = None
            prev_sems = []
            LN_SEMS = ["xld0", "xld1", "yld0", "yld1", "xst0", "xst1",
                       "hst0", "hst1", "vln"]
            FFN_SEMS = ["hld", "gld0", "gld1", "uld0", "uld1", "dwld0", "dwld1",
                        "yst0", "yst1", "yev", "zst", "act", "mm"]
            ATTN_SEMS = ["qev1", "qev2", "pexp", "pmask", "lr", "oev", "yev",
                         "yst0", "yst1", "qw0", "qw1", "ow0", "ow1", "hq0",
                         "hq1", "tabc0", "tabc1", "tabs0", "tabs1", "mm"]
            for k, sl in enumerate(sublayers):
                x_next = x_scr[k % 2]
                h_sc = h_scr[k % 2]
                hb0, hb1 = S.val("hst0"), S.val("hst1")
                emit_ln_phase(block, env, x_cur, x_next, y_prev, True, h_sc,
                              y_gate=y_gate)
                if k >= 1:
                    emit_barrier(block, env, LN_SEMS + prev_sems)

                def h_gate(tau, b0=hb0, b1=hb1):
                    return [(S.h["hst0"], b0 + 16 * (2 * tau + 2)),
                            (S.h["hst1"], b1 + 16 * (2 * tau + 2))]

                yb0, yb1 = S.val("yst0"), S.val("yst1")
                if sl == "ffn":
                    with (
                        nc.sbuf_tensor(f"hsb_{k}", [128, NCC * GROUP], BF16) as hsb,
                        nc.sbuf_tensor(f"zsb_{k}", [128, NI * GROUP], BF16) as zsb,
                        nc.sbuf_tensor(f"gw0_{k}", [128, C], BF16) as gw0,
                        nc.sbuf_tensor(f"gw1_{k}", [128, C], BF16) as gw1,
                        nc.sbuf_tensor(f"uw0_{k}", [128, C], BF16) as uw0,
                        nc.sbuf_tensor(f"uw1_{k}", [128, C], BF16) as uw1,
                        nc.sbuf_tensor(f"dw0_{k}", [128, INTER], BF16) as dw0,
                        nc.sbuf_tensor(f"dw1_{k}", [128, INTER], BF16) as dw1,
                        nc.sbuf_tensor(f"silu0_{k}", [128, QT], BF16) as silu0,
                        nc.sbuf_tensor(f"silu1_{k}", [128, QT], BF16) as silu1,
                        nc.sbuf_tensor(f"ysb0_{k}", [128, QT], BF16) as ysb0,
                        nc.sbuf_tensor(f"ysb1_{k}", [128, QT], BF16) as ysb1,
                        nc.psum_tensor(f"pg0_{k}", [128, QT], F32) as pg0,
                        nc.psum_tensor(f"pg1_{k}", [128, QT], F32) as pg1,
                        nc.psum_tensor(f"pu0_{k}", [128, QT], F32) as pu0,
                        nc.psum_tensor(f"pu1_{k}", [128, QT], F32) as pu1,
                        nc.psum_tensor(f"py0_{k}", [128, QT], F32) as py0,
                        nc.psum_tensor(f"py1_{k}", [128, QT], F32) as py1,
                    ):
                        B.update(hsb=hsb, zsb=zsb, gw=[gw0, gw1], uw=[uw0, uw1],
                                 dw=[dw0, dw1], silu=[silu0, silu1],
                                 ysb=[ysb0, ysb1], pg=[pg0, pg1], pu=[pu0, pu1],
                                 py=[py0, py1])
                        w = {"gWT": gWT, "uWT": uWT, "dWT3": dWT3,
                             "gS": B["gSs"], "uS": B["uSs"], "dS": B["dSs"]}
                        emit_ffn_phase(block, env, h_sc, y_sc, w, h_gate=h_gate)
                    prev_sems = FFN_SEMS

                    def y_gate(i, b0=yb0, b1=yb1):
                        v = 256 * (i // 8 + 1)
                        return [(S.h["yst0"], b0 + v), (S.h["yst1"], b1 + v)]
                elif sl in ("attn_a", "attn_b"):
                    nkv = NKV_A if sl == "attn_a" else 1
                    with (
                        nc.sbuf_tensor(f"ksb{k}", [128, nkv * T], BF16) as ksb,
                        nc.sbuf_tensor(f"vsb{k}", [128, nkv * T], BF16) as vsb,
                        nc.sbuf_tensor(f"hqA{k}", [128, NCC * QT], BF16) as hqA,
                        nc.sbuf_tensor(f"hqB{k}", [128, NCC * QT], BF16) as hqB,
                        nc.sbuf_tensor(f"tabcA{k}", [128, QT], BF16) as tabcA,
                        nc.sbuf_tensor(f"tabcB{k}", [128, QT], BF16) as tabcB,
                        nc.sbuf_tensor(f"tabsA{k}", [128, QT], BF16) as tabsA,
                        nc.sbuf_tensor(f"tabsB{k}", [128, QT], BF16) as tabsB,
                        nc.sbuf_tensor(f"mk2_{k}", [128, 8], F32) as mk2,
                    ):
                        B.update(ksb=ksb, vsb=vsb, hq=[hqA, hqB],
                                 tabc=[tabcA, tabcB], tabs=[tabsA, tabsB], mk2=mk2)
                        if sl == "attn_a":
                            w = {"WqT": aWqT, "WkT": aWkT, "WvT": aWvT,
                                 "WoT": aWoT, "cosA": cosA_d, "sinA": sinA_d,
                                 "cosB": cosB_d, "sinB": sinB_d}
                        else:
                            w = {"WqT": bWqT, "WkT": bWkT, "WvT": bWvT,
                                 "WoT": bWoT, "cosA": cosA_d, "sinA": sinA_d,
                                 "cosB": cosB_d, "sinB": sinB_d}
                        env["sfx"] = f"_{k}"
                        emit_attn_phase(block, env, h_sc, y_sc, w,
                                        "a" if sl == "attn_a" else "b",
                                        h_gate=h_gate)
                    prev_sems = ATTN_SEMS

                    def y_gate(i, b0=yb0, b1=yb1):
                        v = 128 * (i // 4 + 1)
                        return [(S.h["yst0"], b0 + v), (S.h["yst1"], b1 + v)]
                elif sl == "ln":
                    prev_sems = []
                    y_gate = None
                else:
                    raise NotImplementedError(sl)

                x_cur = x_next
                if sl != "ln":
                    y_prev = y_sc

            emit_ln_phase(block, env, x_cur, out, y_prev, False, None,
                          y_gate=y_gate)

            @block.sync
            def _(s):
                s.wait_ge(S.h["xst0"], S.val("xst0"))
                s.wait_ge(S.h["xst1"], S.val("xst1"))

    return nc


def prep_ffn_inputs(gate_w, up_w, down_w, gate_s, up_s, down_s):
    d = {}
    d["gWT"] = np.ascontiguousarray(
        np.asarray(gate_w).reshape(NI, 128, NCC, 128).transpose(0, 3, 2, 1)
        .reshape(NI, 128, NCC * 128))
    d["uWT"] = np.ascontiguousarray(
        np.asarray(up_w).reshape(NI, 128, NCC, 128).transpose(0, 3, 2, 1)
        .reshape(NI, 128, NCC * 128))
    d["dWT3"] = np.ascontiguousarray(
        np.asarray(down_w).reshape(NCC, 128, NI, 128).transpose(0, 3, 2, 1)
        .reshape(NCC, 128, NI * 128))
    d["gS"] = np.ascontiguousarray(np.asarray(gate_s).reshape(NI, 128).T)
    d["uS"] = np.ascontiguousarray(np.asarray(up_s).reshape(NI, 128).T)
    d["dS"] = np.ascontiguousarray(np.asarray(down_s).reshape(NCC, 128).T)
    return d




def prep_attn_inputs(inputs):
    d = {}

    def tile_in(Wname, nheads):
        Wm = np.asarray(inputs[Wname])
        return np.ascontiguousarray(
            Wm.reshape(NCC, 128, nheads, 128).transpose(2, 1, 0, 3)
            .reshape(nheads, 128, NCC * 128))

    def tile_out(Wname):
        Wm = np.asarray(inputs[Wname])
        return np.ascontiguousarray(
            Wm.reshape(NH, 128, NCC, 128).transpose(2, 1, 0, 3)
            .reshape(NCC, 128, NH * 128))

    d["aWqT"] = tile_in("L0_attn_a_Wq", NH)
    d["aWkT"] = tile_in("L0_attn_a_Wk", NKV_A)
    d["aWvT"] = tile_in("L0_attn_a_Wv", NKV_A)
    d["aWoT"] = tile_out("L0_attn_a_Wo")
    d["bWqT"] = tile_in("L0_attn_b_Wq", NH)
    d["bWkT"] = tile_in("L0_attn_b_Wk", 1)
    d["bWvT"] = tile_in("L0_attn_b_Wv", 1)
    d["bWoT"] = tile_out("L0_attn_b_Wo")

    d["bqA"] = np.ascontiguousarray(np.asarray(inputs["L0_attn_a_bq"]).reshape(NH, 128).T)
    d["bkA"] = np.ascontiguousarray(np.asarray(inputs["L0_attn_a_bk"]).reshape(NKV_A, 128).T)
    d["bvA"] = np.ascontiguousarray(np.asarray(inputs["L0_attn_a_bv"]).reshape(NKV_A, 128).T)
    d["boA"] = np.ascontiguousarray(np.asarray(inputs["L0_attn_a_bo"]).reshape(NCC, 128).T)
    d["boB"] = np.ascontiguousarray(np.asarray(inputs["L0_attn_b_bo"]).reshape(NCC, 128).T)

    t = np.arange(T, dtype=np.float64)
    th_a = 10000.0 ** (-np.arange(0, HD, 2, dtype=np.float64) / HD)
    ang_a = np.outer(th_a, t)
    cosA = np.empty((128, T), np.float32)
    sinA = np.empty((128, T), np.float32)
    cosA[0::2] = np.cos(ang_a); cosA[1::2] = np.cos(ang_a)
    sinA[0::2] = -np.sin(ang_a); sinA[1::2] = np.sin(ang_a)
    cosA *= 0.5
    sinA *= 0.5
    d["cosA"], d["sinA"] = np.ascontiguousarray(cosA), np.ascontiguousarray(sinA)
    th_b = np.exp(-np.log(10000.0) * (np.arange(0, 64, 2, dtype=np.float64) / 64.0))
    ang_b = np.outer(th_b, t)
    cosB = np.ones((128, T), np.float32)
    sinB = np.zeros((128, T), np.float32)
    cosB[0:32] = np.cos(ang_b); cosB[32:64] = np.cos(ang_b)
    sinB[0:32] = -np.sin(ang_b); sinB[32:64] = np.sin(ang_b)
    cosB *= 0.25
    sinB *= 0.25
    d["cosB"], d["sinB"] = np.ascontiguousarray(cosB), np.ascontiguousarray(sinB)

    Ra = np.zeros((128, 128), np.float32)
    for i in range(64):
        Ra[2 * i, 2 * i + 1] = 1.0
        Ra[2 * i + 1, 2 * i] = 1.0
    Rb = np.zeros((128, 128), np.float32)
    for dd in range(32):
        Rb[dd, dd + 32] = 1.0
        Rb[dd + 32, dd] = 1.0
    d["RaI"], d["RbI"] = Ra, Rb
    d["identI"] = np.eye(128, dtype=np.float32)
    d["onesI"] = np.ones((128, 128), np.float32)

    kk = np.arange(128)[:, None]
    tt = np.arange(QT)[None, :]
    masks = np.zeros((128, 8 * QT), np.float32)
    for m in range(4):
        dd = m * 128
        masks[:, m * QT:(m + 1) * QT] = (kk + dd <= tt).astype(np.float32)
    for m in range(4):
        dd = -WINDOW + m * 128
        masks[:, (4 + m) * QT:(5 + m) * QT] = ((tt - kk) <= dd + WINDOW - 1).astype(np.float32)
    d["masksI"] = masks
    return d


SUBLAYERS = ("attn_a", "ffn", "attn_b", "ffn")


def make_in_maps(inputs):
    x = np.asarray(inputs["L0_attn_a_x"])
    wd = prep_ffn_inputs(inputs["L0_ffn_gate_w"], inputs["L0_ffn_up_w"],
                         inputs["L0_ffn_down_w"], inputs["L0_ffn_gate_s"],
                         inputs["L0_ffn_up_s"], inputs["L0_ffn_down_s"])
    wd.update(prep_attn_inputs(inputs))
    return [dict(wd, x=np.ascontiguousarray(x[b])) for b in range(8)]


def kernel(**inputs):
    nc = build_program(SUBLAYERS)
    in_maps = make_in_maps(inputs)
    res = run_bass_kernel_spmd(nc, in_maps, core_ids=list(range(8)))
    return np.stack([r["out"] for r in res.results], axis=0)
